# version 09
# speedup vs baseline: 1.2611x; 1.0296x over previous
import numpy as np
from contextlib import ExitStack

import concourse.bass as bass
import concourse.mybir as mybir

F32 = mybir.dt.float32
BF16 = mybir.dt.bfloat16
U32 = mybir.dt.uint32
AF = mybir.ActivationFunctionType
ALU = mybir.AluOpType
AX = mybir.AxisListType

ENGINES = ("sync", "tensor", "vector", "scalar", "gpsimd")

DMA_ENGINES = ("sync", "gpsimd", "scalar", "vector")


class Stage:
    __slots__ = ("idx", "engine", "emit", "deps", "n_inc", "kind", "sem", "val",
                 "name")

    def __init__(self, idx, engine, emit, deps, n_inc, kind, name):
        self.idx = idx
        self.engine = engine
        self.emit = emit
        self.deps = deps
        self.n_inc = n_inc
        self.kind = kind
        self.sem = None
        self.val = None
        self.name = name


class Prog:
    def __init__(self):
        self.stages = []

    def comp(self, engine, emit, deps=(), name=""):
        s = Stage(len(self.stages), engine, emit, tuple(deps), 1, "compute", name)
        self.stages.append(s)
        return s

    def dma(self, engine, emit, deps=(), n_dma=1, name=""):
        s = Stage(len(self.stages), engine, emit, tuple(deps), 16 * n_dma, "dma", name)
        self.stages.append(s)
        return s

    def coll(self, emit, deps=(), name=""):
        s = Stage(len(self.stages), "gpsimd", emit, tuple(deps), 1, "collective", name)
        self.stages.append(s)
        return s

    def build(self, nc, ctx, n_dma_sems=8):
        csem = {e: ctx.enter_context(nc.semaphore(f"c_{e}")) for e in ENGINES}
        dsem = {e: [ctx.enter_context(nc.semaphore(f"d_{e}{i}"))
                    for i in range(n_dma_sems)] for e in DMA_ENGINES}
        run_c = {e: 0 for e in ENGINES}
        run_d = {e: [0] * n_dma_sems for e in DMA_ENGINES}
        rot = {e: 0 for e in DMA_ENGINES}
        coll_sems = []
        for s in self.stages:
            for d in s.deps:
                assert d.idx < s.idx, f"dep ordering: {d.name} -> {s.name}"
            if s.kind == "compute":
                s.sem = csem[s.engine]
                run_c[s.engine] += s.n_inc
                s.val = run_c[s.engine]
            elif s.kind == "dma":
                i = rot[s.engine]
                rot[s.engine] = (i + 1) % n_dma_sems
                s.sem = dsem[s.engine][i]
                run_d[s.engine][i] += s.n_inc
                s.val = run_d[s.engine][i]
            else:
                h = ctx.enter_context(nc.semaphore(f"cc_{len(coll_sems)}"))
                coll_sems.append(h)
                s.sem = h
                s.val = 1

        block = ctx.enter_context(nc.Block())

        def engine_body(ename):
            def body(eng):
                for s in self.stages:
                    if s.engine != ename:
                        continue
                    seen = set()
                    for d in s.deps:
                        key = (id(d.sem), d.val)
                        if key in seen:
                            continue
                        seen.add(key)
                        eng.wait_ge(d.sem, d.val)
                    if s.kind == "compute":
                        last = s.emit(eng)
                        assert last is not None, f"stage {s.name} returned None"
                        last.then_inc(s.sem, 1)
                    elif s.kind == "dma":
                        instrs = s.emit(eng)
                        assert len(instrs) * 16 == s.n_inc, \
                            f"stage {s.name}: {len(instrs)} DMAs != declared {s.n_inc//16}"
                        for i in instrs:
                            i.then_inc(s.sem, 16)
                    else:
                        instr = s.emit(eng)
                        instr.then_inc(s.sem)
            return body

        used = {s.engine for s in self.stages}
        if "sync" in used:
            block.sync(engine_body("sync"))
        if "tensor" in used:
            block.tensor(engine_body("tensor"))
        if "vector" in used:
            block.vector(engine_body("vector"))
        if "scalar" in used:
            block.scalar(engine_body("scalar"))
        if "gpsimd" in used:
            block.gpsimd(engine_body("gpsimd"))


class KB:

    def __init__(self, name="k"):
        self.nc = bass.Bass("TRN2", target_bir_lowering=False)
        self.ctx = ExitStack()
        self.prog = Prog()
        self.inputs = {}
        self.outputs = {}
        self._uniq = 0
        self._cms = {}
        self._alloc_seq = {}

    def u(self, base):
        self._uniq += 1
        return f"{base}_{self._uniq}"

    def inp(self, name, shape, dtype=F32):
        ap = self.nc.dram_tensor(name, list(shape), dtype, kind="ExternalInput").ap()
        self.inputs[name] = ap
        return ap

    def out(self, name, shape, dtype=F32):
        ap = self.nc.dram_tensor(name, list(shape), dtype, kind="ExternalOutput").ap()
        self.outputs[name] = ap
        return ap

    def dram(self, name, shape, dtype=F32):
        return self.nc.dram_tensor(name, list(shape), dtype)

    def sb(self, name, shape, dtype=F32):
        cm = self.nc.sbuf_tensor(self.u(name), list(shape), dtype)
        h = cm.__enter__()
        self._cms[id(h)] = cm
        self._alloc_seq[id(h)] = len(self._alloc_seq)
        return h

    def ps(self, name, shape, dtype=F32):
        cm = self.nc.psum_tensor(self.u(name), list(shape), dtype)
        h = cm.__enter__()
        self._cms[id(h)] = cm
        self._alloc_seq[id(h)] = len(self._alloc_seq)
        return h

    def free(self, *handles):
        hs = sorted(handles, key=lambda h: -self._alloc_seq[id(h)])
        for h in hs:
            cm = self._cms.pop(id(h))
            cm.__exit__(None, None, None)

    def psum_scope(self):
        return ExitStack()

    def finish(self):
        self.prog.build(self.nc, self.ctx)
        for hid in sorted(self._cms, key=lambda i: -self._alloc_seq[i]):
            self._cms[hid].__exit__(None, None, None)
        self._cms.clear()
        self.ctx.close()
        return self.nc



def load(kb, dst_sb, src_ap, deps=(), name="load", engine="sync"):
    def emit(eng):
        return [eng.dma_start(out=dst_sb, in_=src_ap)]
    return kb.prog.dma(engine, emit, deps=deps, n_dma=1, name=name)


def store(kb, dst_ap, src_sb, deps=(), name="store", engine="gpsimd"):
    def emit(eng):
        return [eng.dma_start(out=dst_ap, in_=src_sb)]
    return kb.prog.dma(engine, emit, deps=deps, n_dma=1, name=name)


def matmul_fm(kb, out_ps, lhsT_sb_tiles, rhs_sb_tiles, deps=(), name="mm",
              n_chunk=512, start=True, stop=True, extra=()):
    def emit(eng):
        last = None
        K = len(lhsT_sb_tiles)
        N = rhs_sb_tiles[0].shape[-1]
        for ki in range(K):
            lhsT = lhsT_sb_tiles[ki]
            rhs = rhs_sb_tiles[ki]
            bf = lhsT.dtype != F32
            if bf:
                eng.ldweights(lhsT)
            for n0 in range(0, N, n_chunk):
                n1 = min(N, n0 + n_chunk)
                last = eng.matmul(
                    out_ps[:, n0:n1], lhsT, rhs[:, n0:n1],
                    start=(start and ki == 0), stop=(stop and ki == K - 1))
        return last
    return kb.prog.comp("tensor", emit, deps=deps, name=name)


def emit_ln(kb, x_tiles, out_tiles, T, resid_tiles=None, eps=1e-5, deps=(),
            ones_col=None, ones_row=None, ones_col_bf=None, name="ln"):
    nG = len(x_tiles)
    D = nG * 128
    sx_ps = kb.ps("ln_sx", [1, T], F32)
    sq_ps = kb.ps("ln_sq", [1, T], F32)
    sqg_t = kb.sb("ln_sqg", [128, T], F32)
    sqg = [sqg_t, sqg_t]
    mean_row = kb.sb("ln_mean", [1, T], F32)
    work_row = kb.sb("ln_work", [1, T], F32)
    msq_row = work_row
    var_row = work_row
    srow = work_row
    rstd_row = work_row
    sq_stages = [None, None]
    mmx_stages = []
    mmq_stages = []
    for g in range(nG):
        buf = sqg[g % 2]
        sq_deps = list(deps)
        if g >= 1:
            sq_deps.append(mmq_stages[g - 1])
        st_sq = kb.prog.comp(
            "scalar",
            (lambda b=buf, xg=x_tiles[g]: lambda eng:
                eng.activation(b[:, :], xg, AF.Square))(),
            deps=sq_deps, name=f"{name}_sq{g}")
        sq_stages[g % 2] = st_sq
        oc = ones_col_bf if (x_tiles[g].dtype == BF16 and ones_col_bf is not None) else ones_col
        st_mx = matmul_fm(kb, sx_ps[:, :], [oc], [x_tiles[g]],
                          deps=list(deps), name=f"{name}_mmx{g}",
                          start=(g == 0), stop=(g == nG - 1))
        st_mq = matmul_fm(kb, sq_ps[:, :], [ones_col], [buf[:, :]],
                          deps=[st_sq], name=f"{name}_mmq{g}",
                          start=(g == 0), stop=(g == nG - 1))
        mmx_stages.append(st_mx)
        mmq_stages.append(st_mq)

    st_mean = kb.prog.comp(
        "vector",
        lambda eng: eng.tensor_single_scalar(mean_row[:, :], sx_ps[:, :], 1.0 / D, ALU.mult),
        deps=[mmx_stages[-1]], name=f"{name}_mean")
    st_msq = kb.prog.comp(
        "vector",
        lambda eng: eng.tensor_mul(msq_row[:, :], mean_row[:, :], mean_row[:, :]),
        deps=[st_mean], name=f"{name}_msq")
    st_var = kb.prog.comp(
        "vector",
        lambda eng: eng.scalar_tensor_tensor(
            var_row[:, :], sq_ps[:, :], 1.0 / D, msq_row[:, :], ALU.mult, ALU.subtract),
        deps=[st_msq, mmq_stages[-1]], name=f"{name}_var")
    st_ve = kb.prog.comp(
        "vector",
        lambda eng: eng.tensor_single_scalar(var_row[:, :], var_row[:, :], eps, ALU.add),
        deps=[st_var], name=f"{name}_vareps")
    st_s = kb.prog.comp(
        "scalar",
        lambda eng: eng.activation(srow[:, :], var_row[:, :], AF.Sqrt),
        deps=[st_ve], name=f"{name}_sqrt")
    st_r = kb.prog.comp(
        "vector",
        lambda eng: eng.reciprocal(rstd_row[:, :], srow[:, :]),
        deps=[st_s], name=f"{name}_recip")
    kb.free(sx_ps, sq_ps)
    mean_b = kb.ps("ln_meanb", [128, T], F32)
    rstd_b = kb.ps("ln_rstdb", [128, T], F32)
    st_bm = matmul_fm(kb, mean_b[:, :], [ones_row], [mean_row[:, :]],
                      deps=[st_mean, st_var], name=f"{name}_bm")
    st_br = matmul_fm(kb, rstd_b[:, :], [ones_row], [rstd_row[:, :]],
                      deps=[st_r], name=f"{name}_br")

    fin = []
    tmp = kb.sb("ln_t", [128, T], F32 if resid_tiles is not None else out_tiles[0].dtype)
    if resid_tiles is None:
        for g in range(nG):
            war = [fin[-1]] if fin else []
            st_t = kb.prog.comp(
                "vector",
                (lambda tg=tmp, xg=x_tiles[g]: lambda eng:
                    eng.tensor_sub(tg[:, :], xg, mean_b[:, :]))(),
                deps=[st_bm] + list(deps) + war, name=f"{name}_sub{g}")
            st_o = kb.prog.comp(
                "vector",
                (lambda og=out_tiles[g], tg=tmp: lambda eng:
                    eng.tensor_mul(og, tg[:, :], rstd_b[:, :]))(),
                deps=[st_t, st_br], name=f"{name}_mul{g}")
            fin.append(st_o)
    else:
        tmp_bufs = [tmp, sqg_t]
        adds = {}
        for g in range(nG):
            tb = tmp_bufs[g % 2]
            war = ([adds[g - 2]] if g - 2 in adds else [])
            st_t = kb.prog.comp(
                "vector",
                (lambda tg=tb, xg=x_tiles[g]: lambda eng:
                    eng.tensor_sub(tg[:, :], xg, mean_b[:, :]))(),
                deps=[st_bm] + list(deps) + war +
                     ([mmq_stages[-1]] if g == 1 else []),
                name=f"{name}_sub{g}")
            st_u = kb.prog.comp(
                "vector",
                (lambda tg=tb: lambda eng:
                    eng.tensor_mul(tg[:, :], tg[:, :], rstd_b[:, :]))(),
                deps=[st_t, st_br], name=f"{name}_mul{g}")
            st_a = kb.prog.comp(
                "gpsimd",
                (lambda rg=resid_tiles[g], tg=tb: lambda eng:
                    eng.tensor_add(rg, rg, tg[:, :]))(),
                deps=[st_u], name=f"{name}_add{g}")
            adds[g] = st_a
            fin.append(st_a)
    kb.free(tmp, mean_b, rstd_b, sqg_t, mean_row, work_row)
    return fin


def make_consts(kb):
    ones_col = kb.sb("ones_col", [128, 1], F32)
    ones_row = kb.sb("ones_row", [1, 128], F32)
    ones_col_bf = kb.sb("ones_col_bf", [128, 1], BF16)
    ones_row_bf = kb.sb("ones_row_bf", [1, 128], BF16)

    def emit(eng):
        eng.memset(ones_col[:, :], 1.0)
        eng.memset(ones_col_bf[:, :], 1.0)
        eng.memset(ones_row_bf[:, :], 1.0)
        return eng.memset(ones_row[:, :], 1.0)
    st = kb.prog.comp("gpsimd", emit, name="consts")
    kb.consts = {"ones_row_bf": ones_row_bf[:, :]}
    return ones_col[:, :], ones_row[:, :], ones_col_bf[:, :], st



def emit_attn(kb, x_tiles, T, S, WqT, WkT, WvT, mask_tri, ident_bf,
              ones_col, ones_row, ones_bf, out_attn, deps=(), name="attn",
              xln_out_tiles=None):
    P = kb.prog
    EPS = 1e-8
    import os
    phase = os.environ.get("ATTN_PHASE", "full")

    xln = kb.sb("at_xln", [128, 8, T], BF16)
    xln_tiles = [xln[:, g, :] for g in range(8)]
    ln_fin = emit_ln(kb, x_tiles, xln_tiles, T, deps=deps,
                     ones_col=ones_col, ones_row=ones_row, name=f"{name}_ln1")

    ag_in = kb.dram(kb.u("at_agin"), [1024, T], BF16)
    ag_out = kb.dram(kb.u("at_agout"), [8 * 1024, T], BF16)
    st_stage = store(kb, ag_in.ap().rearrange("(g p) t -> p g t", p=128),
                     xln[:, :, :], deps=ln_fin, name=f"{name}_stage")
    st_ag = P.coll(
        lambda eng: eng.collective_compute(
            "AllGather", ALU.bypass, replica_groups=[list(range(8))],
            ins=[ag_in.ap().opt()], outs=[ag_out.ap().opt()]),
        deps=[st_stage], name=f"{name}_ag")

    wq = kb.sb("at_wq", [128, 8, 128], BF16)
    wk = kb.sb("at_wk", [128, 8, 128], BF16)
    wv = kb.sb("at_wv", [128, 8, 128], BF16)
    st_w = P.dma(
        "gpsimd",
        lambda eng: [
            eng.dma_start(out=wq[:, :, :], in_=WqT.rearrange("(g p) m -> p g m", p=128)),
            eng.dma_start(out=wk[:, :, :], in_=WkT.rearrange("(g p) m -> p g m", p=128)),
            eng.dma_start(out=wv[:, :, :], in_=WvT.rearrange("(g p) m -> p g m", p=128)),
        ], n_dma=3, name=f"{name}_ldw")

    KV = kb.sb("at_kv", [128, 192], F32)
    KVbf = kb.sb("at_kvbf", [128, 192], BF16)
    st_kv0 = P.comp("gpsimd", lambda eng: eng.memset(KV[:, :], 0.0),
                    name=f"{name}_kv0")

    xblk = kb.sb("at_xblk", [128, 8, T], BF16)
    qf = kb.sb("at_q", [128, T], BF16)
    kf = kb.sb("at_k", [128, T], BF16)
    vf = kb.sb("at_v", [128, T], BF16)
    e_scr = kb.sb("at_escr", [128, T], F32)
    p_scr = kb.sb("at_pscr", [128, T], F32)
    vtm = [kb.sb("at_vtm", [128, 192], BF16) for _ in range(2)]
    ktm = [kb.sb("at_ktm", [128, 128], BF16) for _ in range(2)]
    ssb = [kb.sb("at_ssb", [128, 128], BF16) for _ in range(2)]
    rec_sb = kb.sb("at_rec", [128, 128], F32)

    if os.environ.get("ATTN_MEMSET", "1") == "1":
        st_ones_v = P.comp(
            "gpsimd",
            lambda eng: (eng.memset(vtm[0][:, 128:192], 1.0),
                         eng.memset(vtm[1][:, 128:192], 1.0),
                         eng.memset(out_attn[:, :], 0.0))[-1],
            name=f"{name}_vones")
    else:
        st_ones_v = P.comp(
            "gpsimd",
            lambda eng: (eng.memset(vtm[0][:, 128:192], 1.0),
                         eng.memset(vtm[1][:, 128:192], 1.0))[-1],
            name=f"{name}_vones")

    tp_k_t = kb.ps("at_tpk", [128, 128], BF16)
    tp_v_t = kb.ps("at_tpv", [128, 128], BF16)
    tp_k = tp_k_t[:, :]
    tp_v = tp_v_t[:, :]
    PW = T if os.environ.get("PROJ_FULL") else min(T, 512)
    pps = kb.ps("at_pp", [128, PW], F32)
    if phase != "proj":
        num_ps_t = kb.ps("at_num", [128, 128], F32)
        den_ps_t = kb.ps("at_den", [128, 128], F32)
        stp_t = kb.ps("at_stp", [128, 128], F32)
        stp_t2 = kb.ps("at_stp2", [128, 128], F32)
        kvd_t = kb.ps("at_kvd", [128, 192], F32)
        num_ps = num_ps_t[:, :]
        den_ps = den_ps_t[:, :]
        stp2 = [stp_t[:, :], stp_t2[:, :]]
        kvd = kvd_t[:, :]

    n_blocks = int(os.environ.get("ATTN_BLOCKS", S // T))
    n_chunks_per_blk = T // 128
    last_stages = []
    prev_blk_tail = []
    prev = None

    for r in range(n_blocks):
        st_x = load(kb, xblk[:, :, :],
                    ag_out.ap()[r * 1024:(r + 1) * 1024, :].rearrange(
                        "(g p) t -> p g t", p=128),
                    deps=[st_ag] + prev_blk_tail, name=f"{name}_x{r}")
        proj = {}
        prev_pps_readers = []
        prev_elu_done = []
        W = PW
        for wname, wsb, osb, do_elu in (("q", wq, qf, True), ("k", wk, kf, True),
                                        ("v", wv, vf, False)):
            st_os = []
            for hf in range(T // W):
                hsl2 = slice(hf * W, (hf + 1) * W)
                st_mm = matmul_fm(kb, pps[:, :],
                                  [wsb[:, g, :] for g in range(8)],
                                  [xblk[:, g, hsl2] for g in range(8)],
                                  deps=[st_x, st_w] + prev_pps_readers,
                                  name=f"{name}_{wname}mm{r}_{hf}")
                if do_elu:
                    st_min = P.comp("vector",
                                    (lambda h2=hsl2: lambda eng: eng.tensor_scalar_min(
                                        e_scr[:, h2], pps[:, :], 0.0))(),
                                    deps=[st_mm] + prev_elu_done,
                                    name=f"{name}_{wname}min{r}_{hf}")
                    st_exp = P.comp("scalar",
                                    (lambda h2=hsl2: lambda eng: eng.activation(
                                        e_scr[:, h2], e_scr[:, h2], AF.Exp))(),
                                    deps=[st_min], name=f"{name}_{wname}exp{r}_{hf}")
                    st_max = P.comp("vector",
                                    (lambda h2=hsl2: lambda eng: eng.tensor_scalar_max(
                                        p_scr[:, h2], pps[:, :], 0.0))(),
                                    deps=[st_mm] + prev_elu_done,
                                    name=f"{name}_{wname}max{r}_{hf}")
                    st_o = P.comp("vector",
                                  (lambda o=osb, h2=hsl2: lambda eng: eng.tensor_add(
                                      o[:, h2], e_scr[:, h2], p_scr[:, h2]))(),
                                  deps=[st_exp, st_max],
                                  name=f"{name}_{wname}elu{r}_{hf}")
                    prev_pps_readers = [st_min, st_max]
                    prev_elu_done = [st_o]
                else:
                    st_o = P.comp("scalar",
                                  (lambda o=osb, h2=hsl2: lambda eng: eng.copy(
                                      o[:, h2], pps[:, :]))(),
                                  deps=[st_mm], name=f"{name}_vcp{r}_{hf}")
                    prev_pps_readers = [st_o]
                st_os.append(st_o)
            proj[wname] = st_os[-1]

        if phase == "proj":
            st_cp = P.comp("vector",
                           (lambda r=r: lambda eng: eng.tensor_copy(
                               out_attn[:, r * T:(r + 1) * T], qf[:, :]))(),
                           deps=[proj["q"], proj["k"], proj["v"]],
                           name=f"{name}_projcp{r}")
            prev_blk_tail = [st_cp]
            last_stages = [st_cp]
            continue
        blk_readers = []
        for c in range(n_chunks_per_blk):
            gi = r * n_chunks_per_blk + c
            sl = slice(c * 128, (c + 1) * 128)
            gsl = slice(gi * 128, (gi + 1) * 128)
            buf = gi % 2

            dep_tp = [proj["k"], proj["v"]]
            if prev is not None:
                dep_tp = dep_tp + [prev["ek"], prev["ev"], prev["kv_add"]]
                if prev["prev2_tm_readers"]:
                    dep_tp = dep_tp + prev["prev2_tm_readers"]
            st_tk = P.comp("tensor",
                           (lambda s=sl: lambda eng:
                            eng.transpose(tp_k[:, :], kf[:, s], ident_bf))(),
                           deps=dep_tp, name=f"{name}_tk{gi}")
            st_tv = P.comp("tensor",
                           (lambda s=sl: lambda eng:
                            eng.transpose(tp_v[:, :], vf[:, s], ident_bf))(),
                           deps=dep_tp, name=f"{name}_tv{gi}")
            if phase == "tmo":
                prev = {"kv_add": st_tv, "kvbf_readers": [st_tk, st_tv],
                        "tm_readers": [st_tk, st_tv], "prev2_tm_readers":
                        (prev["tm_readers"] if prev is not None else []),
                        "ssb_readers": [st_tk, st_tv], "msks": [st_tk, st_tv],
                        "out_stages": [st_tk, st_tv], "ek": st_tk, "ev": st_tv}
                blk_readers.extend([st_tk, st_tv])
                continue
            st_ek = P.comp("scalar",
                           (lambda d=ktm[buf]: lambda eng:
                            eng.copy(d[:, :], tp_k[:, :]))(),
                           deps=[st_tk], name=f"{name}_ek{gi}")
            st_ev = P.comp("scalar",
                           (lambda d=vtm[buf]: lambda eng:
                            eng.copy(d[:, 0:128], tp_v[:, :]))(),
                           deps=[st_tv, st_ones_v], name=f"{name}_ev{gi}")

            if phase == "tm":
                prev = {"kv_add": st_ev, "kvbf_readers": [st_ek, st_ev],
                        "tm_readers": [st_ek, st_ev], "prev2_tm_readers":
                        (prev["tm_readers"] if prev is not None else []),
                        "ssb_readers": [st_ek, st_ek], "msks": [st_ek, st_ev],
                        "out_stages": [st_ek, st_ev], "ek": st_ek, "ev": st_ev}
                blk_readers.extend([st_ek, st_ev])
                continue
            def emit_st(eng, s=sl):
                eng.ldweights(kf[0:64, s])
                eng.matmul(stp2[0], kf[0:64, s], qf[0:64, s],
                           start=True, stop=True)
                eng.ldweights(kf[64:128, s], tile_position=(64, 0))
                return eng.matmul(stp2[1], kf[64:128, s], qf[64:128, s],
                                  start=True, stop=True, tile_position=(64, 0))
            st_stmm = P.comp("tensor", emit_st,
                             deps=[proj["k"], proj["q"]] +
                                  ([prev["msk"]] if prev is not None else []),
                             name=f"{name}_st{gi}")
            st_msk = P.comp("vector",
                            lambda eng: (
                                eng.tensor_mul(ssb[0][:, :], stp2[0], mask_tri),
                                eng.tensor_mul(ssb[1][:, :], stp2[1], mask_tri))[-1],
                            deps=[st_stmm] +
                                 ([prev["nd"]] if prev is not None else []),
                            name=f"{name}_msk{gi}")

            if gi > 0:
                st_kvbf = P.comp("scalar",
                                 lambda eng: eng.copy(KVbf[:, :], KV[:, :]),
                                 deps=[prev["kv_add"], prev["nd"]],
                                 name=f"{name}_kvbf{gi}")
            else:
                st_kvbf = None

            first = (gi == 0)

            def emit_nd(eng, s=sl, first=first, buf=buf):
                last = None
                for h in range(2):
                    hsl = slice(h * 64, (h + 1) * 64)
                    tp0 = (0, 64) if h == 1 else None
                    tpq = (64, 64) if h == 1 else None
                    eng.ldweights(vtm[buf][:, h * 64:(h + 1) * 64],
                                  tile_position=tp0)
                    eng.matmul(num_ps[hsl, :], vtm[buf][:, h * 64:(h + 1) * 64],
                               ssb[h][:, :], start=True, stop=first,
                               tile_position=tp0)
                    if not first:
                        eng.ldweights(KVbf[hsl, h * 64:(h + 1) * 64],
                                      tile_position=tpq)
                        eng.matmul(num_ps[hsl, :], KVbf[hsl, h * 64:(h + 1) * 64],
                                   qf[hsl, s], start=False, stop=True,
                                   tile_position=tpq)
                    eng.ldweights(ones_bf[:, 0:64], tile_position=tp0)
                    last = eng.matmul(den_ps[hsl, :], ones_bf[:, 0:64],
                                      ssb[h][:, :], start=True, stop=first,
                                      tile_position=tp0)
                    if not first:
                        eng.ldweights(KVbf[hsl, 128:192], tile_position=tpq)
                        last = eng.matmul(den_ps[hsl, :], KVbf[hsl, 128:192],
                                          qf[hsl, s], start=False, stop=True,
                                          tile_position=tpq)
                return last
            deps_nd = [st_msk, st_ev, proj["q"]]
            if st_kvbf is not None:
                deps_nd.append(st_kvbf)
            if prev is not None:
                deps_nd.extend(prev["out_stages"])
            st_nd = P.comp("tensor", emit_nd, deps=deps_nd,
                           name=f"{name}_nd{gi}")
            st_rec = P.comp("vector",
                            lambda eng: (
                                eng.tensor_single_scalar(
                                    rec_sb[:, :], den_ps[:, :], EPS, ALU.add),
                                eng.reciprocal(rec_sb[:, :], rec_sb[:, :]))[-1],
                            deps=[st_nd], name=f"{name}_rec{gi}")
            st_out = P.comp("vector",
                            (lambda s=gsl: lambda eng:
                             eng.tensor_mul(out_attn[:, s], num_ps[:, :],
                                            rec_sb[:, :]))(),
                            deps=[st_nd, st_rec], name=f"{name}_out{gi}")
            na_out = [st_out]

            kv_deps = [st_ek, st_ev, st_kv0]
            if prev is not None:
                kv_deps.append(prev["kv_add"])

            st_kvd = P.comp("tensor",
                            (lambda buf=buf: lambda eng: (
                                eng.ldweights(ktm[buf][:, :]),
                                eng.matmul(kvd[:, :], ktm[buf][:, :], vtm[buf][:, :],
                                           start=True, stop=True))[-1])(),
                            deps=kv_deps, name=f"{name}_kvd{gi}")
            st_kvadd = P.comp("vector",
                              lambda eng: eng.tensor_add(KV[:, :], KV[:, :], kvd[:, :]),
                              deps=[st_kvd] + ([st_kvbf] if st_kvbf is not None else []),
                              name=f"{name}_kvadd{gi}")

            prev2_tm = prev["tm_readers"] if prev is not None else []
            prev = {
                "kv_add": st_kvadd,
                "kvbf_readers": [st_nd],
                "tm_readers": [st_nd, st_kvd],
                "prev2_tm_readers": prev2_tm,
                "msk": st_msk,
                "nd": st_nd,
                "out_stages": [st_rec, st_out],
                "ek": st_ek,
                "ev": st_ev,
            }
            blk_readers.extend(na_out)
        prev_blk_tail = blk_readers[-4:] + [prev["kv_add"]]
        last_stages = blk_readers[-2:]
    kb.free(pps, num_ps_t, den_ps_t, stp_t, stp_t2, kvd_t, tp_k_t, tp_v_t)
    kb.free(rec_sb, ssb[1], ssb[0], ktm[1], ktm[0], vtm[1], vtm[0],
            p_scr, e_scr, vf, kf, qf, xblk, KVbf, KV, wv, wk, wq, xln)
    return last_stages


def emit_l0rest(kb, x_tiles, T, attn_src, WoT, gate_wT, gate_b, W13_all, W2_all,
                b13_cols, be2, bs2_row, sel18, mask128, identf,
                ones_col, ones_row, ones_col_bf, deps=(), name="l0r",
                dbg_outs=None):
    P = kb.prog
    NE = 18
    NC = T // 512

    wo_out = kb.sb("lr_woout", [128, 8, T], BF16)
    wo_sb = kb.sb("lr_wo", [128, 8, 1024], BF16)
    attn_sb = kb.sb("lr_attn", [128, 8, T], BF16)
    st_a = load(kb, attn_sb[:, :, :], attn_src.rearrange("(g p) t -> p g t", p=128),
                deps=deps, name=f"{name}_lda")
    st_w = P.dma("gpsimd",
                 lambda eng: [eng.dma_start(out=wo_sb[:, :, :],
                                            in_=WoT.rearrange("(g p) m -> p g m", p=128))],
                 deps=list(deps), n_dma=1, name=f"{name}_ldwo")
    WOW = min(T, 1024)
    pp = kb.ps("lr_pp", [128, WOW], F32)
    ev_prev = []
    mm_prev = []
    for hf in range(T // WOW):
        h2 = slice(hf * WOW, (hf + 1) * WOW)
        for mg in range(8):
            st_mm = matmul_fm(kb, pp[:, :],
                              [wo_sb[:, kg, mg * 128:(mg + 1) * 128] for kg in range(8)],
                              [attn_sb[:, kg, h2] for kg in range(8)],
                              deps=[st_a, st_w] + ev_prev, name=f"{name}_womm{hf}_{mg}")
            st_ev = P.comp("scalar",
                           (lambda mg=mg, h2=h2: lambda eng:
                            eng.copy(wo_out[:, mg, h2], pp[:, :]))(),
                           deps=[st_mm], name=f"{name}_woev{hf}_{mg}")
            ev_prev = [st_ev]
            mm_prev.append(st_ev)
    kb.free(pp)
    kb.free(attn_sb, wo_sb)
    wo_tiles = [wo_out[:, g, :] for g in range(8)]
    ln2_fin = emit_ln(kb, wo_tiles, None, T, resid_tiles=x_tiles,
                      deps=mm_prev, ones_col=ones_col, ones_row=ones_row,
                      ones_col_bf=ones_col_bf, name=f"{name}_ln2")
    kb.free(wo_out)

    y_sb = kb.sb("lr_y", [128, 8, T], F32)
    xln2 = kb.sb("lr_xln2", [128, 8, T], BF16)
    xln2_tiles = [xln2[:, g, :] for g in range(8)]
    lnm_fin = emit_ln(kb, x_tiles, xln2_tiles, T, deps=ln2_fin,
                      ones_col=ones_col, ones_row=ones_row, name=f"{name}_lnm")

    sel_sb = kb.sb("lr_sel", [18, NE * 128], BF16)
    be2_sb = kb.sb("lr_be2", [16, 1024], BF16)
    bs2c_sb = kb.sb("lr_bs2c", [128, 8], F32)
    b13_sb = kb.sb("lr_b13", [128, 8, NE], F32)
    comb_sb = kb.sb("lr_comb", [18, T], BF16)
    gw_sb = kb.sb("lr_gw", [128, 8, 16], BF16)
    gb_sb = kb.sb("lr_gb", [1, 16], F32)
    st_gw = P.dma("gpsimd",
                  lambda eng: [
                      eng.dma_start(out=gw_sb[:, :, :],
                                    in_=gate_wT.rearrange("(g p) m -> p g m", p=128)),
                      eng.dma_start(out=gb_sb[:, :], in_=gate_b),
                      eng.dma_start(out=sel_sb[:, :], in_=sel18),
                      eng.dma_start(out=be2_sb[:, :], in_=be2),
                      eng.dma_start(out=bs2c_sb[:, :],
                                    in_=bs2_row.rearrange("a (g p) -> (a p) g", p=128),
                                    allow_slow_non_contiguous=True),
                      eng.dma_start(out=b13_sb[:, :, :], in_=b13_cols),
                  ], deps=list(lnm_fin), n_dma=6, name=f"{name}_ldg")

    sc_sb = kb.sb("lr_scsb", [16, T], BF16)
    identbf_g = kb.sb("lr_idbf", [128, 128], BF16)
    gbb_sb = kb.sb("lr_gbbsb", [128, 16], F32)
    sc_ps = kb.ps("lr_scps", [16, T], F32)
    st_scmm = matmul_fm(kb, sc_ps[:, :],
                        [gw_sb[:, kg, :] for kg in range(8)],
                        xln2_tiles,
                        deps=lnm_fin + [st_gw], name=f"{name}_scmm")
    st_scev = P.comp("vector",
                     lambda eng: eng.tensor_copy(sc_sb[:, :], sc_ps[:, :]),
                     deps=[st_scmm], name=f"{name}_scev")
    st_idbf = P.comp("vector",
                     lambda eng: eng.tensor_copy(identbf_g[:, :], identf),
                     deps=list(deps), name=f"{name}_idbf")
    kb.free(sc_ps)

    gbb_ps = kb.ps("lr_gbb", [128, 16], F32)
    st_gbb = matmul_fm(kb, gbb_ps[:, :], [ones_row], [gb_sb[:, :]],
                       deps=[st_gw, st_scev], name=f"{name}_gbb")
    st_gbbe = P.comp("vector",
                     lambda eng: eng.tensor_copy(gbb_sb[:, :], gbb_ps[:, :]),
                     deps=[st_gbb], name=f"{name}_gbbe")
    kb.free(gbb_ps)

    st_comb1 = P.comp("gpsimd", lambda eng: eng.memset(comb_sb[:, :], 1.0),
                      deps=list(deps), name=f"{name}_comb1")

    ntile = T // 128
    sc_all = kb.sb("lr_scall", [128, ntile, 16], F32)
    sm_all = kb.sb("lr_small", [128, ntile, 16], F32)
    s3_all = kb.sb("lr_s3all", [128, ntile, 16], F32)
    red_a = kb.sb("lr_reda", [128, ntile, 2], F32)
    gs_a = kb.sb("lr_gsa", [128, ntile, 8], F32)
    pr_a = kb.sb("lr_pra", [128, ntile, 24], F32)
    keep_a = kb.sb("lr_keepa", [128, ntile, 8], F32)
    tp_ps = kb.ps("lr_tpps", [128, 16], BF16)
    ct_ps = kb.ps("lr_ctps", [16, 128], F32)

    tfin = []
    prev_cp = None
    for t in range(ntile):
        tsl = slice(t * 128, (t + 1) * 128)
        st_tp = P.comp("tensor",
                       (lambda ts=tsl: lambda eng:
                        eng.transpose(tp_ps[:, :], sc_sb[:, ts], identbf_g[0:16, 0:16]))(),
                       deps=[st_scev, st_gbbe, st_idbf] +
                            ([prev_cp] if prev_cp else []),
                       name=f"{name}_gtp{t}")
        st_cp = P.comp("vector",
                       (lambda t=t: lambda eng:
                        eng.tensor_copy(sc_all[:, t, :], tp_ps[:, :]))(),
                       deps=[st_tp], name=f"{name}_gcp{t}")
        prev_cp = st_cp
        tfin.append(st_cp)

    def stage(nm, fn, deps):
        return P.comp("vector", fn, deps=deps, name=f"{name}_{nm}")

    st1 = stage("gmax", lambda eng: eng.tensor_reduce(
        red_a[:, :, 0:1], sc_all[:, :, :], AX.X, ALU.max), tfin)
    st2 = stage("gsub", lambda eng: eng.tensor_tensor(
        sm_all[:, :, :], sc_all[:, :, :],
        red_a[:, :, 0:1].broadcast_to([128, ntile, 16]), ALU.subtract), [st1])
    st3 = P.comp("scalar", lambda eng: eng.activation(
        sm_all[:, :, :], sm_all[:, :, :], AF.Exp), deps=[st2],
        name=f"{name}_gexp")
    st4 = stage("gsum", lambda eng: eng.tensor_reduce(
        red_a[:, :, 1:2], sm_all[:, :, :], AX.X, ALU.add), [st3])
    st5 = stage("grec", lambda eng: eng.reciprocal(
        red_a[:, :, 1:2], red_a[:, :, 1:2]), [st4])
    st6 = stage("gnrm", lambda eng: eng.tensor_tensor(
        sm_all[:, :, :], sm_all[:, :, :],
        red_a[:, :, 1:2].broadcast_to([128, ntile, 16]), ALU.mult), [st5])
    st7 = stage("gs2", lambda eng: eng.tensor_tensor(
        s3_all[:, :, :], sm_all[:, :, :],
        gbb_sb[:, :].rearrange("p e -> p () e").broadcast_to([128, ntile, 16]),
        ALU.add), [st6, st_gbbe])

    pairs = [(0, 1), (0, 2), (0, 3), (1, 2), (1, 3), (2, 3)]

    def emit_pairsum(eng):
        v4 = s3_all[:, :, :].rearrange("p t (a b) -> p t a b", a=4)
        for i, (a, b) in enumerate(pairs):
            eng.tensor_add(pr_a[:, :, 4 * i:4 * i + 4], v4[:, :, :, a], v4[:, :, :, b])
        return eng.tensor_copy(red_a[:, :, 0:1], red_a[:, :, 0:1])
    st8 = stage("gps", emit_pairsum, [st7])

    def emit_gsmax(eng):
        eng.tensor_max(gs_a[:, :, 0:4], pr_a[:, :, 0:4], pr_a[:, :, 4:8])
        eng.tensor_max(gs_a[:, :, 4:8], pr_a[:, :, 8:12], pr_a[:, :, 12:16])
        return eng.tensor_max(pr_a[:, :, 16:20], pr_a[:, :, 16:20], pr_a[:, :, 20:24])
    st9 = stage("ggm0", emit_gsmax, [st8])

    def emit_gsmax2(eng):
        eng.tensor_max(gs_a[:, :, 0:4], gs_a[:, :, 0:4], gs_a[:, :, 4:8])
        return eng.tensor_copy(red_a[:, :, 0:1], red_a[:, :, 0:1])
    st10 = stage("ggm1", emit_gsmax2, [st9])
    st11 = stage("ggm2", lambda eng: eng.tensor_max(
        gs_a[:, :, 0:4], gs_a[:, :, 0:4], pr_a[:, :, 16:20]), [st10])

    def emit_pairmin(eng):
        for i, (a, b) in enumerate(pairs):
            eng.tensor_tensor(pr_a[:, :, i:i + 1], gs_a[:, :, a:a + 1],
                              gs_a[:, :, b:b + 1], ALU.min)
        return eng.tensor_copy(red_a[:, :, 0:1], red_a[:, :, 0:1])
    st12 = stage("gpm", emit_pairmin, [st11])

    def emit_sm2(eng):
        eng.tensor_max(pr_a[:, :, 6:7], pr_a[:, :, 0:1], pr_a[:, :, 1:2])
        eng.tensor_max(pr_a[:, :, 7:8], pr_a[:, :, 2:3], pr_a[:, :, 3:4])
        return eng.tensor_max(pr_a[:, :, 8:9], pr_a[:, :, 4:5], pr_a[:, :, 5:6])
    st13 = stage("gsm2a", emit_sm2, [st12])
    st14 = stage("gsm2b", lambda eng: eng.tensor_max(
        pr_a[:, :, 6:7], pr_a[:, :, 6:7], pr_a[:, :, 7:8]), [st13])
    st15 = stage("gsm2c", lambda eng: eng.tensor_max(
        pr_a[:, :, 6:7], pr_a[:, :, 6:7], pr_a[:, :, 8:9]), [st14])
    st16 = stage("gkeep", lambda eng: eng.tensor_tensor(
        keep_a[:, :, 0:4], gs_a[:, :, 0:4],
        pr_a[:, :, 6:7].broadcast_to([128, ntile, 4]), ALU.is_ge), [st15])

    def emit_s3(eng):
        v4 = s3_all[:, :, :].rearrange("p t (a b) -> p t a b", a=4)
        kb4 = keep_a[:, :, 0:4].rearrange("p t a -> p t a ()").broadcast_to(
            [128, ntile, 4, 4])
        eng.tensor_tensor(v4, v4, kb4, ALU.mult)
        eng.tensor_scalar(keep_a[:, :, 4:8], keep_a[:, :, 0:4], 1e30, 1e30,
                          ALU.mult, ALU.subtract)
        return eng.tensor_copy(red_a[:, :, 0:1], red_a[:, :, 0:1])
    st17 = stage("gs3", emit_s3, [st16])
    st18 = stage("gs3b", lambda eng: eng.tensor_tensor(
        s3_all[:, :, :].rearrange("p t (a b) -> p t a b", a=4),
        s3_all[:, :, :].rearrange("p t (a b) -> p t a b", a=4),
        keep_a[:, :, 4:8].rearrange("p t a -> p t a ()").broadcast_to(
            [128, ntile, 4, 4]), ALU.add), [st17])
    st19 = stage("gm1", lambda eng: eng.tensor_reduce(
        red_a[:, :, 0:1], s3_all[:, :, :], AX.X, ALU.max), [st18])
    st20 = stage("geq", lambda eng: eng.tensor_tensor(
        pr_a[:, :, 0:16], s3_all[:, :, :],
        red_a[:, :, 0:1].broadcast_to([128, ntile, 16]), ALU.is_equal), [st19])
    st21 = stage("gs4", lambda eng: eng.scalar_tensor_tensor(
        pr_a[:, :, 0:16], pr_a[:, :, 0:16], -1e30, s3_all[:, :, :],
        ALU.mult, ALU.add), [st20])
    st22 = stage("gm2", lambda eng: eng.tensor_reduce(
        red_a[:, :, 1:2], pr_a[:, :, 0:16], AX.X, ALU.max), [st21])
    st23 = stage("gsel", lambda eng: eng.tensor_tensor(
        pr_a[:, :, 0:16], s3_all[:, :, :],
        red_a[:, :, 1:2].broadcast_to([128, ntile, 16]), ALU.is_ge), [st22])
    st24 = stage("gcomb", lambda eng: eng.tensor_mul(
        sm_all[:, :, :], sm_all[:, :, :], pr_a[:, :, 0:16]), [st23])

    comb_done = []
    prev_ce = None
    for t in range(ntile):
        tsl = slice(t * 128, (t + 1) * 128)
        st_ct = P.comp("tensor",
                       (lambda t=t: lambda eng:
                        eng.transpose(ct_ps[:, :], sm_all[:, t, :], identf))(),
                       deps=[st24] + ([prev_ce] if prev_ce else []),
                       name=f"{name}_gct{t}")
        st_ce = P.comp("vector",
                       (lambda ts=tsl: lambda eng:
                        eng.tensor_copy(comb_sb[0:16, ts], ct_ps[:, :][0:16, :]))(),
                       deps=[st_ct, st_comb1], name=f"{name}_gce{t}")
        comb_done.append(st_ce)
        prev_ce = st_ce
    kb.free(tp_ps, ct_ps)
    kb.free(keep_a, pr_a, gs_a, red_a, s3_all, sm_all, sc_all)
    kb.free(gbb_sb, identbf_g, sc_sb, gb_sb, gw_sb)

    if dbg_outs and "comb" in dbg_outs:
        store(kb, dbg_outs["comb"], comb_sb[:, :], deps=comb_done,
              name=f"{name}_dbgcomb")
    if dbg_outs and "x1" in dbg_outs:
        for g in range(8):
            store(kb, dbg_outs["x1"][g * 128:(g + 1) * 128, :], x_tiles[g],
                  deps=lnm_fin, name=f"{name}_dbgx1_{g}")
    xspill = kb.dram(kb.u("lr_xspill"), [6 * 128, T], F32)
    st_xsp = P.dma("gpsimd",
                   lambda eng: [eng.dma_start(
                       out=xspill.ap()[g * 128:(g + 1) * 128, :], in_=x_tiles[g])
                       for g in range(6)],
                   deps=list(ln2_fin), n_dma=6, name=f"{name}_xsp")
    t3_01 = x_tiles[0].bitcast(BF16).rearrange("p (a t) -> p a t", a=2)
    t3_23 = x_tiles[1].bitcast(BF16).rearrange("p (a t) -> p a t", a=2)
    t3v = [t3_01[:, 0, :], t3_01[:, 1, :], t3_23[:, 0, :], t3_23[:, 1, :]]
    z_v2 = x_tiles[2].bitcast(BF16).rearrange("p (a t) -> p a t", a=2)
    zv = [z_v2[:, 0, :], z_v2[:, 1, :]]
    s1_v2 = x_tiles[3].bitcast(BF16).rearrange("p (a t) -> p a t", a=2)
    s1v = [s1_v2[:, 0, :], s1_v2[:, 1, :]]
    comb_bc = x_tiles[4].bitcast(BF16).rearrange("p (a t) -> p a t", a=2)[:, 0, :]

    moe_ps = kb.ps("lr_moeps", [128, 8, 512], F32)
    pgA = moe_ps[:, 0:4, :].rearrange("p a t -> p (a t)")
    pgB = moe_ps[:, 4:8, :].rearrange("p a t -> p (a t)")

    seed_done = []
    seed_prev = None
    for c in range(NC):
        csl = slice(c * 512, (c + 1) * 512)

        def emit_seed(eng, csl=csl):
            last = None
            for mg in range(8):
                msl = slice(mg * 128, (mg + 1) * 128)
                eng.ldweights(be2_sb[:, msl])
                last = eng.matmul(moe_ps[:, mg, :], be2_sb[:, msl],
                                  comb_sb[0:16, csl], start=True, stop=True)
            return last
        st_seed = P.comp("tensor", emit_seed,
                         deps=comb_done + [st_gw] +
                              ([seed_prev] if seed_prev else []),
                         name=f"{name}_seed{c}")

        def emit_sev(eng, csl=csl):
            last = None
            for mg in range(8):
                last = eng.activation(y_sb[:, mg, csl], moe_ps[:, mg, :],
                                      AF.Identity, bias=bs2c_sb[:, mg:mg + 1])
            return last
        st_sev = P.comp("scalar", emit_sev, deps=[st_seed],
                        name=f"{name}_seedev{c}")
        seed_prev = st_sev
        seed_done.append(st_sev)

    w13_sb = [kb.sb("lr_w13", [128, 8, 1024], BF16) for _ in range(2)]
    if 2 * T >= 4096:
        w2_sb = x_tiles[5].bitcast(BF16).rearrange("p (a m) -> p a m", a=4)
        w2_alloc = None
    else:
        w2_alloc = kb.sb("lr_w2", [128, 4, 1024], BF16)
        w2_sb = w2_alloc[:, :, :]
    w13_reader = [None, None]
    w2_reader = None
    hp_prev = None
    hp_par_prev = [None, None]
    last_w2 = None
    zev_prev = [None, None]
    acc_last = [None] * 8
    pool_accs = []
    for e in range(NE):
        buf = e % 2
        st_we = P.dma("gpsimd",
                      (lambda e=e, buf=buf: lambda eng: [
                          eng.dma_start(out=w13_sb[buf][:, :, :],
                                        in_=W13_all[e].rearrange(
                                            "(g p) m -> p g m", p=128))])(),
                      deps=([w13_reader[buf]] if w13_reader[buf]
                            else list(comb_done)),
                      n_dma=1, name=f"{name}_we{e}")
        st_w2ld = P.dma("gpsimd",
                        (lambda e=e: lambda eng: [
                            eng.dma_start(out=w2_sb[:, :, :],
                                          in_=W2_all[e].rearrange(
                                              "(g p) m -> p g m", p=128))])(),
                        deps=[st_xsp] +
                             ([w2_reader] if w2_reader else list(comb_done)),
                        n_dma=1, name=f"{name}_w2ld{e}")
        st_cb = P.comp("tensor",
                       (lambda e=e: lambda eng: (
                           eng.ldweights(sel_sb[:, e * 128:(e + 1) * 128]),
                           [eng.matmul(pgA[:, c0:min(c0 + 512, T)],
                                       sel_sb[:, e * 128:(e + 1) * 128],
                                       comb_sb[:, c0:min(c0 + 512, T)],
                                       start=True, stop=True)
                            for c0 in range(0, T, 512)])[-1][-1])(),
                       deps=comb_done + [st_gw] +
                            ([zev_prev[0]] if zev_prev[0] else seed_done),
                       name=f"{name}_cb{e}")
        st_cbe = P.comp("scalar",
                        lambda eng: eng.copy(comb_bc, pgA[:, 0:T]),
                        deps=[st_cb, st_xsp] +
                             ([hp_prev] if hp_prev else []),
                        name=f"{name}_cbe{e}")
        evA_prev = None
        evB_prev = None
        hps = []
        for ig in range(4):
            sp = ig % 2

            def emit_mmA(eng, ig=ig, buf=buf):
                last = None
                for kg in range(8):
                    eng.ldweights(w13_sb[buf][:, kg, ig * 128:(ig + 1) * 128])
                    for c0 in range(0, T, 512):
                        last = eng.matmul(pgA[:, c0:min(c0 + 512, T)],
                                          w13_sb[buf][:, kg, ig * 128:(ig + 1) * 128],
                                          xln2[:, kg, c0:min(c0 + 512, T)],
                                          start=(kg == 0), stop=(kg == 7))
                return last
            st_mmA = P.comp("tensor", emit_mmA,
                            deps=[st_we] +
                                 ([evA_prev] if evA_prev else [st_cbe]),
                            name=f"{name}_mA{e}i{ig}")

            def emit_mmB(eng, ig=ig, buf=buf):
                last = None
                for kg in range(8):
                    eng.ldweights(w13_sb[buf][:, kg, (4 + ig) * 128:(5 + ig) * 128])
                    for c0 in range(0, T, 512):
                        last = eng.matmul(pgB[:, c0:min(c0 + 512, T)],
                                          w13_sb[buf][:, kg, (4 + ig) * 128:(5 + ig) * 128],
                                          xln2[:, kg, c0:min(c0 + 512, T)],
                                          start=(kg == 0), stop=(kg == 7))
                return last
            st_mmB = P.comp("tensor", emit_mmB,
                            deps=[st_we] +
                                 ([evB_prev] if evB_prev else
                                  ([zev_prev[1]] if zev_prev[1] else seed_done)),
                            name=f"{name}_mB{e}i{ig}")
            st_evA = P.comp("scalar",
                            (lambda ig=ig, sp=sp, e=e: lambda eng:
                             eng.activation(s1v[sp], pgA[:, 0:T], AF.Silu,
                                            bias=b13_sb[:, ig, e:e + 1]))(),
                            deps=[st_mmA, st_xsp] +
                                 ([hps[ig - 2]] if ig >= 2 else
                                  ([hp_par_prev[sp]] if hp_par_prev[sp] else [])),
                            name=f"{name}_evA{e}i{ig}")
            st_evB = P.comp("scalar",
                            (lambda ig=ig, e=e: lambda eng:
                             eng.activation(t3v[ig], pgB[:, 0:T], AF.Identity,
                                            bias=b13_sb[:, 4 + ig, e:e + 1]))(),
                            deps=[st_mmB, st_xsp] +
                                 ([last_w2] if last_w2 else []),
                            name=f"{name}_evB{e}i{ig}")
            st_hp = P.comp("vector",
                           (lambda ig=ig, sp=sp: lambda eng: (
                               eng.tensor_mul(t3v[ig], s1v[sp], t3v[ig]),
                               eng.tensor_mul(t3v[ig], t3v[ig], comb_bc))[-1])(),
                           deps=[st_evA, st_evB, st_cbe],
                           name=f"{name}_hp{e}i{ig}")
            hps.append(st_hp)
            evA_prev = st_evA
            evB_prev = st_evB
        zev_now = [None, None]
        for og2 in range(8):
            wb = og2 % 2
            pg = pgA if wb == 0 else pgB

            def emit_w2(eng, og2=og2, pg=pg):
                last = None
                for kg in range(4):
                    eng.ldweights(w2_sb[:, kg, og2 * 128:(og2 + 1) * 128])
                    for c0 in range(0, T, 512):
                        last = eng.matmul(pg[:, c0:min(c0 + 512, T)],
                                          w2_sb[:, kg, og2 * 128:(og2 + 1) * 128],
                                          t3v[kg][:, c0:min(c0 + 512, T)],
                                          start=(kg == 0), stop=(kg == 3))
                return last
            w2deps = [st_w2ld] + hps
            if zev_now[wb] is not None:
                w2deps.append(zev_now[wb])
            elif og2 == 0:
                w2deps.append(evA_prev)
            elif og2 == 1:
                w2deps.append(evB_prev)
            st_w2 = P.comp("tensor", emit_w2, deps=w2deps,
                           name=f"{name}_w2e{e}o{og2}")
            st_zev = P.comp("scalar",
                            (lambda wb=wb, pg=pg: lambda eng:
                             eng.copy(zv[wb], pg[:, 0:T]))(),
                            deps=[st_w2, st_xsp] +
                                 ([acc_last[og2 - 2]] if og2 >= 2 else
                                  [a for a in (acc_last[6], acc_last[7])
                                   if a is not None][-1:]),
                            name=f"{name}_zev{e}o{og2}")
            st_acc = P.comp("gpsimd",
                            (lambda og2=og2, wb=wb: lambda eng:
                             eng.tensor_add(y_sb[:, og2, :], y_sb[:, og2, :],
                                            zv[wb]))(),
                            deps=[st_zev] +
                                 ([acc_last[og2]] if acc_last[og2]
                                  else seed_done),
                            name=f"{name}_acc{e}o{og2}")
            zev_now[wb] = st_zev
            acc_last[og2] = st_acc
            pool_accs.append(st_acc)
            last_w2 = st_w2
        zev_prev = zev_now
        hp_prev = hps[3]
        hp_par_prev = [hps[2], hps[3]]
        w13_reader[buf] = st_mmB
        w2_reader = last_w2
    kb.free(moe_ps)

    if dbg_outs and "y" in dbg_outs:
        for g in range(8):
            store(kb, dbg_outs["y"][g * 128:(g + 1) * 128, :], y_sb[:, g, :],
                  deps=[a for a in acc_last if a is not None],
                  name=f"{name}_dbgy_{g}")
    st_xrl = P.dma("gpsimd",
                   lambda eng: [eng.dma_start(
                       out=x_tiles[g], in_=xspill.ap()[g * 128:(g + 1) * 128, :])
                       for g in range(6)],
                   deps=[last_w2] + hps + [a for a in acc_last if a is not None],
                   n_dma=6, name=f"{name}_xrl")
    kb.free(xln2, sel_sb, be2_sb, bs2c_sb, b13_sb, comb_sb,
            w13_sb[0], w13_sb[1],
            *([w2_alloc] if w2_alloc is not None else []))
    y_tiles = [y_sb[:, g, :] for g in range(8)]
    ln4_fin = emit_ln(kb, y_tiles, None, T, resid_tiles=x_tiles,
                      deps=[a for a in acc_last if a is not None] + [st_xrl],
                      ones_col=ones_col, ones_row=ones_row, name=f"{name}_lnmo")
    kb.free(y_sb)
    return ln4_fin



def emit_mamba(kb, x_tiles, T, S, W_in, conv_w_cols, conv_b_cols, W_x, wdt_col,
               bdt_col, alog_cols, d_cols, W_out, selC, halo_mask, tok_mask,
               ones_col, ones_row, ones_col_bf=None, deps=(), name="mb"):
    P = kb.prog
    NCK = 8
    R = 8

    cw_sb = kb.sb("mb_cw", [128, 16, 4], F32)
    cbb_sb = kb.sb("mb_cb", [128, 16], F32)
    wx_sb = kb.sb("mb_wx", [128, 16, 33], BF16)
    wdt_sb = kb.sb("mb_wdt", [128, 2], F32)
    bdt_sb = kb.sb("mb_bdt", [128, 2], F32)
    acol_sb = kb.sb("mb_acol", [128, 32], F32)
    dcol_sb = kb.sb("mb_dcol", [128, 16], F32)
    selc_sb = kb.sb("mb_selc", [33, 1], F32)
    hmask_sb = kb.sb("mb_hmask", [8, 1], BF16)
    tmask_sb = kb.sb("mb_tmask", [8, 1], F32)
    st_w = P.dma("gpsimd",
                 lambda eng: [
                     eng.dma_start(out=cw_sb[:, :, :], in_=conv_w_cols),
                     eng.dma_start(out=cbb_sb[:, :], in_=conv_b_cols),
                     eng.dma_start(out=wx_sb[:, :, :],
                                   in_=W_x.rearrange("(g p) m -> p g m", p=128)),
                     eng.dma_start(out=wdt_sb[:, :], in_=wdt_col),
                     eng.dma_start(out=bdt_sb[:, :], in_=bdt_col),
                     eng.dma_start(out=acol_sb[:, :], in_=alog_cols),
                     eng.dma_start(out=dcol_sb[:, :], in_=d_cols),
                     eng.dma_start(out=selc_sb[:, :], in_=selC),
                     eng.dma_start(out=hmask_sb[:, :], in_=halo_mask),
                     eng.dma_start(out=tmask_sb[:, :], in_=tok_mask),
                 ], deps=list(deps), n_dma=10, name=f"{name}_ldw")
    st_a = P.comp("scalar",
                  lambda eng: eng.activation(acol_sb[:, :], acol_sb[:, :], AF.Exp),
                  deps=[st_w], name=f"{name}_aexp")
    st_a2 = P.comp("vector",
                   lambda eng: eng.tensor_single_scalar(acol_sb[:, :], acol_sb[:, :],
                                                        -1.0, ALU.mult),
                   deps=[st_a], name=f"{name}_aneg")

    xln = kb.sb("mb_xln", [128, 8, T], BF16)
    csum_sb = kb.sb("mb_csum", [1, T], F32)
    xln_tiles = [xln[:, g, :] for g in range(8)]
    ln_fin = emit_ln(kb, x_tiles, xln_tiles, T, deps=deps,
                     ones_col=ones_col, ones_row=ones_row, name=f"{name}_ln3")

    xs_a2a_in = kb.dram(kb.u("mb_a2ain"), [2048, T], BF16)
    xs_a2a_out = kb.dram(kb.u("mb_a2aout"), [2048, T], BF16)
    z_stage = kb.dram(kb.u("mb_zstage"), [2048, T], BF16)
    y3_stage = kb.dram(kb.u("mb_y3stage"), [2048, T], BF16)
    win_sb = kb.sb("mb_win", [128, 8, 2048], BF16)
    st_wa = P.dma("gpsimd",
                  lambda eng: [eng.dma_start(
                      out=win_sb[:, :, :],
                      in_=W_in[:, 0:2048].rearrange("(g p) m -> p g m", p=128))],
                  deps=ln_fin, n_dma=1, name=f"{name}_ldwa")

    halo_ag_in = kb.dram(kb.u("mb_haloin"), [2048, 4], BF16)
    halo_ag_out = kb.dram(kb.u("mb_haloout"), [R * 2048, 4], BF16)
    halo_bounce = kb.dram(kb.u("mb_halob"), [1, 2048 * 4], BF16)
    halo_sb = kb.sb("mb_halo", [128, 16, 4], F32)
    halo_all = kb.sb("mb_haloall", [8, 2048 * 4], BF16)
    hsel_sb = kb.sb("mb_hselsb", [1, 2048 * 4], BF16)
    hz_ps = kb.ps("mb_hzps", [128, 4], F32)
    hz_sb = kb.sb("mb_hzsb", [128, 16, 4], BF16)

    halo_stages = []
    st_prev = None
    for og in range(16):
        def emit_hz(eng, og=og):
            last = None
            for kg in range(8):
                eng.ldweights(win_sb[:, kg, og * 128:(og + 1) * 128])
                last = eng.matmul(hz_ps[:, :], win_sb[:, kg, og * 128:(og + 1) * 128],
                                  xln[:, kg, T - 4:T], start=(kg == 0), stop=(kg == 7))
            return last
        st_hz = P.comp("tensor", emit_hz,
                       deps=ln_fin + [st_wa] + ([st_prev] if st_prev else []),
                       name=f"{name}_hz{og}")
        st_he = P.comp("vector",
                       (lambda og=og: lambda eng:
                        eng.tensor_copy(hz_sb[:, og, :], hz_ps[:, :]))(),
                       deps=[st_hz], name=f"{name}_he{og}")
        st_prev = st_he
        halo_stages.append(st_he)
    st_hst = store(kb, halo_ag_in.ap().rearrange("(g p) c -> p g c", p=128),
                   hz_sb[:, :, :], deps=halo_stages, name=f"{name}_hstage")
    st_hag = P.coll(
        lambda eng: eng.collective_compute(
            "AllGather", ALU.bypass, replica_groups=[list(range(R))],
            ins=[halo_ag_in.ap().opt()], outs=[halo_ag_out.ap().opt()]),
        deps=[st_hst], name=f"{name}_hag")
    st_hld = load(kb, halo_all[:, :],
                  halo_ag_out.ap().rearrange("(r q) c -> r (q c)", r=8),
                  deps=[st_hag], name=f"{name}_hld")
    hsel_ps = kb.ps("mb_hselps", [1, 2048], F32)
    st_hsev = None
    for rr in range(4):
        def emit_hsel(eng, rr=rr):
            last = None
            eng.ldweights(hmask_sb[:, :])
            for n0 in range(0, 2048, 512):
                last = eng.matmul(hsel_ps[:, n0:n0 + 512], hmask_sb[:, :],
                                  halo_all[:, rr * 2048 + n0:rr * 2048 + n0 + 512],
                                  start=True, stop=True)
            return last
        st_hsel = P.comp("tensor", emit_hsel,
                         deps=[st_hld, st_w] + ([st_hsev] if st_hsev else []),
                         name=f"{name}_hsel{rr}")
        st_hsev = P.comp("vector",
                         (lambda rr=rr: lambda eng: eng.tensor_copy(
                             hsel_sb[:, rr * 2048:(rr + 1) * 2048], hsel_ps[:, :]))(),
                         deps=[st_hsel], name=f"{name}_hsev{rr}")
    st_hbo = store(kb, halo_bounce.ap(), hsel_sb[:, :], deps=[st_hsev],
                   name=f"{name}_hbo")
    st_hbi = P.dma("gpsimd",
                   lambda eng: [eng.dma_start(
                       out=halo_sb[:, :, :],
                       in_=halo_bounce.ap().rearrange("a (g p c) -> (a p) g c",
                                                      p=128, c=4))],
                   deps=[st_hbo], n_dma=1, name=f"{name}_hbi")
    kb.free(hz_sb, hsel_sb, halo_all)
    kb.free(hz_ps, hsel_ps)

    xs_og = [kb.sb("mb_xsog", [128, T], BF16) for _ in range(2)]
    acc_sb = kb.sb("mb_acc", [128, T], F32)
    corr_sb = kb.sb("mb_corr", [128, 4], F32)
    bcd_ps = kb.ps("mb_bcdps", [33, T], F32)
    xz_ps = kb.ps("mb_xzps", [128, T], F32)
    og_stages = []
    z_stages = []
    bcd_mms = []
    xs_stores = []
    prev_readers = []
    prev_st = {}
    st_wz = None
    st_bcd = None
    for og in range(32):
        is_x = og < 16
        buf = og % 2
        if og == 16:
            st_wz = P.dma("gpsimd",
                          lambda eng: [eng.dma_start(
                              out=win_sb[:, :, :],
                              in_=W_in[:, 2048:4096].rearrange(
                                  "(g p) m -> p g m", p=128))],
                          deps=[og_stages[-1], bcd_mms[-1], halo_stages[-1]],
                          n_dma=1, name=f"{name}_ldwz")

        def emit_xz(eng, og=og):
            last = None
            for kg in range(8):
                eng.ldweights(win_sb[:, kg, (og % 16) * 128:(og % 16 + 1) * 128])
                for n0 in range(0, T, 512):
                    last = eng.matmul(xz_ps[:, n0:min(n0 + 512, T)],
                                      win_sb[:, kg, (og % 16) * 128:(og % 16 + 1) * 128],
                                      xln[:, kg, n0:min(n0 + 512, T)],
                                      start=(kg == 0), stop=(kg == 7))
            return last
        st_mm = P.comp("tensor", emit_xz,
                       deps=ln_fin + [st_wa] + prev_readers +
                            ([st_wz] if og >= 16 else []) +
                            ([st_hsev, st_hbi] if og == 0 else []),
                       name=f"{name}_xz{og}")
        if is_x:
            st_cv = P.comp("vector",
                           (lambda og=og: lambda eng: (
                               eng.tensor_scalar(acc_sb[:, :], xz_ps[:, :],
                                                 cw_sb[:, og, 3:4], None, ALU.mult),
                               eng.scalar_tensor_tensor(
                                   acc_sb[:, 1:T], xz_ps[:, 0:T - 1],
                                   cw_sb[:, og, 2:3], acc_sb[:, 1:T],
                                   ALU.mult, ALU.add),
                               eng.scalar_tensor_tensor(
                                   acc_sb[:, 2:T], xz_ps[:, 0:T - 2],
                                   cw_sb[:, og, 1:2], acc_sb[:, 2:T],
                                   ALU.mult, ALU.add),
                               eng.scalar_tensor_tensor(
                                   acc_sb[:, 3:T], xz_ps[:, 0:T - 3],
                                   cw_sb[:, og, 0:1], acc_sb[:, 3:T],
                                   ALU.mult, ALU.add))[-1])(),
                           deps=[st_mm, st_w], name=f"{name}_cv{og}")
            st_c1 = P.comp("vector",
                           (lambda og=og: lambda eng:
                            eng.tensor_scalar(corr_sb[:, 0:3], halo_sb[:, og, 1:4],
                                              cw_sb[:, og, 0:1], None, ALU.mult))(),
                           deps=[st_hbi, st_w] + ([og_stages[-1]] if og > 0 else []),
                           name=f"{name}_c1o{og}")
            st_c2 = P.comp("vector",
                           (lambda og=og: lambda eng:
                            eng.scalar_tensor_tensor(
                                corr_sb[:, 0:2], halo_sb[:, og, 2:4],
                                cw_sb[:, og, 1:2], corr_sb[:, 0:2],
                                ALU.mult, ALU.add))(),
                           deps=[st_c1], name=f"{name}_c2o{og}")
            st_c3 = P.comp("vector",
                           (lambda og=og: lambda eng:
                            eng.scalar_tensor_tensor(
                                corr_sb[:, 0:1], halo_sb[:, og, 3:4],
                                cw_sb[:, og, 2:3], corr_sb[:, 0:1],
                                ALU.mult, ALU.add))(),
                           deps=[st_c2], name=f"{name}_c3o{og}")
            st_c4 = P.comp("vector",
                           (lambda: lambda eng:
                            eng.tensor_add(acc_sb[:, 0:3], acc_sb[:, 0:3],
                                           corr_sb[:, 0:3]))(),
                           deps=[st_c3, st_cv], name=f"{name}_c4o{og}")
            st_si = P.comp("scalar",
                           (lambda og=og, buf=buf: lambda eng:
                            eng.activation(xs_og[buf][:, :], acc_sb[:, :], AF.Silu,
                                           bias=cbb_sb[:, og:og + 1]))(),
                           deps=[st_c4] + prev_st.get(buf, []),
                           name=f"{name}_si{og}")
            st_so = store(kb, xs_a2a_in.ap()[og * 128:(og + 1) * 128, :],
                          xs_og[buf][:, :], deps=[st_si], name=f"{name}_so{og}")
            xs_stores.append(st_so)
            st_bcd = P.comp("tensor",
                            (lambda og=og, buf=buf: lambda eng: (
                                eng.ldweights(wx_sb[:, og, :]),
                                [eng.matmul(bcd_ps[:, n0:min(n0 + 512, T)],
                                            wx_sb[:, og, :],
                                            xs_og[buf][:, n0:min(n0 + 512, T)],
                                            start=(og == 0), stop=(og == 15))
                                 for n0 in range(0, T, 512)][-1])[-1])(),
                            deps=[st_si, st_w] +
                                 ([bcd_mms[-1]] if bcd_mms else []),
                            name=f"{name}_bcdp{og}")
            bcd_mms.append(st_bcd)
            prev_st[buf] = [st_so, st_bcd]
            prev_readers = [st_cv]
            og_stages.append(st_si)
        else:
            st_cp = P.comp("scalar",
                           (lambda og=og, buf=buf: lambda eng:
                            eng.copy(xs_og[buf][:, :], xz_ps[:, :]))(),
                           deps=[st_mm] + prev_st.get(buf, []),
                           name=f"{name}_zcp{og}")
            st_so = store(kb, z_stage.ap()[(og - 16) * 128:(og - 15) * 128, :],
                          xs_og[buf][:, :], deps=[st_cp], name=f"{name}_zso{og}")
            prev_st[buf] = [st_so]
            prev_readers = [st_cp]
            z_stages.append(st_so)
    kb.free(xz_ps)
    kb.free(corr_sb, acc_sb, xs_og[1], xs_og[0], halo_sb, win_sb)

    bcd_sb = kb.sb("mb_bcdsb", [33, T], F32)
    st_bcde = P.comp("vector",
                     lambda eng: eng.tensor_copy(bcd_sb[:, :], bcd_ps[:, :]),
                     deps=[bcd_mms[-1], z_stages[-1]], name=f"{name}_bcde")
    kb.free(bcd_ps)
    csum_ps = kb.ps("mb_csps", [1, T], F32)
    st_cs = P.comp("tensor",
                   (lambda: lambda eng: [
                       eng.matmul(csum_ps[:, n0:min(n0 + 512, T)], selc_sb[:, :],
                                  bcd_sb[:, n0:min(n0 + 512, T)],
                                  start=True, stop=True)
                       for n0 in range(0, T, 512)][-1])(),
                   deps=[st_bcde, st_w], name=f"{name}_cs")
    st_cse = P.comp("vector",
                    lambda eng: eng.tensor_copy(csum_sb[:, :], csum_ps[:, :]),
                    deps=[st_cs], name=f"{name}_cse")
    kb.free(csum_ps)

    bcd_ag_in = kb.dram(kb.u("mb_bcdin"), [17, T], F32)
    bcd_ag_out = kb.dram(kb.u("mb_bcdout"), [R * 17, T], F32)
    st_bc_st = P.dma("gpsimd",
                     lambda eng: [
                         eng.dma_start(out=bcd_ag_in.ap()[0:16, :],
                                       in_=bcd_sb[0:16, :]),
                         eng.dma_start(out=bcd_ag_in.ap()[16:17, :],
                                       in_=bcd_sb[32:33, :]),
                     ], deps=[st_bcde], n_dma=2, name=f"{name}_bcst")

    def emit_a2a(eng):
        from concourse.aot_env import allow_ici_all_to_all
        with allow_ici_all_to_all():
            return eng.collective_compute(
                "AllToAll", ALU.bypass, replica_groups=[list(range(R))],
                ins=[xs_a2a_in.ap().opt()], outs=[xs_a2a_out.ap().opt()])
    st_a2a = P.coll(emit_a2a, deps=xs_stores, name=f"{name}_a2a")
    st_bag = P.coll(
        lambda eng: eng.collective_compute(
            "AllGather", ALU.bypass, replica_groups=[list(range(R))],
            ins=[bcd_ag_in.ap().opt()], outs=[bcd_ag_out.ap().opt()]),
        deps=[st_bc_st], name=f"{name}_bag")
    kb.free(bcd_sb)

    x_ck = xln[:, 0:2, :]
    u_ck = xln[:, 2:4, :]
    dt_ck = xln[:, 4:8, :].rearrange("p a t -> p (a t)").bitcast(F32).rearrange(
        "p (g t) -> p g t", g=2)
    dtraw_sb = kb.sb("mb_dtraw", [1, T], F32)
    brow_sb = [kb.sb("mb_brow", [1, T], BF16) for _ in range(2)]
    bbc_sb = [kb.sb("mb_bbc", [128, T], BF16) for _ in range(2)]
    scr2 = kb.sb("mb_scr2", [128, 2, T], F32)
    t_scr = scr2
    abar_sb = [scr2[:, 0, :], scr2[:, 1, :]]
    bu_sb = [kb.sb("mb_bu", [128, T], BF16) for _ in range(2)]
    h2_sb = kb.sb("mb_h", [128, 2, T], F32)
    yrow_sb = kb.sb("mb_yrow", [1, T], F32)
    hstate = kb.sb("mb_hstate", [128, 32], F32)
    xck_b = kb.sb("mb_xckb", [128, 2, T], BF16)
    x_bufs = [x_ck, xck_b[:, :, :]]
    st_h0 = P.comp("gpsimd", lambda eng: eng.memset(hstate[:, :], 0.0),
                   deps=[st_cse, z_stages[-1]], name=f"{name}_h0")
    ones_col_b = ones_col_bf if ones_col_bf is not None else kb.consts["ones_col_bf"]
    ones_row_b = kb.consts["ones_row_bf"]

    ar_in = kb.dram(kb.u("mb_arin"), [8, T], F32)
    ar_out = kb.dram(kb.u("mb_arout"), [8, T], F32)
    dtb_ps = kb.ps("mb_dtbps", [128, T], F32)
    y_ps = kb.ps("mb_yps", [1, T], F32)

    prev_ck = {}
    sc_p = {}
    hb_p = {}
    hc_p = {}
    red_p = {}
    bu_p = {}
    ab_p = {}
    bbcm_p = {}
    bbc_prev = [None]
    bu_of_bbcbuf = {}
    st_yev_prev = None
    last_yst = None
    ldx_stages = {}
    u_stages = {}
    for ck in range(NCK):
        if ck == 0:
            ldx_stages[0] = load(
                kb, x_bufs[0],
                xs_a2a_out.ap()[0:256, :].rearrange("(g p) t -> p g t", p=128),
                deps=[st_a2a] + og_stages + z_stages, name=f"{name}_ldx0")
        st_ldx = ldx_stages[ck]
        st_lddt = load(kb, dtraw_sb[:, :],
                       bcd_ag_out.ap()[ck * 17 + 16:ck * 17 + 17, :],
                       deps=[st_bag] + prev_ck.get("dtraw_readers", []) +
                            ([st_cse] if ck == 0 else []),
                       name=f"{name}_lddt{ck}", engine="sync")
        st_dtb = P.comp("tensor",
                        (lambda: lambda eng: [
                            eng.matmul(dtb_ps[:, n0:min(n0 + 512, T)], ones_row,
                                       dtraw_sb[:, n0:min(n0 + 512, T)],
                                       start=True, stop=True)
                            for n0 in range(0, T, 512)][-1])(),
                        deps=[st_lddt, st_h0] + prev_ck.get("dtb_readers", []),
                        name=f"{name}_dtb{ck}")
        st_dts = []
        t1s = []
        for g in range(2):
            st_t1 = P.comp("vector",
                           (lambda g=g: lambda eng:
                            eng.tensor_scalar(t_scr[:, g, :], dtb_ps[:, :],
                                              wdt_sb[:, g:g + 1], bdt_sb[:, g:g + 1],
                                              ALU.mult, ALU.add))(),
                           deps=[st_dtb, st_w] +
                                ([sc_p[g]] if g in sc_p else []),
                           name=f"{name}_dt1c{ck}g{g}")
            t1s.append(st_t1)
            st_t2 = P.comp("scalar",
                           (lambda g=g: lambda eng:
                            eng.activation(dt_ck[:, g, :], t_scr[:, g, :], AF.Abs))(),
                           deps=[st_t1] +
                                ([ab_p[g]] if g in ab_p else []) +
                                prev_ck.get("uck_readers", []) +
                                ([st_ldx] if ck == 0 else []),
                           name=f"{name}_dt2c{ck}g{g}")
            st_t3 = P.comp("scalar",
                           (lambda g=g: lambda eng:
                            eng.activation(dt_ck[:, g, :], dt_ck[:, g, :], AF.Exp,
                                           scale=-1.0))(),
                           deps=[st_t2], name=f"{name}_dt3c{ck}g{g}")
            st_t4 = P.comp("vector",
                           (lambda g=g: lambda eng:
                            eng.tensor_single_scalar(dt_ck[:, g, :], dt_ck[:, g, :],
                                                     1.0, ALU.add))(),
                           deps=[st_t3], name=f"{name}_dt4c{ck}g{g}")
            st_t5 = P.comp("scalar",
                           (lambda g=g: lambda eng:
                            eng.activation(dt_ck[:, g, :], dt_ck[:, g, :], AF.Ln))(),
                           deps=[st_t4], name=f"{name}_dt5c{ck}g{g}")
            st_t6 = P.comp("vector",
                           (lambda g=g: lambda eng: (
                             eng.tensor_scalar_max(t_scr[:, g, :], t_scr[:, g, :], 0.0),
                             eng.tensor_add(dt_ck[:, g, :], dt_ck[:, g, :],
                                            t_scr[:, g, :]))[-1])(),
                           deps=[st_t5], name=f"{name}_dt6c{ck}g{g}")
            st_dts.append(st_t6)
        st_u = P.comp("vector",
                      (lambda ck=ck: lambda eng:
                       eng.tensor_mul(u_ck, dt_ck, x_bufs[ck % 2]))(),
                      deps=st_dts + [st_ldx] +
                           [x for x in (bu_p.get(0), bu_p.get(1)) if x],
                      name=f"{name}_u{ck}")
        u_stages[ck] = st_u
        if ck + 1 < NCK:
            ldx_stages[ck + 1] = load(
                kb, x_bufs[(ck + 1) % 2],
                xs_a2a_out.ap()[(ck + 1) * 256:(ck + 2) * 256, :].rearrange(
                    "(g p) t -> p g t", p=128),
                deps=[st_a2a] +
                     ([u_stages[ck - 1]] if ck - 1 in u_stages else
                      z_stages[-1:] + [st_cse]),
                name=f"{name}_ldx{ck + 1}")

        bbc_st = {}

        def _emit_bn(n, ck=ck, t1s=t1s):
            lbuf = n % 2
            st_lb = P.dma("gpsimd",
                          (lambda n=n, ck=ck, lbuf=lbuf: lambda eng: [
                              eng.dma_start(
                                  out=brow_sb[lbuf][:, :],
                                  in_=bcd_ag_out.ap()[ck * 17 + n:ck * 17 + n + 1, :])
                          ])(),
                          deps=[st_bag] +
                               ([bbcm_p[lbuf]] if lbuf in bbcm_p else []) +
                               ([st_cse] if (ck == 0 and n < 2) else []),
                          n_dma=1, name=f"{name}_lbc{ck}n{n}")
            st_bbcm = P.comp("tensor",
                             (lambda lbuf=lbuf: lambda eng: [
                                 eng.matmul(dtb_ps[:, n0:min(n0 + 512, T)],
                                            ones_row_b,
                                            brow_sb[lbuf][:, n0:min(n0 + 512, T)],
                                            start=True, stop=True)
                                 for n0 in range(0, T, 512)][-1])(),
                             deps=[st_lb] +
                                  ([t1s[1]] if n == 0 else []) +
                                  ([bbc_prev[0]] if bbc_prev[0] else []),
                             name=f"{name}_bbm{ck}n{n}")
            bbcm_p[lbuf] = st_bbcm
            nbuf = n % 2
            st_bbc = P.comp("scalar",
                            (lambda nbuf=nbuf: lambda eng:
                             eng.copy(bbc_sb[nbuf][:, :], dtb_ps[:, :]))(),
                            deps=[st_bbcm] +
                                 ([bu_of_bbcbuf[nbuf]] if nbuf in bu_of_bbcbuf
                                  else []),
                            name=f"{name}_bbc{ck}n{n}")
            bbc_prev[0] = st_bbc
            bbc_st[n] = st_bbc

        ab_st = {}

        def _emit_ab(s, ck=ck, st_dts=st_dts):
            g = s % 2
            buf = s % 2
            st = P.comp("scalar",
                        (lambda g=g, s=s, buf=buf: lambda eng:
                         eng.activation(abar_sb[buf], dt_ck[:, g, :],
                                        AF.Exp, scale=acol_sb[:, s:s + 1]))(),
                        deps=[st_dts[g], st_a2] +
                             ([sc_p[buf]] if buf in sc_p else []),
                        name=f"{name}_ab{ck}s{s}")
            ab_st[s] = st
            ab_p[g] = st

        _emit_bn(0)
        _emit_ab(0)
        _emit_ab(1)
        _emit_bn(1)
        for n in range(16):
            st_bbc = bbc_st[n]
            nbuf = n % 2
            for g in range(2):
                s = n * 2 + g
                buf = s % 2
                st_ab = ab_st[s]
                st_bu = P.comp("gpsimd",
                               (lambda g=g, buf=buf, nbuf=nbuf: lambda eng:
                                eng.tensor_mul(bu_sb[buf][:, :], u_ck[:, g, :],
                                               bbc_sb[nbuf][:, :]))(),
                               deps=[st_u, st_bbc] +
                                    ([sc_p[buf]] if buf in sc_p else []),
                               name=f"{name}_bu{ck}s{s}")
                st_sc = P.comp("vector",
                               (lambda s=s, buf=buf: lambda eng:
                                eng.tensor_tensor_scan(
                                    h2_sb[:, buf, :], abar_sb[buf],
                                    bu_sb[buf][:, :], hstate[:, s:s + 1],
                                    ALU.mult, ALU.add))(),
                               deps=[st_ab, st_bu, st_h0] +
                                    ([red_p[buf]] if buf in red_p else []) +
                                    ([hc_p[buf]] if buf in hc_p else []),
                               name=f"{name}_sc{ck}s{s}")
                st_hc = P.comp("vector",
                               (lambda s=s, buf=buf: lambda eng:
                                eng.tensor_copy(hstate[:, s:s + 1],
                                                h2_sb[:, buf, T - 1:T]))(),
                               deps=[st_sc], name=f"{name}_hc{ck}s{s}")
                st_red = P.comp("tensor",
                                (lambda s=s, buf=buf, first=(s == 0): lambda eng: [
                                    eng.matmul(y_ps[:, n0:min(n0 + 512, T)],
                                               ones_col,
                                               h2_sb[:, buf, n0:min(n0 + 512, T)],
                                               start=first, stop=(s == 31))
                                    for n0 in range(0, T, 512)][-1])(),
                                deps=[st_sc] +
                                     ([st_yev_prev] if (s == 0 and st_yev_prev) else []),
                                name=f"{name}_red{ck}s{s}")
                sc_p[buf] = st_sc
                hc_p[buf] = st_hc
                red_p[buf] = st_red
                bu_p[buf] = st_bu
                bu_of_bbcbuf[nbuf] = st_bu
            if n + 1 < 16:
                _emit_ab(2 * n + 2)
                _emit_ab(2 * n + 3)
            if n + 2 < 16:
                _emit_bn(n + 2)
        st_yev = P.comp("scalar",
                        lambda eng: eng.copy(yrow_sb[:, :], y_ps[:, :]),
                        deps=[red_p[1]], name=f"{name}_yev{ck}")
        st_yst = store(kb, ar_in.ap()[ck:ck + 1, :], yrow_sb[:, :],
                       deps=[st_yev], name=f"{name}_yst{ck}")
        st_yev_prev = st_yev
        last_yst = st_yst
        prev_ck = {
            "xck_readers": [st_u],
            "dtraw_readers": [st_dtb],
            "dtb_readers": [bbc_prev[0]],
            "uck_readers": [st_u],
        }
    st_ar = P.coll(
        lambda eng: eng.collective_compute(
            "AllReduce", ALU.add, replica_groups=[list(range(R))],
            ins=[ar_in.ap().opt()], outs=[ar_out.ap().opt()]),
        deps=[last_yst], name=f"{name}_ar")
    kb.free(dtb_ps, y_ps)
    kb.free(xck_b, hstate, yrow_sb, h2_sb, bu_sb[1], bu_sb[0], scr2,
            bbc_sb[1], bbc_sb[0], brow_sb[1], brow_sb[0], dtraw_sb)

    yall_sb = kb.sb("mb_yall", [8, T], F32)
    yscal_sb = kb.sb("mb_yscal", [1, T], F32)
    st_ya = load(kb, yall_sb[:, :], ar_out.ap(), deps=[st_ar], name=f"{name}_ya")
    ysel_ps = kb.ps("mb_yselps", [1, T], F32)
    st_ysel = P.comp("tensor",
                     (lambda: lambda eng: [
                         eng.matmul(ysel_ps[:, n0:min(n0 + 512, T)], tmask_sb[:, :],
                                    yall_sb[:, n0:min(n0 + 512, T)],
                                    start=True, stop=True)
                         for n0 in range(0, T, 512)][-1])(),
                     deps=[st_ya, st_w], name=f"{name}_ysel")
    st_ysc = P.comp("vector",
                    lambda eng: (
                        eng.tensor_mul(yscal_sb[:, :], ysel_ps[:, :], csum_sb[:, :]),
                        eng.tensor_single_scalar(yscal_sb[:, :], yscal_sb[:, :],
                                                 1.0 / 2048.0, ALU.mult))[-1],
                    deps=[st_ysel, st_cse], name=f"{name}_ysc")
    kb.free(ysel_ps)
    yb_ps = kb.ps("mb_ybps", [128, T], F32)
    st_yb = P.comp("tensor",
                   (lambda: lambda eng: [
                       eng.matmul(yb_ps[:, n0:min(n0 + 512, T)], ones_row,
                                  yscal_sb[:, n0:min(n0 + 512, T)],
                                  start=True, stop=True)
                       for n0 in range(0, T, 512)][-1])(),
                   deps=[st_ysc], name=f"{name}_yb")

    xs_in = [kb.sb("mb_xsin", [128, T], BF16) for _ in range(2)]
    zz_in = [kb.sb("mb_zzin", [128, T], BF16) for _ in range(2)]
    ts_bf = kb.sb("mb_tsbf", [128, T], BF16)
    y3_stages = []
    prev_ld = {}
    for og in range(16):
        buf = og % 2
        st_lx = load(kb, xs_in[buf][:, :],
                     xs_a2a_in.ap()[og * 128:(og + 1) * 128, :],
                     deps=[st_yb] + prev_ld.get(buf, []), name=f"{name}_c1x{og}")
        st_lz = load(kb, zz_in[buf][:, :],
                     z_stage.ap()[og * 128:(og + 1) * 128, :],
                     deps=[st_yb] + prev_ld.get(buf, []), name=f"{name}_c1z{og}",
                     engine="sync")
        st_m1 = P.comp("vector",
                       (lambda og=og, buf=buf: lambda eng:
                        eng.scalar_tensor_tensor(ts_bf[:, :], xs_in[buf][:, :],
                                                 dcol_sb[:, og:og + 1], yb_ps[:, :],
                                                 ALU.mult, ALU.add))(),
                       deps=[st_lx, st_w] + y3_stages[-1:], name=f"{name}_m1o{og}")
        st_sz = P.comp("scalar",
                       (lambda buf=buf: lambda eng:
                        eng.activation(zz_in[buf][:, :], zz_in[buf][:, :], AF.Silu))(),
                       deps=[st_lz], name=f"{name}_szo{og}")
        st_m2 = P.comp("vector",
                       (lambda buf=buf: lambda eng:
                        eng.tensor_mul(zz_in[buf][:, :], ts_bf[:, :],
                                       zz_in[buf][:, :]))(),
                       deps=[st_m1, st_sz], name=f"{name}_m2o{og}")
        st_y3s = store(kb, y3_stage.ap()[og * 128:(og + 1) * 128, :],
                       zz_in[buf][:, :], deps=[st_m2], name=f"{name}_y3s{og}")
        prev_ld[buf] = [st_m2, st_y3s]
        y3_stages.append(st_y3s)
    kb.free(yb_ps)

    wout_sb2 = kb.sb("mb_wout2", [128, 16, 256], BF16)
    mo_sb = xln
    mo_ps2 = [kb.ps("mb_mops", [128, T], F32) for _ in range(2)]
    mo_evs = []
    prev_pass = []
    for ps_i in range(4):
        mgs = (2 * ps_i, 2 * ps_i + 1)
        st_lw = P.dma("gpsimd",
                      (lambda ps_i=ps_i: lambda eng: [eng.dma_start(
                          out=wout_sb2[:, :, :],
                          in_=W_out[:, ps_i * 256:(ps_i + 1) * 256].rearrange(
                              "(g p) m -> p g m", p=128))])(),
                      deps=y3_stages + prev_pass, n_dma=1,
                      name=f"{name}_ldwo{ps_i}")
        mm_stages = []
        prev_y3ld = {}
        for kg in range(16):
            buf = kg % 2
            st_ly = load(kb, xs_in[buf][:, :],
                         y3_stage.ap()[kg * 128:(kg + 1) * 128, :],
                         deps=y3_stages + prev_y3ld.get(buf, []) + prev_pass,
                         name=f"{name}_c2y{ps_i}k{kg}")
            def emit_mo(eng, kg=kg, buf=buf):
                last = None
                for j in range(2):
                    eng.ldweights(wout_sb2[:, kg, j * 128:(j + 1) * 128])
                    for n0 in range(0, T, 512):
                        last = eng.matmul(mo_ps2[j][:, n0:min(n0 + 512, T)],
                                          wout_sb2[:, kg, j * 128:(j + 1) * 128],
                                          xs_in[buf][:, n0:min(n0 + 512, T)],
                                          start=(kg == 0), stop=(kg == 15))
                return last
            st_mo = P.comp("tensor", emit_mo,
                           deps=[st_ly, st_lw] + prev_pass, name=f"{name}_mo{ps_i}k{kg}")
            mm_stages.append(st_mo)
            prev_y3ld[buf] = [st_mo]
        evs = []
        for j in range(2):
            st_ev = P.comp("scalar",
                           (lambda j=j, mg=mgs[j]: lambda eng:
                            eng.copy(mo_sb[:, mg, :], mo_ps2[j][:, :]))(),
                           deps=[mm_stages[-1]], name=f"{name}_moev{ps_i}_{j}")
            evs.append(st_ev)
            mo_evs.append(st_ev)
        prev_pass = evs
    kb.free(mo_ps2[1], mo_ps2[0])

    kb.free(wout_sb2, ts_bf, zz_in[1], zz_in[0], xs_in[1], xs_in[0],
            yscal_sb, yall_sb)
    mo_tiles = [mo_sb[:, g, :] for g in range(8)]
    ln_fin2 = emit_ln(kb, mo_tiles, None, T, resid_tiles=x_tiles,
                      deps=mo_evs, ones_col=ones_col, ones_row=ones_row,
                      ones_col_bf=ones_col_bf, name=f"{name}_lnmo")
    kb.free(csum_sb, cw_sb, cbb_sb, wx_sb, wdt_sb, bdt_sb, acol_sb, dcol_sb,
            selc_sb, hmask_sb, tmask_sb, xln)
    return ln_fin2



def emit_rwkv(kb, x_tiles, T, WrT, WkT, WvT, mur_cols, muk_cols, halo_mask,
              ones_col, ones_row, ones_col_bf, deps=(), name="rw"):
    P = kb.prog
    R = 8

    mur_sb = kb.sb("rw_mur", [128, 8], F32)
    muk_sb = kb.sb("rw_muk", [128, 8], F32)
    hmask_sb = kb.sb("rw_hmask", [8, 1], F32)
    st_w0 = P.dma("gpsimd",
                  lambda eng: [
                      eng.dma_start(out=mur_sb[:, :], in_=mur_cols),
                      eng.dma_start(out=muk_sb[:, :], in_=muk_cols),
                      eng.dma_start(out=hmask_sb[:, :], in_=halo_mask),
                  ], deps=list(deps), n_dma=3, name=f"{name}_ldc")

    h_ag_in = kb.dram(kb.u("rw_hin"), [1024, 1], F32)
    h_ag_out = kb.dram(kb.u("rw_hout"), [R * 1024, 1], F32)
    h_bounce = kb.dram(kb.u("rw_hb"), [1, 1024], F32)
    hall_sb = kb.sb("rw_hall", [8, 1024], F32)
    hprev_sb = kb.sb("rw_hprev", [128, 8], F32)
    hrow_sb = kb.sb("rw_hrow", [128, 8, 1], F32)
    st_hst = P.dma("gpsimd",
                   lambda eng: [
                       eng.dma_start(
                           out=h_ag_in.ap()[g * 128:(g + 1) * 128, :],
                           in_=x_tiles[g][:, T - 1:T])
                       for g in range(8)],
                   deps=list(deps), n_dma=8, name=f"{name}_hst")
    st_hag = P.coll(
        lambda eng: eng.collective_compute(
            "AllGather", ALU.bypass, replica_groups=[list(range(R))],
            ins=[h_ag_in.ap().opt()], outs=[h_ag_out.ap().opt()]),
        deps=[st_hst], name=f"{name}_hag")
    st_hld = load(kb, hall_sb[:, :],
                  h_ag_out.ap().rearrange("(r q) c -> r (q c)", r=8),
                  deps=[st_hag], name=f"{name}_hld")
    hsel_ps = kb.ps("rw_hselps", [1, 1024], F32)
    st_hsel = P.comp("tensor",
                     (lambda: lambda eng: [
                         eng.matmul(hsel_ps[:, n0:n0 + 512], hmask_sb[:, :],
                                    hall_sb[:, n0:n0 + 512], start=True, stop=True)
                         for n0 in range(0, 1024, 512)][-1])(),
                     deps=[st_hld, st_w0], name=f"{name}_hsel")
    hsel_sb = kb.sb("rw_hselsb", [1, 1024], F32)
    st_hsev = P.comp("vector",
                     lambda eng: eng.tensor_copy(hsel_sb[:, :], hsel_ps[:, :]),
                     deps=[st_hsel], name=f"{name}_hsev")
    st_hbo = store(kb, h_bounce.ap(), hsel_sb[:, :], deps=[st_hsev],
                   name=f"{name}_hbo")
    def emit_hbi(eng):
        with kb.nc.allow_non_contiguous_dma(reason="tiny 4KB halo gather"):
            return [eng.dma_start(
                out=hrow_sb[:, :, :],
                in_=h_bounce.ap().rearrange("a (g p) -> (a p) g ()", p=128))]
    st_hbi = P.dma("gpsimd", emit_hbi, deps=[st_hbo], n_dma=1, name=f"{name}_hbi")
    kb.free(hsel_sb)
    kb.free(hsel_ps)

    hs_ps = kb.ps("rw_hsps", [1, 2], F32)
    hsq_sb = kb.sb("rw_hsq", [128, 8], F32)
    st_hsq = P.comp("scalar",
                    lambda eng: eng.activation(hsq_sb[:, :],
                                               hrow_sb[:, :, :].rearrange("p g c -> p (g c)"),
                                               AF.Square),
                    deps=[st_hbi], name=f"{name}_hsq")

    def emit_hstats(eng):
        last = None
        for g in range(8):
            eng.matmul(hs_ps[:, 0:1], ones_col, hrow_sb[:, g, :],
                       start=(g == 0), stop=(g == 7))
        for g in range(8):
            last = eng.matmul(hs_ps[:, 1:2], ones_col, hsq_sb[:, g:g + 1],
                              start=(g == 0), stop=(g == 7))
        return last
    st_hss = P.comp("tensor", emit_hstats, deps=[st_hbi, st_hsq],
                    name=f"{name}_hss")
    hst_sb = kb.sb("rw_hst", [1, 4], F32)
    st_hm = P.comp("vector",
                   lambda eng: eng.tensor_single_scalar(
                       hst_sb[:, 0:2], hs_ps[:, :], 1.0 / 1024.0, ALU.mult),
                   deps=[st_hss], name=f"{name}_hm")
    st_hv = P.comp("vector",
                   lambda eng: (
                       eng.tensor_mul(hst_sb[:, 2:3], hst_sb[:, 0:1], hst_sb[:, 0:1]),
                       eng.tensor_sub(hst_sb[:, 1:2], hst_sb[:, 1:2], hst_sb[:, 2:3]),
                       eng.tensor_single_scalar(hst_sb[:, 1:2], hst_sb[:, 1:2],
                                                1e-5, ALU.add))[-1],
                   deps=[st_hm], name=f"{name}_hv")
    st_hs2 = P.comp("scalar",
                    lambda eng: eng.activation(hst_sb[:, 1:2], hst_sb[:, 1:2], AF.Sqrt),
                    deps=[st_hv], name=f"{name}_hs2")
    st_hr = P.comp("vector",
                   lambda eng: eng.reciprocal(hst_sb[:, 1:2], hst_sb[:, 1:2]),
                   deps=[st_hs2], name=f"{name}_hr")
    hb_ps = kb.ps("rw_hbps", [128, 2], F32)
    st_hbb = P.comp("tensor",
                    lambda eng: eng.matmul(hb_ps[:, :], ones_row, hst_sb[:, 0:2],
                                           start=True, stop=True),
                    deps=[st_hr], name=f"{name}_hbb")
    hpb_sb = kb.sb("rw_hpb", [128, 2], F32)
    st_hbe = P.comp("vector",
                    lambda eng: eng.tensor_copy(hpb_sb[:, :], hb_ps[:, :]),
                    deps=[st_hbb], name=f"{name}_hbe")
    kb.free(hb_ps, hs_ps)
    st_hln = P.comp("vector",
                    lambda eng: eng.tensor_scalar(
                        hprev_sb[:, :], hrow_sb[:, :, :].rearrange("p g c -> p (g c)"),
                        hpb_sb[:, 0:1], hpb_sb[:, 1:2], ALU.subtract, ALU.mult),
                    deps=[st_hbe], name=f"{name}_hln")

    xr_sb = kb.sb("rw_xr", [128, 8, T], BF16)
    xk_sb = kb.sb("rw_xk", [128, 8, T], BF16)
    dscr = kb.sb("rw_dscr", [128, T], BF16)
    xln4 = kb.sb("rw_xln4", [128, 8, T], BF16)
    xln_tiles = [xln4[:, g, :] for g in range(8)]
    ln_fin = emit_ln(kb, x_tiles, xln_tiles, T, deps=deps,
                     ones_col=ones_col, ones_row=ones_row, name=f"{name}_ln4")
    shift_stages = []
    prev_d = []
    for g in range(8):
        st_d = P.comp("vector",
                      (lambda g=g: lambda eng:
                       eng.tensor_sub(dscr[:, 1:T], xln4[:, g, 1:T],
                                      xln4[:, g, 0:T - 1]))(),
                      deps=ln_fin + prev_d, name=f"{name}_d{g}")
        st_d0 = P.comp("vector",
                       (lambda g=g: lambda eng:
                        eng.tensor_sub(dscr[:, 0:1], xln4[:, g, 0:1],
                                       hprev_sb[:, g:g + 1]))(),
                       deps=[st_d, st_hln], name=f"{name}_d0{g}")
        st_xr = P.comp("vector",
                       (lambda g=g: lambda eng: (
                           eng.scalar_tensor_tensor(
                               xr_sb[:, g, 1:T], dscr[:, 1:T], mur_sb[:, g:g + 1],
                               xln4[:, g, 0:T - 1], ALU.mult, ALU.add),
                           eng.scalar_tensor_tensor(
                               xk_sb[:, g, 1:T], dscr[:, 1:T], muk_sb[:, g:g + 1],
                               xln4[:, g, 0:T - 1], ALU.mult, ALU.add))[-1])(),
                       deps=[st_d0, st_w0], name=f"{name}_xrk{g}")
        st_xr0 = P.comp("vector",
                        (lambda g=g: lambda eng: (
                            eng.scalar_tensor_tensor(
                                xr_sb[:, g, 0:1], dscr[:, 0:1], mur_sb[:, g:g + 1],
                                hprev_sb[:, g:g + 1], ALU.mult, ALU.add),
                            eng.scalar_tensor_tensor(
                                xk_sb[:, g, 0:1], dscr[:, 0:1], muk_sb[:, g:g + 1],
                                hprev_sb[:, g:g + 1], ALU.mult, ALU.add))[-1])(),
                        deps=[st_xr], name=f"{name}_xrk0{g}")
        prev_d = [st_xr0]
        shift_stages.append(st_xr0)
    kb.free(xln4)

    wk_sb = kb.sb("rw_wk", [128, 8, 1024], BF16)
    st_wk = P.dma("gpsimd",
                  lambda eng: [eng.dma_start(
                      out=wk_sb[:, :, :],
                      in_=WkT.rearrange("(g p) m -> p g m", p=128))],
                  deps=shift_stages, n_dma=1, name=f"{name}_ldwk")
    k2_sb = kb.sb("rw_k2", [128, 8, T], BF16)
    kp_ps = kb.ps("rw_kps", [128, T], F32)
    prev_ev = []
    k2_stages = []
    for mg in range(8):
        def emit_k(eng, mg=mg):
            last = None
            for kg in range(8):
                eng.ldweights(wk_sb[:, kg, mg * 128:(mg + 1) * 128])
                for n0 in range(0, T, 512):
                    last = eng.matmul(kp_ps[:, n0:min(n0 + 512, T)],
                                      wk_sb[:, kg, mg * 128:(mg + 1) * 128],
                                      xk_sb[:, kg, n0:min(n0 + 512, T)],
                                      start=(kg == 0), stop=(kg == 7))
            return last
        st_km = P.comp("tensor", emit_k,
                       deps=shift_stages + [st_wk] + prev_ev,
                       name=f"{name}_km{mg}")
        st_kr = P.comp("vector",
                       (lambda mg=mg: lambda eng:
                        eng.tensor_scalar_max(k2_sb[:, mg, :], kp_ps[:, :], 0.0))(),
                       deps=[st_km], name=f"{name}_kr{mg}")
        st_k2 = P.comp("vector",
                       (lambda mg=mg: lambda eng:
                        eng.tensor_mul(k2_sb[:, mg, :], k2_sb[:, mg, :],
                                       k2_sb[:, mg, :]))(),
                       deps=[st_kr], name=f"{name}_k2{mg}")
        prev_ev = [st_kr]
        k2_stages.append(st_k2)
    kb.free(kp_ps)

    rw_sb = xk_sb
    wv_sb = kb.sb("rw_wv", [128, 8, 1024], BF16)
    st_wv = P.dma("gpsimd",
                  lambda eng: [
                      eng.dma_start(out=wk_sb[:, :, :],
                                    in_=WrT.rearrange("(g p) m -> p g m", p=128)),
                      eng.dma_start(out=wv_sb[:, :, :],
                                    in_=WvT.rearrange("(g p) m -> p g m", p=128)),
                  ], deps=k2_stages + [st_km], n_dma=2, name=f"{name}_ldwv")
    wr_sb = wk_sb
    rs_scr = kb.sb("rw_rs", [128, T], BF16)
    r_ps = kb.ps("rw_rps", [128, T], F32)
    v_ps = kb.ps("rw_vps", [128, T], F32)
    prev_rw = []
    rw_stages = []
    for mg in range(8):
        def emit_r(eng, mg=mg):
            last = None
            for kg in range(8):
                eng.ldweights(wr_sb[:, kg, mg * 128:(mg + 1) * 128])
                for n0 in range(0, T, 512):
                    last = eng.matmul(r_ps[:, n0:min(n0 + 512, T)],
                                      wr_sb[:, kg, mg * 128:(mg + 1) * 128],
                                      xr_sb[:, kg, n0:min(n0 + 512, T)],
                                      start=(kg == 0), stop=(kg == 7))
            return last

        def emit_v(eng, mg=mg):
            last = None
            for kg in range(8):
                eng.ldweights(wv_sb[:, kg, mg * 128:(mg + 1) * 128])
                for n0 in range(0, T, 512):
                    last = eng.matmul(v_ps[:, n0:min(n0 + 512, T)],
                                      wv_sb[:, kg, mg * 128:(mg + 1) * 128],
                                      k2_sb[:, kg, n0:min(n0 + 512, T)],
                                      start=(kg == 0), stop=(kg == 7))
            return last
        st_rm = P.comp("tensor", emit_r, deps=shift_stages + [st_wv] + prev_rw,
                       name=f"{name}_rm{mg}")
        st_vm = P.comp("tensor", emit_v, deps=k2_stages + [st_wv] + prev_rw,
                       name=f"{name}_vm{mg}")
        st_sg = P.comp("scalar",
                       (lambda: lambda eng:
                        eng.activation(rs_scr[:, :], r_ps[:, :], AF.Sigmoid))(),
                       deps=[st_rm], name=f"{name}_sg{mg}")
        st_rw = P.comp("vector",
                       (lambda mg=mg: lambda eng:
                        eng.tensor_mul(rw_sb[:, mg, :], rs_scr[:, :], v_ps[:, :]))(),
                       deps=[st_sg, st_vm], name=f"{name}_rw{mg}")
        prev_rw = [st_rw]
        rw_stages.append(st_rw)
    kb.free(v_ps, r_ps)

    kb.free(rs_scr, wv_sb, k2_sb, wk_sb, dscr)
    rw_tiles = [rw_sb[:, g, :] for g in range(8)]
    ln_fin2 = emit_ln(kb, rw_tiles, None, T, resid_tiles=x_tiles,
                      deps=rw_stages, ones_col=ones_col, ones_row=ones_row,
                      ones_col_bf=ones_col_bf, name=f"{name}_lnrw")
    kb.free(xr_sb, xk_sb,
            hst_sb, hsq_sb, hpb_sb, hrow_sb, hprev_sb,
            hall_sb, hmask_sb, muk_sb, mur_sb)
    return ln_fin2

"""Driver: FUSED single-launch megakernel."""
import numpy as np
import ml_dtypes
from concourse.bass_utils import run_bass_kernel_spmd

T = 2048
S = 16384


def _build_fused():
    import os
    DBGF = os.environ.get("DBGF")
    kb = KB()
    x = kb.inp("x", [1024, T], F32)
    WqT = kb.inp("WqT", [1024, 128], F32)
    WkT = kb.inp("WkT", [1024, 128], F32)
    WvT = kb.inp("WvT", [1024, 128], F32)
    maskf = kb.inp("mask", [128, 128], F32)
    identf_i = kb.inp("identf", [128, 128], F32)
    WoT = kb.inp("WoT", [1024, 1024], F32)
    gate_wT = kb.inp("gate_wT", [1024, 16], F32)
    gate_b = kb.inp("gate_b", [1, 16], F32)
    W13 = kb.inp("W13", [18, 1024, 1024], F32)
    W2 = kb.inp("W2", [18, 512, 1024], F32)
    b13 = kb.inp("b13", [128, 8, 18], F32)
    be2 = kb.inp("be2", [16, 1024], F32)
    bs2 = kb.inp("bs2", [1, 1024], F32)
    sel18 = kb.inp("sel18", [18, 18 * 128], F32)
    W_in = kb.inp("W_in", [1024, 4096], F32)
    cw = kb.inp("cw", [128, 16, 4], F32)
    cb = kb.inp("cb", [128, 16], F32)
    W_x = kb.inp("W_x", [2048, 33], F32)
    wdt = kb.inp("wdt", [128, 2], F32)
    bdt = kb.inp("bdt", [128, 2], F32)
    alog = kb.inp("alog", [128, 32], F32)
    dcol = kb.inp("dcol", [128, 16], F32)
    W_out = kb.inp("W_out", [2048, 1024], F32)
    selC = kb.inp("selC", [33, 1], F32)
    hmask = kb.inp("hmask", [8, 1], F32)
    tmask = kb.inp("tmask", [8, 1], F32)
    WrT = kb.inp("WrT", [1024, 1024], F32)
    WkT2 = kb.inp("WkT2", [1024, 1024], F32)
    WvT2 = kb.inp("WvT2", [1024, 1024], F32)
    mur = kb.inp("mur", [128, 8], F32)
    muk = kb.inp("muk", [128, 8], F32)
    y = kb.out("y", [1024, T], F32)

    x_sb = kb.sb("x", [128, 8, T], F32)
    ones_col, ones_row, ones_col_bf, st_c = make_consts(kb)
    identf = kb.sb("identf", [128, 128], F32)
    mask_sb = kb.sb("mask", [128, 128], F32)
    ident_bf = kb.sb("identbf", [128, 128], BF16)
    ones_bf = kb.sb("onesbf", [128, 128], BF16)
    st_ob = kb.prog.comp("gpsimd", lambda eng: eng.memset(ones_bf[:, :], 1.0),
                         name="onesbf")
    st_ld = load(kb, x_sb[:, :, :], x.rearrange("(g p) t -> p g t", p=128),
                 name="ld_x")
    st_m = load(kb, mask_sb[:, :], maskf, name="ld_mask")
    st_i = load(kb, identf[:, :], identf_i, name="ld_i")
    st_ib = kb.prog.dma("gpsimd",
                        lambda eng: [eng.dma_start(out=ident_bf[:, :], in_=identf_i)],
                        n_dma=1, name="ld_identbf")
    x_tiles = [x_sb[:, g, :] for g in range(8)]

    out_attn = kb.sb("oattn", [128, S], BF16)
    at_fin = emit_attn(kb, x_tiles, T, S, WqT, WkT, WvT,
                       mask_sb[:, :], ident_bf[:, :],
                       ones_col, ones_row, ones_bf[:, :], out_attn[:, :],
                       deps=[st_ld, st_c, st_m, st_i, st_ib, st_ob], name="at")
    at_a2a_in = kb.dram(kb.u("at_a2ain"), [8 * 128, T], BF16)
    at_a2a_out = kb.dram(kb.u("at_a2aout"), [8 * 128, T], BF16)
    st_ast = kb.prog.dma("gpsimd",
                         lambda eng: [
                             eng.dma_start(out=at_a2a_in.ap()[j * 128:(j + 1) * 128, :],
                                           in_=out_attn[:, j * T:(j + 1) * T])
                             for j in range(8)],
                         deps=at_fin, n_dma=8, name="at_a2ast")

    def emit_a2a(eng):
        from concourse.aot_env import allow_ici_all_to_all
        with allow_ici_all_to_all():
            return eng.collective_compute(
                "AllToAll", ALU.bypass, replica_groups=[list(range(8))],
                ins=[at_a2a_in.ap().opt()], outs=[at_a2a_out.ap().opt()])
    st_a2a = kb.prog.coll(emit_a2a, deps=[st_ast], name="at_a2a")
    kb.free(out_attn, ones_bf, ident_bf, mask_sb)

    if DBGF:
        d_attn = kb.out("d_attn", [1024, T], BF16)
        st_da = kb.prog.dma("gpsimd",
                            lambda eng: [eng.dma_start(out=d_attn, in_=at_a2a_out.ap())],
                            deps=[st_a2a], n_dma=1, name="d_attn")
    lr_fin = emit_l0rest(kb, x_tiles, T, at_a2a_out.ap(), WoT, gate_wT, gate_b,
                         W13, W2, b13, be2, bs2, sel18, None, identf[:, :],
                         ones_col, ones_row, ones_col_bf,
                         deps=[st_a2a], name="lr")

    if DBGF:
        d_lr = kb.out("d_lr", [1024, T], F32)
        for g in range(8):
            store(kb, d_lr[g * 128:(g + 1) * 128, :], x_tiles[g], deps=lr_fin,
                  name=f"d_lr{g}")
    mb_fin = emit_mamba(kb, x_tiles, T, S, W_in, cw, cb, W_x, wdt, bdt, alog,
                        dcol, W_out, selC, hmask, tmask, ones_col, ones_row,
                        ones_col_bf=ones_col_bf, deps=lr_fin, name="mb")

    if DBGF:
        d_mb = kb.out("d_mb", [1024, T], F32)
        for g in range(8):
            store(kb, d_mb[g * 128:(g + 1) * 128, :], x_tiles[g], deps=mb_fin,
                  name=f"d_mb{g}")
    rw_fin = emit_rwkv(kb, x_tiles, T, WrT, WkT2, WvT2, mur, muk, hmask,
                       ones_col, ones_row, ones_col_bf, deps=mb_fin, name="rw")

    store(kb, y.rearrange("(g p) t -> p g t", p=128), x_sb[:, :, :],
          deps=rw_fin, name="st_y")
    kb.finish()
    return kb


def _prep_inputs(inp):
    x_in = np.ascontiguousarray(np.asarray(inp["L0_attn_x"])[0])
    Wq = np.asarray(inp["L0_attn_Wq"]); Wk = np.asarray(inp["L0_attn_Wk"])
    Wv = np.asarray(inp["L0_attn_Wv"]); Wo = np.asarray(inp["L0_attn_Wo"])
    mask = np.triu(np.ones((128, 128), np.float32))
    ident = np.eye(128, dtype=np.float32)
    gate_w = np.asarray(inp["L0_ffn_gate_w"])
    gate_b = np.asarray(inp["L0_ffn_gate_b"]).reshape(1, 16)
    We1 = np.asarray(inp["L0_ffn_We1"]); We2 = np.asarray(inp["L0_ffn_We2"])
    We3 = np.asarray(inp["L0_ffn_We3"])
    be1 = np.asarray(inp["L0_ffn_be1"]); be2 = np.asarray(inp["L0_ffn_be2"])
    be3 = np.asarray(inp["L0_ffn_be3"])
    ws1 = np.asarray(inp["L0_ffn_ws1"]); ws2 = np.asarray(inp["L0_ffn_ws2"])
    ws3 = np.asarray(inp["L0_ffn_ws3"])
    bs1 = np.asarray(inp["L0_ffn_bs1"])
    bs2 = np.asarray(inp["L0_ffn_bs2"]).reshape(1, 1024)
    bs3 = np.asarray(inp["L0_ffn_bs3"])
    W13 = np.zeros((18, 1024, 1024), np.float32)
    W2a = np.zeros((18, 512, 1024), np.float32)
    b13 = np.zeros((128, 8, 18), np.float32)
    for e in range(16):
        W13[e] = np.concatenate([We1[e], We3[e]], axis=1)
        W2a[e] = We2[e]
        b13[:, 0:4, e] = be1[e].reshape(4, 128).T
        b13[:, 4:8, e] = be3[e].reshape(4, 128).T
    for i, e in enumerate((16, 17)):
        hsl = slice(i * 512, (i + 1) * 512)
        W13[e] = np.concatenate([ws1[:, hsl], ws3[:, hsl]], axis=1)
        W2a[e] = ws2[hsl, :]
        b13[:, 0:4, e] = bs1[hsl].reshape(4, 128).T
        b13[:, 4:8, e] = bs3[hsl].reshape(4, 128).T
    sel18 = np.zeros((18, 18 * 128), np.float32)
    for e in range(18):
        sel18[e, e * 128:(e + 1) * 128] = 1.0
    W_in = np.asarray(inp["L1_attn_W_in"])
    conv_w = np.asarray(inp["L1_attn_conv_w"])
    conv_b = np.asarray(inp["L1_attn_conv_b"])
    W_x = np.asarray(inp["L1_attn_W_x"])
    W_dt = np.asarray(inp["L1_attn_W_dt"])
    b_dt = np.asarray(inp["L1_attn_b_dt"])
    A_log = np.asarray(inp["L1_attn_A_log"])
    D_par = np.asarray(inp["L1_attn_D_param"])
    W_out = np.asarray(inp["L1_attn_W_out"])
    selC = np.zeros((33, 1), np.float32)
    selC[16:32] = 1.0
    cw_cols = np.zeros((128, 16, 4), np.float32)
    cb_cols = np.zeros((128, 16), np.float32)
    d_cols = np.zeros((128, 16), np.float32)
    for og in range(16):
        cw_cols[:, og, :] = conv_w[og * 128:(og + 1) * 128, :]
        cb_cols[:, og] = conv_b[og * 128:(og + 1) * 128]
        d_cols[:, og] = D_par[og * 128:(og + 1) * 128]
    Wr = np.asarray(inp["L1_ffn_W_r"]); Wkk = np.asarray(inp["L1_ffn_W_k"])
    Wvv = np.asarray(inp["L1_ffn_W_v"])
    mur_c = np.asarray(inp["L1_ffn_mu_r"]).reshape(8, 128).T.copy()
    muk_c = np.asarray(inp["L1_ffn_mu_k"]).reshape(8, 128).T.copy()

    in_maps = []
    for c in range(8):
        sl = slice(c * 128, (c + 1) * 128)
        wdt_c = np.zeros((128, 2), np.float32)
        bdt_c = np.zeros((128, 2), np.float32)
        alog_c = np.zeros((128, 32), np.float32)
        for g in range(2):
            dd = slice(c * 256 + g * 128, c * 256 + (g + 1) * 128)
            wdt_c[:, g] = W_dt[0, dd]
            bdt_c[:, g] = b_dt[dd]
            for n in range(16):
                alog_c[:, n * 2 + g] = A_log[dd, n]
        hmask = np.zeros((8, 1), np.float32)
        if c > 0:
            hmask[c - 1, 0] = 1.0
        tmask = np.zeros((8, 1), np.float32)
        tmask[c, 0] = 1.0
        in_maps.append({
            "x": np.ascontiguousarray(x_in[c * T:(c + 1) * T].T),
            "WqT": np.ascontiguousarray(Wq.T[:, sl]),
            "WkT": np.ascontiguousarray(Wk.T[:, sl]),
            "WvT": np.ascontiguousarray(Wv.T[:, sl]),
            "mask": mask, "identf": ident,
            "WoT": np.ascontiguousarray(Wo.T),
            "gate_wT": np.ascontiguousarray(gate_w.T),
            "gate_b": gate_b, "W13": W13, "W2": W2a, "b13": b13,
            "be2": be2, "bs2": bs2, "sel18": sel18,
            "W_in": W_in, "cw": cw_cols, "cb": cb_cols, "W_x": W_x,
            "wdt": wdt_c, "bdt": bdt_c, "alog": alog_c, "dcol": d_cols,
            "W_out": W_out, "selC": selC, "hmask": hmask, "tmask": tmask,
            "WrT": np.ascontiguousarray(Wr.T),
            "WkT2": np.ascontiguousarray(Wkk.T),
            "WvT2": np.ascontiguousarray(Wvv.T),
            "mur": mur_c, "muk": muk_c,
        })
    return in_maps


def kernel(**inp):
    kb = _build_fused()
    in_maps = _prep_inputs(inp)
    res = run_bass_kernel_spmd(kb.nc, in_maps, core_ids=list(range(8)))
    out = np.concatenate([res.results[c]["y"].T for c in range(8)], axis=0)
    return out[None].astype(np.float32)


# version 11
# speedup vs baseline: 1.2824x; 1.0169x over previous
import numpy as np
from contextlib import ExitStack

import concourse.bass as bass
import concourse.mybir as mybir

F32 = mybir.dt.float32
BF16 = mybir.dt.bfloat16
U32 = mybir.dt.uint32
AF = mybir.ActivationFunctionType
ALU = mybir.AluOpType
AX = mybir.AxisListType

ENGINES = ("sync", "tensor", "vector", "scalar", "gpsimd")

DMA_ENGINES = ("sync", "gpsimd", "scalar", "vector")


class Stage:
    __slots__ = ("idx", "engine", "emit", "deps", "n_inc", "kind", "sem", "val",
                 "name")

    def __init__(self, idx, engine, emit, deps, n_inc, kind, name):
        self.idx = idx
        self.engine = engine
        self.emit = emit
        self.deps = deps
        self.n_inc = n_inc
        self.kind = kind
        self.sem = None
        self.val = None
        self.name = name


class Prog:
    def __init__(self):
        self.stages = []

    def comp(self, engine, emit, deps=(), name=""):
        s = Stage(len(self.stages), engine, emit, tuple(deps), 1, "compute", name)
        self.stages.append(s)
        return s

    def dma(self, engine, emit, deps=(), n_dma=1, name=""):
        s = Stage(len(self.stages), engine, emit, tuple(deps), 16 * n_dma, "dma", name)
        self.stages.append(s)
        return s

    def coll(self, emit, deps=(), name=""):
        s = Stage(len(self.stages), "gpsimd", emit, tuple(deps), 1, "collective", name)
        self.stages.append(s)
        return s

    def build(self, nc, ctx, n_dma_sems=8):
        csem = {e: ctx.enter_context(nc.semaphore(f"c_{e}")) for e in ENGINES}
        dsem = {e: [ctx.enter_context(nc.semaphore(f"d_{e}{i}"))
                    for i in range(n_dma_sems)] for e in DMA_ENGINES}
        run_c = {e: 0 for e in ENGINES}
        run_d = {e: [0] * n_dma_sems for e in DMA_ENGINES}
        rot = {e: 0 for e in DMA_ENGINES}
        coll_sems = []
        for s in self.stages:
            for d in s.deps:
                assert d.idx < s.idx, f"dep ordering: {d.name} -> {s.name}"
            if s.kind == "compute":
                s.sem = csem[s.engine]
                run_c[s.engine] += s.n_inc
                s.val = run_c[s.engine]
            elif s.kind == "dma":
                i = rot[s.engine]
                rot[s.engine] = (i + 1) % n_dma_sems
                s.sem = dsem[s.engine][i]
                run_d[s.engine][i] += s.n_inc
                s.val = run_d[s.engine][i]
            else:
                h = ctx.enter_context(nc.semaphore(f"cc_{len(coll_sems)}"))
                coll_sems.append(h)
                s.sem = h
                s.val = 1

        block = ctx.enter_context(nc.Block())

        def engine_body(ename):
            def body(eng):
                for s in self.stages:
                    if s.engine != ename:
                        continue
                    seen = set()
                    for d in s.deps:
                        key = (id(d.sem), d.val)
                        if key in seen:
                            continue
                        seen.add(key)
                        eng.wait_ge(d.sem, d.val)
                    if s.kind == "compute":
                        last = s.emit(eng)
                        assert last is not None, f"stage {s.name} returned None"
                        last.then_inc(s.sem, 1)
                    elif s.kind == "dma":
                        instrs = s.emit(eng)
                        assert len(instrs) * 16 == s.n_inc, \
                            f"stage {s.name}: {len(instrs)} DMAs != declared {s.n_inc//16}"
                        for i in instrs:
                            i.then_inc(s.sem, 16)
                    else:
                        instr = s.emit(eng)
                        instr.then_inc(s.sem)
            return body

        used = {s.engine for s in self.stages}
        if "sync" in used:
            block.sync(engine_body("sync"))
        if "tensor" in used:
            block.tensor(engine_body("tensor"))
        if "vector" in used:
            block.vector(engine_body("vector"))
        if "scalar" in used:
            block.scalar(engine_body("scalar"))
        if "gpsimd" in used:
            block.gpsimd(engine_body("gpsimd"))


class KB:

    def __init__(self, name="k"):
        self.nc = bass.Bass("TRN2", target_bir_lowering=False)
        self.ctx = ExitStack()
        self.prog = Prog()
        self.inputs = {}
        self.outputs = {}
        self._uniq = 0
        self._cms = {}
        self._alloc_seq = {}

    def u(self, base):
        self._uniq += 1
        return f"{base}_{self._uniq}"

    def inp(self, name, shape, dtype=F32):
        ap = self.nc.dram_tensor(name, list(shape), dtype, kind="ExternalInput").ap()
        self.inputs[name] = ap
        return ap

    def out(self, name, shape, dtype=F32):
        ap = self.nc.dram_tensor(name, list(shape), dtype, kind="ExternalOutput").ap()
        self.outputs[name] = ap
        return ap

    def dram(self, name, shape, dtype=F32):
        return self.nc.dram_tensor(name, list(shape), dtype)

    def sb(self, name, shape, dtype=F32):
        cm = self.nc.sbuf_tensor(self.u(name), list(shape), dtype)
        h = cm.__enter__()
        self._cms[id(h)] = cm
        self._alloc_seq[id(h)] = len(self._alloc_seq)
        return h

    def ps(self, name, shape, dtype=F32):
        cm = self.nc.psum_tensor(self.u(name), list(shape), dtype)
        h = cm.__enter__()
        self._cms[id(h)] = cm
        self._alloc_seq[id(h)] = len(self._alloc_seq)
        return h

    def free(self, *handles):
        hs = sorted(handles, key=lambda h: -self._alloc_seq[id(h)])
        for h in hs:
            cm = self._cms.pop(id(h))
            cm.__exit__(None, None, None)

    def psum_scope(self):
        return ExitStack()

    def finish(self):
        self.prog.build(self.nc, self.ctx)
        for hid in sorted(self._cms, key=lambda i: -self._alloc_seq[i]):
            self._cms[hid].__exit__(None, None, None)
        self._cms.clear()
        self.ctx.close()
        return self.nc



def load(kb, dst_sb, src_ap, deps=(), name="load", engine="sync"):
    def emit(eng):
        return [eng.dma_start(out=dst_sb, in_=src_ap)]
    return kb.prog.dma(engine, emit, deps=deps, n_dma=1, name=name)


def store(kb, dst_ap, src_sb, deps=(), name="store", engine="gpsimd"):
    def emit(eng):
        return [eng.dma_start(out=dst_ap, in_=src_sb)]
    return kb.prog.dma(engine, emit, deps=deps, n_dma=1, name=name)


def matmul_fm(kb, out_ps, lhsT_sb_tiles, rhs_sb_tiles, deps=(), name="mm",
              n_chunk=512, start=True, stop=True, extra=()):
    def emit(eng):
        last = None
        K = len(lhsT_sb_tiles)
        N = rhs_sb_tiles[0].shape[-1]
        for ki in range(K):
            lhsT = lhsT_sb_tiles[ki]
            rhs = rhs_sb_tiles[ki]
            bf = lhsT.dtype != F32
            if bf:
                eng.ldweights(lhsT)
            for n0 in range(0, N, n_chunk):
                n1 = min(N, n0 + n_chunk)
                last = eng.matmul(
                    out_ps[:, n0:n1], lhsT, rhs[:, n0:n1],
                    start=(start and ki == 0), stop=(stop and ki == K - 1))
        return last
    return kb.prog.comp("tensor", emit, deps=deps, name=name)


def emit_ln(kb, x_tiles, out_tiles, T, resid_tiles=None, eps=1e-5, deps=(),
            ones_col=None, ones_row=None, ones_col_bf=None, name="ln"):
    nG = len(x_tiles)
    D = nG * 128
    sx_ps = kb.ps("ln_sx", [1, T], F32)
    sq_ps = kb.ps("ln_sq", [1, T], F32)
    sqg_t = kb.sb("ln_sqg", [128, T], F32)
    sqg = [sqg_t, sqg_t]
    mean_row = kb.sb("ln_mean", [1, T], F32)
    work_row = kb.sb("ln_work", [1, T], F32)
    msq_row = work_row
    var_row = work_row
    srow = work_row
    rstd_row = work_row
    sq_stages = [None, None]
    mmx_stages = []
    mmq_stages = []
    for g in range(nG):
        buf = sqg[g % 2]
        sq_deps = list(deps)
        if g >= 1:
            sq_deps.append(mmq_stages[g - 1])
        st_sq = kb.prog.comp(
            "scalar",
            (lambda b=buf, xg=x_tiles[g]: lambda eng:
                eng.activation(b[:, :], xg, AF.Square))(),
            deps=sq_deps, name=f"{name}_sq{g}")
        sq_stages[g % 2] = st_sq
        oc = ones_col_bf if (x_tiles[g].dtype == BF16 and ones_col_bf is not None) else ones_col
        st_mx = matmul_fm(kb, sx_ps[:, :], [oc], [x_tiles[g]],
                          deps=list(deps), name=f"{name}_mmx{g}",
                          start=(g == 0), stop=(g == nG - 1))
        st_mq = matmul_fm(kb, sq_ps[:, :], [ones_col], [buf[:, :]],
                          deps=[st_sq], name=f"{name}_mmq{g}",
                          start=(g == 0), stop=(g == nG - 1))
        mmx_stages.append(st_mx)
        mmq_stages.append(st_mq)

    st_mean = kb.prog.comp(
        "vector",
        lambda eng: eng.tensor_single_scalar(mean_row[:, :], sx_ps[:, :], 1.0 / D, ALU.mult),
        deps=[mmx_stages[-1]], name=f"{name}_mean")
    st_msq = kb.prog.comp(
        "vector",
        lambda eng: eng.tensor_mul(msq_row[:, :], mean_row[:, :], mean_row[:, :]),
        deps=[st_mean], name=f"{name}_msq")
    st_var = kb.prog.comp(
        "vector",
        lambda eng: eng.scalar_tensor_tensor(
            var_row[:, :], sq_ps[:, :], 1.0 / D, msq_row[:, :], ALU.mult, ALU.subtract),
        deps=[st_msq, mmq_stages[-1]], name=f"{name}_var")
    st_ve = kb.prog.comp(
        "vector",
        lambda eng: eng.tensor_single_scalar(var_row[:, :], var_row[:, :], eps, ALU.add),
        deps=[st_var], name=f"{name}_vareps")
    st_s = kb.prog.comp(
        "scalar",
        lambda eng: eng.activation(srow[:, :], var_row[:, :], AF.Sqrt),
        deps=[st_ve], name=f"{name}_sqrt")
    st_r = kb.prog.comp(
        "vector",
        lambda eng: eng.reciprocal(rstd_row[:, :], srow[:, :]),
        deps=[st_s], name=f"{name}_recip")
    kb.free(sx_ps, sq_ps)
    mean_b = kb.ps("ln_meanb", [128, T], F32)
    rstd_b = kb.ps("ln_rstdb", [128, T], F32)
    st_bm = matmul_fm(kb, mean_b[:, :], [ones_row], [mean_row[:, :]],
                      deps=[st_mean, st_var], name=f"{name}_bm")
    st_br = matmul_fm(kb, rstd_b[:, :], [ones_row], [rstd_row[:, :]],
                      deps=[st_r], name=f"{name}_br")

    fin = []
    tmp = kb.sb("ln_t", [128, T], F32 if resid_tiles is not None else out_tiles[0].dtype)
    if resid_tiles is None:
        for g in range(nG):
            war = [fin[-1]] if fin else []
            st_t = kb.prog.comp(
                "vector",
                (lambda tg=tmp, xg=x_tiles[g]: lambda eng:
                    eng.tensor_sub(tg[:, :], xg, mean_b[:, :]))(),
                deps=[st_bm] + list(deps) + war, name=f"{name}_sub{g}")
            st_o = kb.prog.comp(
                "vector",
                (lambda og=out_tiles[g], tg=tmp: lambda eng:
                    eng.tensor_mul(og, tg[:, :], rstd_b[:, :]))(),
                deps=[st_t, st_br], name=f"{name}_mul{g}")
            fin.append(st_o)
    else:
        tmp_bufs = [tmp, sqg_t]
        adds = {}
        for g in range(nG):
            tb = tmp_bufs[g % 2]
            war = ([adds[g - 2]] if g - 2 in adds else [])
            st_t = kb.prog.comp(
                "vector",
                (lambda tg=tb, xg=x_tiles[g]: lambda eng:
                    eng.tensor_sub(tg[:, :], xg, mean_b[:, :]))(),
                deps=[st_bm] + list(deps) + war +
                     ([mmq_stages[-1]] if g == 1 else []),
                name=f"{name}_sub{g}")
            st_u = kb.prog.comp(
                "vector",
                (lambda tg=tb: lambda eng:
                    eng.tensor_mul(tg[:, :], tg[:, :], rstd_b[:, :]))(),
                deps=[st_t, st_br], name=f"{name}_mul{g}")
            st_a = kb.prog.comp(
                "gpsimd",
                (lambda rg=resid_tiles[g], tg=tb: lambda eng:
                    eng.tensor_add(rg, rg, tg[:, :]))(),
                deps=[st_u], name=f"{name}_add{g}")
            adds[g] = st_a
            fin.append(st_a)
    kb.free(tmp, mean_b, rstd_b, sqg_t, mean_row, work_row)
    return fin


def make_consts(kb):
    ones_col = kb.sb("ones_col", [128, 1], F32)
    ones_row = kb.sb("ones_row", [1, 128], F32)
    ones_col_bf = kb.sb("ones_col_bf", [128, 1], BF16)
    ones_row_bf = kb.sb("ones_row_bf", [1, 128], BF16)

    def emit(eng):
        eng.memset(ones_col[:, :], 1.0)
        eng.memset(ones_col_bf[:, :], 1.0)
        eng.memset(ones_row_bf[:, :], 1.0)
        return eng.memset(ones_row[:, :], 1.0)
    st = kb.prog.comp("gpsimd", emit, name="consts")
    kb.consts = {"ones_row_bf": ones_row_bf[:, :]}
    return ones_col[:, :], ones_row[:, :], ones_col_bf[:, :], st



def emit_attn(kb, x_tiles, T, S, WqT, WkT, WvT, mask_tri, ident_bf,
              ones_col, ones_row, ones_bf, out_attn, deps=(), name="attn",
              xln_out_tiles=None):
    P = kb.prog
    EPS = 1e-8
    import os
    phase = os.environ.get("ATTN_PHASE", "full")

    xln = kb.sb("at_xln", [128, 8, T], BF16)
    xln_tiles = [xln[:, g, :] for g in range(8)]
    ln_fin = emit_ln(kb, x_tiles, xln_tiles, T, deps=deps,
                     ones_col=ones_col, ones_row=ones_row, name=f"{name}_ln1")

    ag_in = kb.dram(kb.u("at_agin"), [1024, T], BF16)
    ag_out = kb.dram(kb.u("at_agout"), [8 * 1024, T], BF16)
    st_stage = store(kb, ag_in.ap().rearrange("(g p) t -> p g t", p=128),
                     xln[:, :, :], deps=ln_fin, name=f"{name}_stage")
    st_ag = P.coll(
        lambda eng: eng.collective_compute(
            "AllGather", ALU.bypass, replica_groups=[list(range(8))],
            ins=[ag_in.ap().opt()], outs=[ag_out.ap().opt()]),
        deps=[st_stage], name=f"{name}_ag")

    wq = kb.sb("at_wq", [128, 8, 128], BF16)
    wk = kb.sb("at_wk", [128, 8, 128], BF16)
    wv = kb.sb("at_wv", [128, 8, 128], BF16)
    st_w = P.dma(
        "gpsimd",
        lambda eng: [
            eng.dma_start(out=wq[:, :, :], in_=WqT.rearrange("(g p) m -> p g m", p=128)),
            eng.dma_start(out=wk[:, :, :], in_=WkT.rearrange("(g p) m -> p g m", p=128)),
            eng.dma_start(out=wv[:, :, :], in_=WvT.rearrange("(g p) m -> p g m", p=128)),
        ], n_dma=3, name=f"{name}_ldw")

    KV = kb.sb("at_kv", [128, 192], F32)
    KVbf = kb.sb("at_kvbf", [128, 192], BF16)
    st_kv0 = P.comp("gpsimd", lambda eng: eng.memset(KV[:, :], 0.0),
                    name=f"{name}_kv0")

    xblk = kb.sb("at_xblk", [128, 8, T], BF16)
    qf = kb.sb("at_q", [128, T], BF16)
    kf = kb.sb("at_k", [128, T], BF16)
    vf = kb.sb("at_v", [128, T], BF16)
    e_scr = kb.sb("at_escr", [128, T], F32)
    p_scr = kb.sb("at_pscr", [128, T], F32)
    vtm = [kb.sb("at_vtm", [128, 192], BF16) for _ in range(2)]
    ktm = [kb.sb("at_ktm", [128, 128], BF16) for _ in range(2)]
    ssb = [kb.sb("at_ssb", [128, 128], BF16) for _ in range(2)]
    rec_sb = kb.sb("at_rec", [128, 128], F32)

    if os.environ.get("ATTN_MEMSET", "1") == "1":
        st_ones_v = P.comp(
            "gpsimd",
            lambda eng: (eng.memset(vtm[0][:, 128:192], 1.0),
                         eng.memset(vtm[1][:, 128:192], 1.0),
                         eng.memset(out_attn[:, :], 0.0))[-1],
            name=f"{name}_vones")
    else:
        st_ones_v = P.comp(
            "gpsimd",
            lambda eng: (eng.memset(vtm[0][:, 128:192], 1.0),
                         eng.memset(vtm[1][:, 128:192], 1.0))[-1],
            name=f"{name}_vones")

    tp_k_t = kb.ps("at_tpk", [128, 128], BF16)
    tp_v_t = kb.ps("at_tpv", [128, 128], BF16)
    tp_k = tp_k_t[:, :]
    tp_v = tp_v_t[:, :]
    PW = T if os.environ.get("PROJ_FULL") else min(T, 512)
    pps = kb.ps("at_pp", [128, PW], F32)
    if phase != "proj":
        num_ps_t = kb.ps("at_num", [128, 128], F32)
        den_ps_t = kb.ps("at_den", [128, 128], F32)
        stp_t = kb.ps("at_stp", [128, 128], F32)
        stp_t2 = kb.ps("at_stp2", [128, 128], F32)
        kvd_t = kb.ps("at_kvd", [128, 192], F32)
        num_ps = num_ps_t[:, :]
        den_ps = den_ps_t[:, :]
        stp2 = [stp_t[:, :], stp_t2[:, :]]
        kvd = kvd_t[:, :]

    n_blocks = int(os.environ.get("ATTN_BLOCKS", S // T))
    n_chunks_per_blk = T // 128
    last_stages = []
    prev_blk_tail = []
    prev = None

    for r in range(n_blocks):
        st_x = load(kb, xblk[:, :, :],
                    ag_out.ap()[r * 1024:(r + 1) * 1024, :].rearrange(
                        "(g p) t -> p g t", p=128),
                    deps=[st_ag] + prev_blk_tail, name=f"{name}_x{r}")
        proj = {}
        prev_pps_readers = []
        prev_elu_done = []
        W = PW
        for wname, wsb, osb, do_elu in (("q", wq, qf, True), ("k", wk, kf, True),
                                        ("v", wv, vf, False)):
            st_os = []
            for hf in range(T // W):
                hsl2 = slice(hf * W, (hf + 1) * W)
                st_mm = matmul_fm(kb, pps[:, :],
                                  [wsb[:, g, :] for g in range(8)],
                                  [xblk[:, g, hsl2] for g in range(8)],
                                  deps=[st_x, st_w] + prev_pps_readers,
                                  name=f"{name}_{wname}mm{r}_{hf}")
                if do_elu:
                    st_min = P.comp("vector",
                                    (lambda h2=hsl2: lambda eng: eng.tensor_scalar_min(
                                        e_scr[:, h2], pps[:, :], 0.0))(),
                                    deps=[st_mm] + prev_elu_done,
                                    name=f"{name}_{wname}min{r}_{hf}")
                    st_exp = P.comp("scalar",
                                    (lambda h2=hsl2: lambda eng: eng.activation(
                                        e_scr[:, h2], e_scr[:, h2], AF.Exp))(),
                                    deps=[st_min], name=f"{name}_{wname}exp{r}_{hf}")
                    st_max = P.comp("vector",
                                    (lambda h2=hsl2: lambda eng: eng.tensor_scalar_max(
                                        p_scr[:, h2], pps[:, :], 0.0))(),
                                    deps=[st_mm] + prev_elu_done,
                                    name=f"{name}_{wname}max{r}_{hf}")
                    st_o = P.comp("vector",
                                  (lambda o=osb, h2=hsl2: lambda eng: eng.tensor_add(
                                      o[:, h2], e_scr[:, h2], p_scr[:, h2]))(),
                                  deps=[st_exp, st_max],
                                  name=f"{name}_{wname}elu{r}_{hf}")
                    prev_pps_readers = [st_min, st_max]
                    prev_elu_done = [st_o]
                else:
                    st_o = P.comp("scalar",
                                  (lambda o=osb, h2=hsl2: lambda eng: eng.copy(
                                      o[:, h2], pps[:, :]))(),
                                  deps=[st_mm], name=f"{name}_vcp{r}_{hf}")
                    prev_pps_readers = [st_o]
                st_os.append(st_o)
            proj[wname] = st_os[-1]

        if phase == "proj":
            st_cp = P.comp("vector",
                           (lambda r=r: lambda eng: eng.tensor_copy(
                               out_attn[:, r * T:(r + 1) * T], qf[:, :]))(),
                           deps=[proj["q"], proj["k"], proj["v"]],
                           name=f"{name}_projcp{r}")
            prev_blk_tail = [st_cp]
            last_stages = [st_cp]
            continue
        blk_readers = []
        for c in range(n_chunks_per_blk):
            gi = r * n_chunks_per_blk + c
            sl = slice(c * 128, (c + 1) * 128)
            gsl = slice(gi * 128, (gi + 1) * 128)
            buf = gi % 2

            dep_tp = [proj["k"], proj["v"]]
            if prev is not None:
                dep_tp = dep_tp + [prev["ek"], prev["ev"], prev["kv_add"]]
                if prev["prev2_tm_readers"]:
                    dep_tp = dep_tp + prev["prev2_tm_readers"]
            st_tk = P.comp("tensor",
                           (lambda s=sl: lambda eng:
                            eng.transpose(tp_k[:, :], kf[:, s], ident_bf))(),
                           deps=dep_tp, name=f"{name}_tk{gi}")
            st_tv = P.comp("tensor",
                           (lambda s=sl: lambda eng:
                            eng.transpose(tp_v[:, :], vf[:, s], ident_bf))(),
                           deps=dep_tp, name=f"{name}_tv{gi}")
            if phase == "tmo":
                prev = {"kv_add": st_tv, "kvbf_readers": [st_tk, st_tv],
                        "tm_readers": [st_tk, st_tv], "prev2_tm_readers":
                        (prev["tm_readers"] if prev is not None else []),
                        "ssb_readers": [st_tk, st_tv], "msks": [st_tk, st_tv],
                        "out_stages": [st_tk, st_tv], "ek": st_tk, "ev": st_tv}
                blk_readers.extend([st_tk, st_tv])
                continue
            st_ek = P.comp("scalar",
                           (lambda d=ktm[buf]: lambda eng:
                            eng.copy(d[:, :], tp_k[:, :]))(),
                           deps=[st_tk], name=f"{name}_ek{gi}")
            st_ev = P.comp("scalar",
                           (lambda d=vtm[buf]: lambda eng:
                            eng.copy(d[:, 0:128], tp_v[:, :]))(),
                           deps=[st_tv, st_ones_v], name=f"{name}_ev{gi}")

            if phase == "tm":
                prev = {"kv_add": st_ev, "kvbf_readers": [st_ek, st_ev],
                        "tm_readers": [st_ek, st_ev], "prev2_tm_readers":
                        (prev["tm_readers"] if prev is not None else []),
                        "ssb_readers": [st_ek, st_ek], "msks": [st_ek, st_ev],
                        "out_stages": [st_ek, st_ev], "ek": st_ek, "ev": st_ev}
                blk_readers.extend([st_ek, st_ev])
                continue
            def emit_st(eng, s=sl):
                eng.ldweights(kf[0:64, s])
                eng.matmul(stp2[0], kf[0:64, s], qf[0:64, s],
                           start=True, stop=True)
                eng.ldweights(kf[64:128, s], tile_position=(64, 0))
                return eng.matmul(stp2[1], kf[64:128, s], qf[64:128, s],
                                  start=True, stop=True, tile_position=(64, 0))
            st_stmm = P.comp("tensor", emit_st,
                             deps=[proj["k"], proj["q"]] +
                                  ([prev["msk"]] if prev is not None else []),
                             name=f"{name}_st{gi}")
            st_msk = P.comp("vector",
                            lambda eng: (
                                eng.tensor_mul(ssb[0][:, :], stp2[0], mask_tri),
                                eng.tensor_mul(ssb[1][:, :], stp2[1], mask_tri))[-1],
                            deps=[st_stmm] +
                                 ([prev["nd"]] if prev is not None else []),
                            name=f"{name}_msk{gi}")

            if gi > 0:
                st_kvbf = P.comp("scalar",
                                 lambda eng: eng.copy(KVbf[:, :], KV[:, :]),
                                 deps=[prev["kv_add"], prev["nd"]],
                                 name=f"{name}_kvbf{gi}")
            else:
                st_kvbf = None

            first = (gi == 0)

            def emit_nd(eng, s=sl, first=first, buf=buf):
                last = None
                for h in range(2):
                    hsl = slice(h * 64, (h + 1) * 64)
                    tp0 = (0, 64) if h == 1 else None
                    tpq = (64, 64) if h == 1 else None
                    eng.ldweights(vtm[buf][:, h * 64:(h + 1) * 64],
                                  tile_position=tp0)
                    eng.matmul(num_ps[hsl, :], vtm[buf][:, h * 64:(h + 1) * 64],
                               ssb[h][:, :], start=True, stop=first,
                               tile_position=tp0)
                    if not first:
                        eng.ldweights(KVbf[hsl, h * 64:(h + 1) * 64],
                                      tile_position=tpq)
                        eng.matmul(num_ps[hsl, :], KVbf[hsl, h * 64:(h + 1) * 64],
                                   qf[hsl, s], start=False, stop=True,
                                   tile_position=tpq)
                    eng.ldweights(ones_bf[:, 0:64], tile_position=tp0)
                    last = eng.matmul(den_ps[hsl, :], ones_bf[:, 0:64],
                                      ssb[h][:, :], start=True, stop=first,
                                      tile_position=tp0)
                    if not first:
                        eng.ldweights(KVbf[hsl, 128:192], tile_position=tpq)
                        last = eng.matmul(den_ps[hsl, :], KVbf[hsl, 128:192],
                                          qf[hsl, s], start=False, stop=True,
                                          tile_position=tpq)
                return last
            deps_nd = [st_msk, st_ev, proj["q"]]
            if st_kvbf is not None:
                deps_nd.append(st_kvbf)
            if prev is not None:
                deps_nd.extend(prev["out_stages"])
            st_nd = P.comp("tensor", emit_nd, deps=deps_nd,
                           name=f"{name}_nd{gi}")
            st_rec = P.comp("vector",
                            lambda eng: (
                                eng.tensor_single_scalar(
                                    rec_sb[:, :], den_ps[:, :], EPS, ALU.add),
                                eng.reciprocal(rec_sb[:, :], rec_sb[:, :]))[-1],
                            deps=[st_nd], name=f"{name}_rec{gi}")
            st_out = P.comp("vector",
                            (lambda s=gsl: lambda eng:
                             eng.tensor_mul(out_attn[:, s], num_ps[:, :],
                                            rec_sb[:, :]))(),
                            deps=[st_nd, st_rec], name=f"{name}_out{gi}")
            na_out = [st_out]

            kv_deps = [st_ek, st_ev, st_kv0]
            if prev is not None:
                kv_deps.append(prev["kv_add"])

            st_kvd = P.comp("tensor",
                            (lambda buf=buf: lambda eng: (
                                eng.ldweights(ktm[buf][:, :]),
                                eng.matmul(kvd[:, :], ktm[buf][:, :], vtm[buf][:, :],
                                           start=True, stop=True))[-1])(),
                            deps=kv_deps, name=f"{name}_kvd{gi}")
            st_kvadd = P.comp("vector",
                              lambda eng: eng.tensor_add(KV[:, :], KV[:, :], kvd[:, :]),
                              deps=[st_kvd] + ([st_kvbf] if st_kvbf is not None else []),
                              name=f"{name}_kvadd{gi}")

            prev2_tm = prev["tm_readers"] if prev is not None else []
            prev = {
                "kv_add": st_kvadd,
                "kvbf_readers": [st_nd],
                "tm_readers": [st_nd, st_kvd],
                "prev2_tm_readers": prev2_tm,
                "msk": st_msk,
                "nd": st_nd,
                "out_stages": [st_rec, st_out],
                "ek": st_ek,
                "ev": st_ev,
            }
            blk_readers.extend(na_out)
        prev_blk_tail = blk_readers[-4:] + [prev["kv_add"]]
        last_stages = blk_readers[-2:]
    kb.free(pps, num_ps_t, den_ps_t, stp_t, stp_t2, kvd_t, tp_k_t, tp_v_t)
    kb.free(rec_sb, ssb[1], ssb[0], ktm[1], ktm[0], vtm[1], vtm[0],
            p_scr, e_scr, vf, kf, qf, xblk, KVbf, KV, wv, wk, wq, xln)
    return last_stages


def emit_l0rest(kb, x_tiles, T, attn_src, WoT, gate_wT, gate_b, W13_all, W2_all,
                b13_cols, be2, bs2_row, sel18, mask128, identf,
                ones_col, ones_row, ones_col_bf, deps=(), name="l0r",
                dbg_outs=None):
    P = kb.prog
    NE = 18
    NC = T // 512

    wo_out = kb.sb("lr_woout", [128, 8, T], BF16)
    wo_sb = kb.sb("lr_wo", [128, 8, 1024], BF16)
    attn_sb = kb.sb("lr_attn", [128, 8, T], BF16)
    st_a = load(kb, attn_sb[:, :, :], attn_src.rearrange("(g p) t -> p g t", p=128),
                deps=deps, name=f"{name}_lda")
    st_w = P.dma("gpsimd",
                 lambda eng: [eng.dma_start(out=wo_sb[:, :, :],
                                            in_=WoT.rearrange("(g p) m -> p g m", p=128))],
                 deps=list(deps), n_dma=1, name=f"{name}_ldwo")
    WOW = min(T, 1024)
    pp = kb.ps("lr_pp", [128, WOW], F32)
    ev_prev = []
    mm_prev = []
    for hf in range(T // WOW):
        h2 = slice(hf * WOW, (hf + 1) * WOW)
        for mg in range(8):
            st_mm = matmul_fm(kb, pp[:, :],
                              [wo_sb[:, kg, mg * 128:(mg + 1) * 128] for kg in range(8)],
                              [attn_sb[:, kg, h2] for kg in range(8)],
                              deps=[st_a, st_w] + ev_prev, name=f"{name}_womm{hf}_{mg}")
            st_ev = P.comp("scalar",
                           (lambda mg=mg, h2=h2: lambda eng:
                            eng.copy(wo_out[:, mg, h2], pp[:, :]))(),
                           deps=[st_mm], name=f"{name}_woev{hf}_{mg}")
            ev_prev = [st_ev]
            mm_prev.append(st_ev)
    kb.free(pp)
    kb.free(attn_sb, wo_sb)
    wo_tiles = [wo_out[:, g, :] for g in range(8)]
    ln2_fin = emit_ln(kb, wo_tiles, None, T, resid_tiles=x_tiles,
                      deps=mm_prev, ones_col=ones_col, ones_row=ones_row,
                      ones_col_bf=ones_col_bf, name=f"{name}_ln2")
    kb.free(wo_out)

    y_sb = kb.sb("lr_y", [128, 8, T], F32)
    xln2 = kb.sb("lr_xln2", [128, 8, T], BF16)
    xln2_tiles = [xln2[:, g, :] for g in range(8)]
    lnm_fin = emit_ln(kb, x_tiles, xln2_tiles, T, deps=ln2_fin,
                      ones_col=ones_col, ones_row=ones_row, name=f"{name}_lnm")

    sel_sb = kb.sb("lr_sel", [18, NE * 128], BF16)
    be2_sb = kb.sb("lr_be2", [16, 1024], BF16)
    bs2c_sb = kb.sb("lr_bs2c", [128, 8], F32)
    b13_sb = kb.sb("lr_b13", [128, 8, NE], F32)
    comb_sb = kb.sb("lr_comb", [18, T], BF16)
    gw_sb = kb.sb("lr_gw", [128, 8, 16], BF16)
    gb_sb = kb.sb("lr_gb", [1, 16], F32)
    st_gw = P.dma("gpsimd",
                  lambda eng: [
                      eng.dma_start(out=gw_sb[:, :, :],
                                    in_=gate_wT.rearrange("(g p) m -> p g m", p=128)),
                      eng.dma_start(out=gb_sb[:, :], in_=gate_b),
                      eng.dma_start(out=sel_sb[:, :], in_=sel18),
                      eng.dma_start(out=be2_sb[:, :], in_=be2),
                      eng.dma_start(out=bs2c_sb[:, :],
                                    in_=bs2_row.rearrange("a (g p) -> (a p) g", p=128),
                                    allow_slow_non_contiguous=True),
                      eng.dma_start(out=b13_sb[:, :, :], in_=b13_cols),
                  ], deps=list(lnm_fin), n_dma=6, name=f"{name}_ldg")

    sc_sb = kb.sb("lr_scsb", [16, T], BF16)
    identbf_g = kb.sb("lr_idbf", [128, 128], BF16)
    gbb_sb = kb.sb("lr_gbbsb", [128, 16], F32)
    sc_ps = kb.ps("lr_scps", [16, T], F32)
    st_scmm = matmul_fm(kb, sc_ps[:, :],
                        [gw_sb[:, kg, :] for kg in range(8)],
                        xln2_tiles,
                        deps=lnm_fin + [st_gw], name=f"{name}_scmm")
    st_scev = P.comp("vector",
                     lambda eng: eng.tensor_copy(sc_sb[:, :], sc_ps[:, :]),
                     deps=[st_scmm], name=f"{name}_scev")
    st_idbf = P.comp("vector",
                     lambda eng: eng.tensor_copy(identbf_g[:, :], identf),
                     deps=list(deps), name=f"{name}_idbf")
    kb.free(sc_ps)

    gbb_ps = kb.ps("lr_gbb", [128, 16], F32)
    st_gbb = matmul_fm(kb, gbb_ps[:, :], [ones_row], [gb_sb[:, :]],
                       deps=[st_gw, st_scev], name=f"{name}_gbb")
    st_gbbe = P.comp("vector",
                     lambda eng: eng.tensor_copy(gbb_sb[:, :], gbb_ps[:, :]),
                     deps=[st_gbb], name=f"{name}_gbbe")
    kb.free(gbb_ps)

    st_comb1 = P.comp("gpsimd", lambda eng: eng.memset(comb_sb[:, :], 1.0),
                      deps=list(deps), name=f"{name}_comb1")

    ntile = T // 128
    sc_all = kb.sb("lr_scall", [128, ntile, 16], F32)
    sm_all = kb.sb("lr_small", [128, ntile, 16], F32)
    s3_all = kb.sb("lr_s3all", [128, ntile, 16], F32)
    red_a = kb.sb("lr_reda", [128, ntile, 2], F32)
    gs_a = kb.sb("lr_gsa", [128, ntile, 8], F32)
    pr_a = kb.sb("lr_pra", [128, ntile, 24], F32)
    keep_a = kb.sb("lr_keepa", [128, ntile, 8], F32)
    tp_ps = kb.ps("lr_tpps", [128, 16], BF16)
    ct_ps = kb.ps("lr_ctps", [16, 128], F32)

    tfin = []
    prev_cp = None
    for t in range(ntile):
        tsl = slice(t * 128, (t + 1) * 128)
        st_tp = P.comp("tensor",
                       (lambda ts=tsl: lambda eng:
                        eng.transpose(tp_ps[:, :], sc_sb[:, ts], identbf_g[0:16, 0:16]))(),
                       deps=[st_scev, st_gbbe, st_idbf] +
                            ([prev_cp] if prev_cp else []),
                       name=f"{name}_gtp{t}")
        st_cp = P.comp("vector",
                       (lambda t=t: lambda eng:
                        eng.tensor_copy(sc_all[:, t, :], tp_ps[:, :]))(),
                       deps=[st_tp], name=f"{name}_gcp{t}")
        prev_cp = st_cp
        tfin.append(st_cp)

    def stage(nm, fn, deps):
        return P.comp("vector", fn, deps=deps, name=f"{name}_{nm}")

    st1 = stage("gmax", lambda eng: eng.tensor_reduce(
        red_a[:, :, 0:1], sc_all[:, :, :], AX.X, ALU.max), tfin)
    st2 = stage("gsub", lambda eng: eng.tensor_tensor(
        sm_all[:, :, :], sc_all[:, :, :],
        red_a[:, :, 0:1].broadcast_to([128, ntile, 16]), ALU.subtract), [st1])
    st3 = P.comp("scalar", lambda eng: eng.activation(
        sm_all[:, :, :], sm_all[:, :, :], AF.Exp), deps=[st2],
        name=f"{name}_gexp")
    st4 = stage("gsum", lambda eng: eng.tensor_reduce(
        red_a[:, :, 1:2], sm_all[:, :, :], AX.X, ALU.add), [st3])
    st5 = stage("grec", lambda eng: eng.reciprocal(
        red_a[:, :, 1:2], red_a[:, :, 1:2]), [st4])
    st6 = stage("gnrm", lambda eng: eng.tensor_tensor(
        sm_all[:, :, :], sm_all[:, :, :],
        red_a[:, :, 1:2].broadcast_to([128, ntile, 16]), ALU.mult), [st5])
    st7 = stage("gs2", lambda eng: eng.tensor_tensor(
        s3_all[:, :, :], sm_all[:, :, :],
        gbb_sb[:, :].rearrange("p e -> p () e").broadcast_to([128, ntile, 16]),
        ALU.add), [st6, st_gbbe])

    pairs = [(0, 1), (0, 2), (0, 3), (1, 2), (1, 3), (2, 3)]

    def emit_pairsum(eng):
        v4 = s3_all[:, :, :].rearrange("p t (a b) -> p t a b", a=4)
        for i, (a, b) in enumerate(pairs):
            eng.tensor_add(pr_a[:, :, 4 * i:4 * i + 4], v4[:, :, :, a], v4[:, :, :, b])
        return eng.tensor_copy(red_a[:, :, 0:1], red_a[:, :, 0:1])
    st8 = stage("gps", emit_pairsum, [st7])

    def emit_gsmax(eng):
        eng.tensor_max(gs_a[:, :, 0:4], pr_a[:, :, 0:4], pr_a[:, :, 4:8])
        eng.tensor_max(gs_a[:, :, 4:8], pr_a[:, :, 8:12], pr_a[:, :, 12:16])
        return eng.tensor_max(pr_a[:, :, 16:20], pr_a[:, :, 16:20], pr_a[:, :, 20:24])
    st9 = stage("ggm0", emit_gsmax, [st8])

    def emit_gsmax2(eng):
        eng.tensor_max(gs_a[:, :, 0:4], gs_a[:, :, 0:4], gs_a[:, :, 4:8])
        return eng.tensor_copy(red_a[:, :, 0:1], red_a[:, :, 0:1])
    st10 = stage("ggm1", emit_gsmax2, [st9])
    st11 = stage("ggm2", lambda eng: eng.tensor_max(
        gs_a[:, :, 0:4], gs_a[:, :, 0:4], pr_a[:, :, 16:20]), [st10])

    def emit_pairmin(eng):
        for i, (a, b) in enumerate(pairs):
            eng.tensor_tensor(pr_a[:, :, i:i + 1], gs_a[:, :, a:a + 1],
                              gs_a[:, :, b:b + 1], ALU.min)
        return eng.tensor_copy(red_a[:, :, 0:1], red_a[:, :, 0:1])
    st12 = stage("gpm", emit_pairmin, [st11])

    def emit_sm2(eng):
        eng.tensor_max(pr_a[:, :, 6:7], pr_a[:, :, 0:1], pr_a[:, :, 1:2])
        eng.tensor_max(pr_a[:, :, 7:8], pr_a[:, :, 2:3], pr_a[:, :, 3:4])
        return eng.tensor_max(pr_a[:, :, 8:9], pr_a[:, :, 4:5], pr_a[:, :, 5:6])
    st13 = stage("gsm2a", emit_sm2, [st12])
    st14 = stage("gsm2b", lambda eng: eng.tensor_max(
        pr_a[:, :, 6:7], pr_a[:, :, 6:7], pr_a[:, :, 7:8]), [st13])
    st15 = stage("gsm2c", lambda eng: eng.tensor_max(
        pr_a[:, :, 6:7], pr_a[:, :, 6:7], pr_a[:, :, 8:9]), [st14])
    st16 = stage("gkeep", lambda eng: eng.tensor_tensor(
        keep_a[:, :, 0:4], gs_a[:, :, 0:4],
        pr_a[:, :, 6:7].broadcast_to([128, ntile, 4]), ALU.is_ge), [st15])

    def emit_s3(eng):
        v4 = s3_all[:, :, :].rearrange("p t (a b) -> p t a b", a=4)
        kb4 = keep_a[:, :, 0:4].rearrange("p t a -> p t a ()").broadcast_to(
            [128, ntile, 4, 4])
        eng.tensor_tensor(v4, v4, kb4, ALU.mult)
        eng.tensor_scalar(keep_a[:, :, 4:8], keep_a[:, :, 0:4], 1e30, 1e30,
                          ALU.mult, ALU.subtract)
        return eng.tensor_copy(red_a[:, :, 0:1], red_a[:, :, 0:1])
    st17 = stage("gs3", emit_s3, [st16])
    st18 = stage("gs3b", lambda eng: eng.tensor_tensor(
        s3_all[:, :, :].rearrange("p t (a b) -> p t a b", a=4),
        s3_all[:, :, :].rearrange("p t (a b) -> p t a b", a=4),
        keep_a[:, :, 4:8].rearrange("p t a -> p t a ()").broadcast_to(
            [128, ntile, 4, 4]), ALU.add), [st17])
    st19 = stage("gm1", lambda eng: eng.tensor_reduce(
        red_a[:, :, 0:1], s3_all[:, :, :], AX.X, ALU.max), [st18])
    st20 = stage("geq", lambda eng: eng.tensor_tensor(
        pr_a[:, :, 0:16], s3_all[:, :, :],
        red_a[:, :, 0:1].broadcast_to([128, ntile, 16]), ALU.is_equal), [st19])
    st21 = stage("gs4", lambda eng: eng.scalar_tensor_tensor(
        pr_a[:, :, 0:16], pr_a[:, :, 0:16], -1e30, s3_all[:, :, :],
        ALU.mult, ALU.add), [st20])
    st22 = stage("gm2", lambda eng: eng.tensor_reduce(
        red_a[:, :, 1:2], pr_a[:, :, 0:16], AX.X, ALU.max), [st21])
    st23 = stage("gsel", lambda eng: eng.tensor_tensor(
        pr_a[:, :, 0:16], s3_all[:, :, :],
        red_a[:, :, 1:2].broadcast_to([128, ntile, 16]), ALU.is_ge), [st22])
    st24 = stage("gcomb", lambda eng: eng.tensor_mul(
        sm_all[:, :, :], sm_all[:, :, :], pr_a[:, :, 0:16]), [st23])

    comb_done = []
    prev_ce = None
    for t in range(ntile):
        tsl = slice(t * 128, (t + 1) * 128)
        st_ct = P.comp("tensor",
                       (lambda t=t: lambda eng:
                        eng.transpose(ct_ps[:, :], sm_all[:, t, :], identf))(),
                       deps=[st24] + ([prev_ce] if prev_ce else []),
                       name=f"{name}_gct{t}")
        st_ce = P.comp("vector",
                       (lambda ts=tsl: lambda eng:
                        eng.tensor_copy(comb_sb[0:16, ts], ct_ps[:, :][0:16, :]))(),
                       deps=[st_ct, st_comb1], name=f"{name}_gce{t}")
        comb_done.append(st_ce)
        prev_ce = st_ce
    kb.free(tp_ps, ct_ps)
    kb.free(keep_a, pr_a, gs_a, red_a, s3_all, sm_all, sc_all)
    kb.free(gbb_sb, identbf_g, sc_sb, gb_sb, gw_sb)

    if dbg_outs and "comb" in dbg_outs:
        store(kb, dbg_outs["comb"], comb_sb[:, :], deps=comb_done,
              name=f"{name}_dbgcomb")
    if dbg_outs and "x1" in dbg_outs:
        for g in range(8):
            store(kb, dbg_outs["x1"][g * 128:(g + 1) * 128, :], x_tiles[g],
                  deps=lnm_fin, name=f"{name}_dbgx1_{g}")
    xspill = kb.dram(kb.u("lr_xspill"), [6 * 128, T], F32)
    st_xsp = P.dma("gpsimd",
                   lambda eng: [eng.dma_start(
                       out=xspill.ap()[g * 128:(g + 1) * 128, :], in_=x_tiles[g])
                       for g in range(6)],
                   deps=list(ln2_fin), n_dma=6, name=f"{name}_xsp")
    t3_01 = x_tiles[0].bitcast(BF16).rearrange("p (a t) -> p a t", a=2)
    t3_23 = x_tiles[1].bitcast(BF16).rearrange("p (a t) -> p a t", a=2)
    t3v = [t3_01[:, 0, :], t3_01[:, 1, :], t3_23[:, 0, :], t3_23[:, 1, :]]
    z_v2 = x_tiles[2].bitcast(BF16).rearrange("p (a t) -> p a t", a=2)
    zv = [z_v2[:, 0, :], z_v2[:, 1, :]]
    s1_v2 = x_tiles[3].bitcast(BF16).rearrange("p (a t) -> p a t", a=2)
    s1v = [s1_v2[:, 0, :], s1_v2[:, 1, :]]
    comb_bc = x_tiles[4].bitcast(BF16).rearrange("p (a t) -> p a t", a=2)[:, 0, :]

    moe_ps = kb.ps("lr_moeps", [128, 8, 512], F32)
    pgA = moe_ps[:, 0:4, :].rearrange("p a t -> p (a t)")
    pgB = moe_ps[:, 4:8, :].rearrange("p a t -> p (a t)")

    seed_done = []
    seed_prev = None
    for c in range(NC):
        csl = slice(c * 512, (c + 1) * 512)

        def emit_seed(eng, csl=csl):
            last = None
            for mg in range(8):
                msl = slice(mg * 128, (mg + 1) * 128)
                eng.ldweights(be2_sb[:, msl])
                last = eng.matmul(moe_ps[:, mg, :], be2_sb[:, msl],
                                  comb_sb[0:16, csl], start=True, stop=True)
            return last
        st_seed = P.comp("tensor", emit_seed,
                         deps=comb_done + [st_gw] +
                              ([seed_prev] if seed_prev else []),
                         name=f"{name}_seed{c}")

        def emit_sev(eng, csl=csl):
            last = None
            for mg in range(8):
                last = eng.activation(y_sb[:, mg, csl], moe_ps[:, mg, :],
                                      AF.Identity, bias=bs2c_sb[:, mg:mg + 1])
            return last
        st_sev = P.comp("scalar", emit_sev, deps=[st_seed],
                        name=f"{name}_seedev{c}")
        seed_prev = st_sev
        seed_done.append(st_sev)

    w13_sb = [kb.sb("lr_w13", [128, 8, 1024], BF16) for _ in range(2)]
    if 2 * T >= 4096:
        w2_sb = x_tiles[5].bitcast(BF16).rearrange("p (a m) -> p a m", a=4)
        w2_alloc = None
    else:
        w2_alloc = kb.sb("lr_w2", [128, 4, 1024], BF16)
        w2_sb = w2_alloc[:, :, :]
    w13_reader = [None, None]
    w2_reader = None
    hp_prev = None
    hp_par_prev = [None, None]
    last_w2 = None
    zev_prev = [None, None]
    acc_last = [None] * 8
    pool_accs = []
    for e in range(NE):
        buf = e % 2
        st_we = P.dma("gpsimd",
                      (lambda e=e, buf=buf: lambda eng: [
                          eng.dma_start(out=w13_sb[buf][:, :, :],
                                        in_=W13_all[e].rearrange(
                                            "(g p) m -> p g m", p=128))])(),
                      deps=([w13_reader[buf]] if w13_reader[buf]
                            else list(comb_done)),
                      n_dma=1, name=f"{name}_we{e}")
        st_w2ld = P.dma("gpsimd",
                        (lambda e=e: lambda eng: [
                            eng.dma_start(out=w2_sb[:, :, :],
                                          in_=W2_all[e].rearrange(
                                              "(g p) m -> p g m", p=128))])(),
                        deps=[st_xsp] +
                             ([w2_reader] if w2_reader else list(comb_done)),
                        n_dma=1, name=f"{name}_w2ld{e}")
        st_cb = P.comp("tensor",
                       (lambda e=e: lambda eng: (
                           eng.ldweights(sel_sb[:, e * 128:(e + 1) * 128]),
                           [eng.matmul(pgA[:, c0:min(c0 + 512, T)],
                                       sel_sb[:, e * 128:(e + 1) * 128],
                                       comb_sb[:, c0:min(c0 + 512, T)],
                                       start=True, stop=True)
                            for c0 in range(0, T, 512)])[-1][-1])(),
                       deps=comb_done + [st_gw] +
                            ([zev_prev[0]] if zev_prev[0] else seed_done),
                       name=f"{name}_cb{e}")
        st_cbe = P.comp("scalar",
                        lambda eng: eng.copy(comb_bc, pgA[:, 0:T]),
                        deps=[st_cb, st_xsp] +
                             ([hp_prev] if hp_prev else []),
                        name=f"{name}_cbe{e}")
        evA_prev = None
        evB_prev = None
        hps = []
        for ig in range(4):
            sp = ig % 2

            def emit_mmA(eng, ig=ig, buf=buf):
                last = None
                for kg in range(8):
                    eng.ldweights(w13_sb[buf][:, kg, ig * 128:(ig + 1) * 128])
                    for c0 in range(0, T, 512):
                        last = eng.matmul(pgA[:, c0:min(c0 + 512, T)],
                                          w13_sb[buf][:, kg, ig * 128:(ig + 1) * 128],
                                          xln2[:, kg, c0:min(c0 + 512, T)],
                                          start=(kg == 0), stop=(kg == 7))
                return last
            st_mmA = P.comp("tensor", emit_mmA,
                            deps=[st_we] +
                                 ([evA_prev] if evA_prev else [st_cbe]),
                            name=f"{name}_mA{e}i{ig}")

            def emit_mmB(eng, ig=ig, buf=buf):
                last = None
                for kg in range(8):
                    eng.ldweights(w13_sb[buf][:, kg, (4 + ig) * 128:(5 + ig) * 128])
                    for c0 in range(0, T, 512):
                        last = eng.matmul(pgB[:, c0:min(c0 + 512, T)],
                                          w13_sb[buf][:, kg, (4 + ig) * 128:(5 + ig) * 128],
                                          xln2[:, kg, c0:min(c0 + 512, T)],
                                          start=(kg == 0), stop=(kg == 7))
                return last
            st_mmB = P.comp("tensor", emit_mmB,
                            deps=[st_we] +
                                 ([evB_prev] if evB_prev else
                                  ([zev_prev[1]] if zev_prev[1] else seed_done)),
                            name=f"{name}_mB{e}i{ig}")
            st_evA = P.comp("scalar",
                            (lambda ig=ig, sp=sp, e=e: lambda eng:
                             eng.activation(s1v[sp], pgA[:, 0:T], AF.Silu,
                                            bias=b13_sb[:, ig, e:e + 1]))(),
                            deps=[st_mmA, st_xsp] +
                                 ([hps[ig - 2]] if ig >= 2 else
                                  ([hp_par_prev[sp]] if hp_par_prev[sp] else [])),
                            name=f"{name}_evA{e}i{ig}")
            st_evB = P.comp("scalar",
                            (lambda ig=ig, e=e: lambda eng:
                             eng.activation(t3v[ig], pgB[:, 0:T], AF.Identity,
                                            bias=b13_sb[:, 4 + ig, e:e + 1]))(),
                            deps=[st_mmB, st_xsp] +
                                 ([last_w2] if last_w2 else []),
                            name=f"{name}_evB{e}i{ig}")
            st_hp = P.comp("vector",
                           (lambda ig=ig, sp=sp: lambda eng: (
                               eng.tensor_mul(t3v[ig], s1v[sp], t3v[ig]),
                               eng.tensor_mul(t3v[ig], t3v[ig], comb_bc))[-1])(),
                           deps=[st_evA, st_evB, st_cbe],
                           name=f"{name}_hp{e}i{ig}")
            hps.append(st_hp)
            evA_prev = st_evA
            evB_prev = st_evB
        zev_now = [None, None]
        for og2 in range(8):
            wb = og2 % 2
            pg = pgA if wb == 0 else pgB

            def emit_w2(eng, og2=og2, pg=pg):
                last = None
                for kg in range(4):
                    eng.ldweights(w2_sb[:, kg, og2 * 128:(og2 + 1) * 128])
                    for c0 in range(0, T, 512):
                        last = eng.matmul(pg[:, c0:min(c0 + 512, T)],
                                          w2_sb[:, kg, og2 * 128:(og2 + 1) * 128],
                                          t3v[kg][:, c0:min(c0 + 512, T)],
                                          start=(kg == 0), stop=(kg == 3))
                return last
            w2deps = [st_w2ld] + hps
            if zev_now[wb] is not None:
                w2deps.append(zev_now[wb])
            elif og2 == 0:
                w2deps.append(evA_prev)
            elif og2 == 1:
                w2deps.append(evB_prev)
            st_w2 = P.comp("tensor", emit_w2, deps=w2deps,
                           name=f"{name}_w2e{e}o{og2}")
            st_zev = P.comp("scalar",
                            (lambda wb=wb, pg=pg: lambda eng:
                             eng.copy(zv[wb], pg[:, 0:T]))(),
                            deps=[st_w2, st_xsp] +
                                 ([acc_last[og2 - 2]] if og2 >= 2 else
                                  [a for a in (acc_last[6], acc_last[7])
                                   if a is not None][-1:]),
                            name=f"{name}_zev{e}o{og2}")
            st_acc = P.comp("gpsimd" if og2 % 2 == 0 else "vector",
                            (lambda og2=og2, wb=wb: lambda eng:
                             eng.tensor_add(y_sb[:, og2, :], y_sb[:, og2, :],
                                            zv[wb]))(),
                            deps=[st_zev] +
                                 ([acc_last[og2]] if acc_last[og2]
                                  else seed_done),
                            name=f"{name}_acc{e}o{og2}")
            zev_now[wb] = st_zev
            acc_last[og2] = st_acc
            pool_accs.append(st_acc)
            last_w2 = st_w2
        zev_prev = zev_now
        hp_prev = hps[3]
        hp_par_prev = [hps[2], hps[3]]
        w13_reader[buf] = st_mmB
        w2_reader = last_w2
    kb.free(moe_ps)

    if dbg_outs and "y" in dbg_outs:
        for g in range(8):
            store(kb, dbg_outs["y"][g * 128:(g + 1) * 128, :], y_sb[:, g, :],
                  deps=[a for a in acc_last if a is not None],
                  name=f"{name}_dbgy_{g}")
    st_xrl = P.dma("gpsimd",
                   lambda eng: [eng.dma_start(
                       out=x_tiles[g], in_=xspill.ap()[g * 128:(g + 1) * 128, :])
                       for g in range(6)],
                   deps=[last_w2] + hps + [a for a in acc_last if a is not None],
                   n_dma=6, name=f"{name}_xrl")
    kb.free(xln2, sel_sb, be2_sb, bs2c_sb, b13_sb, comb_sb,
            w13_sb[0], w13_sb[1],
            *([w2_alloc] if w2_alloc is not None else []))
    y_tiles = [y_sb[:, g, :] for g in range(8)]
    ln4_fin = emit_ln(kb, y_tiles, None, T, resid_tiles=x_tiles,
                      deps=[a for a in acc_last if a is not None] + [st_xrl],
                      ones_col=ones_col, ones_row=ones_row, name=f"{name}_lnmo")
    kb.free(y_sb)
    return ln4_fin



def emit_mamba(kb, x_tiles, T, S, W_in, conv_w_cols, conv_b_cols, W_x, wdt_col,
               bdt_col, alog_cols, d_cols, W_out, selC, halo_mask, tok_mask,
               ones_col, ones_row, ones_col_bf=None, deps=(), name="mb"):
    P = kb.prog
    NCK = 8
    R = 8

    cw_sb = kb.sb("mb_cw", [128, 16, 4], F32)
    cbb_sb = kb.sb("mb_cb", [128, 16], F32)
    wx_sb = kb.sb("mb_wx", [128, 16, 33], BF16)
    wdt_sb = kb.sb("mb_wdt", [128, 2], F32)
    bdt_sb = kb.sb("mb_bdt", [128, 2], F32)
    acol_sb = kb.sb("mb_acol", [128, 32], F32)
    dcol_sb = kb.sb("mb_dcol", [128, 16], F32)
    selc_sb = kb.sb("mb_selc", [33, 1], F32)
    hmask_sb = kb.sb("mb_hmask", [8, 1], BF16)
    tmask_sb = kb.sb("mb_tmask", [8, 1], F32)
    st_w = P.dma("gpsimd",
                 lambda eng: [
                     eng.dma_start(out=cw_sb[:, :, :], in_=conv_w_cols),
                     eng.dma_start(out=cbb_sb[:, :], in_=conv_b_cols),
                     eng.dma_start(out=wx_sb[:, :, :],
                                   in_=W_x.rearrange("(g p) m -> p g m", p=128)),
                     eng.dma_start(out=wdt_sb[:, :], in_=wdt_col),
                     eng.dma_start(out=bdt_sb[:, :], in_=bdt_col),
                     eng.dma_start(out=acol_sb[:, :], in_=alog_cols),
                     eng.dma_start(out=dcol_sb[:, :], in_=d_cols),
                     eng.dma_start(out=selc_sb[:, :], in_=selC),
                     eng.dma_start(out=hmask_sb[:, :], in_=halo_mask),
                     eng.dma_start(out=tmask_sb[:, :], in_=tok_mask),
                 ], deps=list(deps), n_dma=10, name=f"{name}_ldw")
    st_a = P.comp("scalar",
                  lambda eng: eng.activation(acol_sb[:, :], acol_sb[:, :], AF.Exp),
                  deps=[st_w], name=f"{name}_aexp")
    st_a2 = P.comp("vector",
                   lambda eng: eng.tensor_single_scalar(acol_sb[:, :], acol_sb[:, :],
                                                        -1.0, ALU.mult),
                   deps=[st_a], name=f"{name}_aneg")

    xln = kb.sb("mb_xln", [128, 8, T], BF16)
    csum_sb = kb.sb("mb_csum", [1, T], F32)
    xln_tiles = [xln[:, g, :] for g in range(8)]
    ln_fin = emit_ln(kb, x_tiles, xln_tiles, T, deps=deps,
                     ones_col=ones_col, ones_row=ones_row, name=f"{name}_ln3")

    xs_a2a_in = kb.dram(kb.u("mb_a2ain"), [2048, T], BF16)
    xs_a2a_out = kb.dram(kb.u("mb_a2aout"), [2048, T], BF16)
    z_stage = kb.dram(kb.u("mb_zstage"), [2048, T], BF16)
    y3_stage = kb.dram(kb.u("mb_y3stage"), [2048, T], BF16)
    win_sb = kb.sb("mb_win", [128, 8, 2048], BF16)
    st_wa = P.dma("gpsimd",
                  lambda eng: [eng.dma_start(
                      out=win_sb[:, :, :],
                      in_=W_in[:, 0:2048].rearrange("(g p) m -> p g m", p=128))],
                  deps=ln_fin, n_dma=1, name=f"{name}_ldwa")

    halo_ag_in = kb.dram(kb.u("mb_haloin"), [2048, 4], BF16)
    halo_ag_out = kb.dram(kb.u("mb_haloout"), [R * 2048, 4], BF16)
    halo_bounce = kb.dram(kb.u("mb_halob"), [1, 2048 * 4], BF16)
    halo_sb = kb.sb("mb_halo", [128, 16, 4], F32)
    halo_all = kb.sb("mb_haloall", [8, 2048 * 4], BF16)
    hsel_sb = kb.sb("mb_hselsb", [1, 2048 * 4], BF16)
    hz_ps = kb.ps("mb_hzps", [128, 4], F32)
    hz_sb = kb.sb("mb_hzsb", [128, 16, 4], BF16)

    halo_stages = []
    st_prev = None
    for og in range(16):
        def emit_hz(eng, og=og):
            last = None
            for kg in range(8):
                eng.ldweights(win_sb[:, kg, og * 128:(og + 1) * 128])
                last = eng.matmul(hz_ps[:, :], win_sb[:, kg, og * 128:(og + 1) * 128],
                                  xln[:, kg, T - 4:T], start=(kg == 0), stop=(kg == 7))
            return last
        st_hz = P.comp("tensor", emit_hz,
                       deps=ln_fin + [st_wa] + ([st_prev] if st_prev else []),
                       name=f"{name}_hz{og}")
        st_he = P.comp("vector",
                       (lambda og=og: lambda eng:
                        eng.tensor_copy(hz_sb[:, og, :], hz_ps[:, :]))(),
                       deps=[st_hz], name=f"{name}_he{og}")
        st_prev = st_he
        halo_stages.append(st_he)
    st_hst = store(kb, halo_ag_in.ap().rearrange("(g p) c -> p g c", p=128),
                   hz_sb[:, :, :], deps=halo_stages, name=f"{name}_hstage")
    st_hag = P.coll(
        lambda eng: eng.collective_compute(
            "AllGather", ALU.bypass, replica_groups=[list(range(R))],
            ins=[halo_ag_in.ap().opt()], outs=[halo_ag_out.ap().opt()]),
        deps=[st_hst], name=f"{name}_hag")
    st_hld = load(kb, halo_all[:, :],
                  halo_ag_out.ap().rearrange("(r q) c -> r (q c)", r=8),
                  deps=[st_hag], name=f"{name}_hld")
    hsel_ps = kb.ps("mb_hselps", [1, 2048], F32)
    st_hsev = None
    for rr in range(4):
        def emit_hsel(eng, rr=rr):
            last = None
            eng.ldweights(hmask_sb[:, :])
            for n0 in range(0, 2048, 512):
                last = eng.matmul(hsel_ps[:, n0:n0 + 512], hmask_sb[:, :],
                                  halo_all[:, rr * 2048 + n0:rr * 2048 + n0 + 512],
                                  start=True, stop=True)
            return last
        st_hsel = P.comp("tensor", emit_hsel,
                         deps=[st_hld, st_w] + ([st_hsev] if st_hsev else []),
                         name=f"{name}_hsel{rr}")
        st_hsev = P.comp("vector",
                         (lambda rr=rr: lambda eng: eng.tensor_copy(
                             hsel_sb[:, rr * 2048:(rr + 1) * 2048], hsel_ps[:, :]))(),
                         deps=[st_hsel], name=f"{name}_hsev{rr}")
    st_hbo = store(kb, halo_bounce.ap(), hsel_sb[:, :], deps=[st_hsev],
                   name=f"{name}_hbo")
    st_hbi = P.dma("gpsimd",
                   lambda eng: [eng.dma_start(
                       out=halo_sb[:, :, :],
                       in_=halo_bounce.ap().rearrange("a (g p c) -> (a p) g c",
                                                      p=128, c=4))],
                   deps=[st_hbo], n_dma=1, name=f"{name}_hbi")
    kb.free(hz_sb, hsel_sb, halo_all)
    kb.free(hz_ps, hsel_ps)

    xs_og = [kb.sb("mb_xsog", [128, T], BF16) for _ in range(2)]
    acc_sb = kb.sb("mb_acc", [128, T], F32)
    corr_sb = kb.sb("mb_corr", [128, 4], F32)
    bcd_ps = kb.ps("mb_bcdps", [33, T], F32)
    xz_ps = kb.ps("mb_xzps", [128, T], F32)
    og_stages = []
    z_stages = []
    bcd_mms = []
    xs_stores = []
    prev_readers = []
    prev_st = {}
    st_wz = None
    st_bcd = None
    for og in range(32):
        is_x = og < 16
        buf = og % 2
        if og == 16:
            st_wz = P.dma("gpsimd",
                          lambda eng: [eng.dma_start(
                              out=win_sb[:, :, :],
                              in_=W_in[:, 2048:4096].rearrange(
                                  "(g p) m -> p g m", p=128))],
                          deps=[og_stages[-1], bcd_mms[-1], halo_stages[-1]],
                          n_dma=1, name=f"{name}_ldwz")

        def emit_xz(eng, og=og):
            last = None
            for kg in range(8):
                eng.ldweights(win_sb[:, kg, (og % 16) * 128:(og % 16 + 1) * 128])
                for n0 in range(0, T, 512):
                    last = eng.matmul(xz_ps[:, n0:min(n0 + 512, T)],
                                      win_sb[:, kg, (og % 16) * 128:(og % 16 + 1) * 128],
                                      xln[:, kg, n0:min(n0 + 512, T)],
                                      start=(kg == 0), stop=(kg == 7))
            return last
        st_mm = P.comp("tensor", emit_xz,
                       deps=ln_fin + [st_wa] + prev_readers +
                            ([st_wz] if og >= 16 else []) +
                            ([st_hsev, st_hbi] if og == 0 else []),
                       name=f"{name}_xz{og}")
        if is_x:
            st_cv = P.comp("vector",
                           (lambda og=og: lambda eng: (
                               eng.tensor_scalar(acc_sb[:, :], xz_ps[:, :],
                                                 cw_sb[:, og, 3:4], None, ALU.mult),
                               eng.scalar_tensor_tensor(
                                   acc_sb[:, 1:T], xz_ps[:, 0:T - 1],
                                   cw_sb[:, og, 2:3], acc_sb[:, 1:T],
                                   ALU.mult, ALU.add),
                               eng.scalar_tensor_tensor(
                                   acc_sb[:, 2:T], xz_ps[:, 0:T - 2],
                                   cw_sb[:, og, 1:2], acc_sb[:, 2:T],
                                   ALU.mult, ALU.add),
                               eng.scalar_tensor_tensor(
                                   acc_sb[:, 3:T], xz_ps[:, 0:T - 3],
                                   cw_sb[:, og, 0:1], acc_sb[:, 3:T],
                                   ALU.mult, ALU.add))[-1])(),
                           deps=[st_mm, st_w], name=f"{name}_cv{og}")
            st_c1 = P.comp("vector",
                           (lambda og=og: lambda eng:
                            eng.tensor_scalar(corr_sb[:, 0:3], halo_sb[:, og, 1:4],
                                              cw_sb[:, og, 0:1], None, ALU.mult))(),
                           deps=[st_hbi, st_w] + ([og_stages[-1]] if og > 0 else []),
                           name=f"{name}_c1o{og}")
            st_c2 = P.comp("vector",
                           (lambda og=og: lambda eng:
                            eng.scalar_tensor_tensor(
                                corr_sb[:, 0:2], halo_sb[:, og, 2:4],
                                cw_sb[:, og, 1:2], corr_sb[:, 0:2],
                                ALU.mult, ALU.add))(),
                           deps=[st_c1], name=f"{name}_c2o{og}")
            st_c3 = P.comp("vector",
                           (lambda og=og: lambda eng:
                            eng.scalar_tensor_tensor(
                                corr_sb[:, 0:1], halo_sb[:, og, 3:4],
                                cw_sb[:, og, 2:3], corr_sb[:, 0:1],
                                ALU.mult, ALU.add))(),
                           deps=[st_c2], name=f"{name}_c3o{og}")
            st_c4 = P.comp("vector",
                           (lambda: lambda eng:
                            eng.tensor_add(acc_sb[:, 0:3], acc_sb[:, 0:3],
                                           corr_sb[:, 0:3]))(),
                           deps=[st_c3, st_cv], name=f"{name}_c4o{og}")
            st_si = P.comp("scalar",
                           (lambda og=og, buf=buf: lambda eng:
                            eng.activation(xs_og[buf][:, :], acc_sb[:, :], AF.Silu,
                                           bias=cbb_sb[:, og:og + 1]))(),
                           deps=[st_c4] + prev_st.get(buf, []),
                           name=f"{name}_si{og}")
            st_so = store(kb, xs_a2a_in.ap()[og * 128:(og + 1) * 128, :],
                          xs_og[buf][:, :], deps=[st_si], name=f"{name}_so{og}")
            xs_stores.append(st_so)
            st_bcd = P.comp("tensor",
                            (lambda og=og, buf=buf: lambda eng: (
                                eng.ldweights(wx_sb[:, og, :]),
                                [eng.matmul(bcd_ps[:, n0:min(n0 + 512, T)],
                                            wx_sb[:, og, :],
                                            xs_og[buf][:, n0:min(n0 + 512, T)],
                                            start=(og == 0), stop=(og == 15))
                                 for n0 in range(0, T, 512)][-1])[-1])(),
                            deps=[st_si, st_w] +
                                 ([bcd_mms[-1]] if bcd_mms else []),
                            name=f"{name}_bcdp{og}")
            bcd_mms.append(st_bcd)
            prev_st[buf] = [st_so, st_bcd]
            prev_readers = [st_cv]
            og_stages.append(st_si)
        else:
            st_cp = P.comp("scalar",
                           (lambda og=og, buf=buf: lambda eng:
                            eng.copy(xs_og[buf][:, :], xz_ps[:, :]))(),
                           deps=[st_mm] + prev_st.get(buf, []),
                           name=f"{name}_zcp{og}")
            st_so = store(kb, z_stage.ap()[(og - 16) * 128:(og - 15) * 128, :],
                          xs_og[buf][:, :], deps=[st_cp], name=f"{name}_zso{og}")
            prev_st[buf] = [st_so]
            prev_readers = [st_cp]
            z_stages.append(st_so)
    kb.free(xz_ps)
    kb.free(corr_sb, acc_sb, xs_og[1], xs_og[0], halo_sb, win_sb)

    bcd_sb = kb.sb("mb_bcdsb", [33, T], F32)
    st_bcde = P.comp("vector",
                     lambda eng: eng.tensor_copy(bcd_sb[:, :], bcd_ps[:, :]),
                     deps=[bcd_mms[-1], z_stages[-1]], name=f"{name}_bcde")
    kb.free(bcd_ps)
    csum_ps = kb.ps("mb_csps", [1, T], F32)
    st_cs = P.comp("tensor",
                   (lambda: lambda eng: [
                       eng.matmul(csum_ps[:, n0:min(n0 + 512, T)], selc_sb[:, :],
                                  bcd_sb[:, n0:min(n0 + 512, T)],
                                  start=True, stop=True)
                       for n0 in range(0, T, 512)][-1])(),
                   deps=[st_bcde, st_w], name=f"{name}_cs")
    st_cse = P.comp("vector",
                    lambda eng: eng.tensor_copy(csum_sb[:, :], csum_ps[:, :]),
                    deps=[st_cs], name=f"{name}_cse")
    kb.free(csum_ps)

    bcd_ag_in = kb.dram(kb.u("mb_bcdin"), [17, T], F32)
    bcd_ag_out = kb.dram(kb.u("mb_bcdout"), [R * 17, T], F32)
    st_bc_st = P.dma("gpsimd",
                     lambda eng: [
                         eng.dma_start(out=bcd_ag_in.ap()[0:16, :],
                                       in_=bcd_sb[0:16, :]),
                         eng.dma_start(out=bcd_ag_in.ap()[16:17, :],
                                       in_=bcd_sb[32:33, :]),
                     ], deps=[st_bcde], n_dma=2, name=f"{name}_bcst")

    def emit_a2a(eng):
        from concourse.aot_env import allow_ici_all_to_all
        with allow_ici_all_to_all():
            return eng.collective_compute(
                "AllToAll", ALU.bypass, replica_groups=[list(range(R))],
                ins=[xs_a2a_in.ap().opt()], outs=[xs_a2a_out.ap().opt()])
    st_a2a = P.coll(emit_a2a, deps=xs_stores, name=f"{name}_a2a")
    st_bag = P.coll(
        lambda eng: eng.collective_compute(
            "AllGather", ALU.bypass, replica_groups=[list(range(R))],
            ins=[bcd_ag_in.ap().opt()], outs=[bcd_ag_out.ap().opt()]),
        deps=[st_bc_st], name=f"{name}_bag")
    kb.free(bcd_sb)

    x_ck = xln[:, 0:2, :]
    u_ck = xln[:, 2:4, :]
    dt_ck = xln[:, 4:8, :].rearrange("p a t -> p (a t)").bitcast(F32).rearrange(
        "p (g t) -> p g t", g=2)
    dtraw_sb = kb.sb("mb_dtraw", [1, T], F32)
    brow_sb = [kb.sb("mb_brow", [1, T], BF16) for _ in range(2)]
    bbc_sb = [kb.sb("mb_bbc", [128, T], BF16) for _ in range(2)]
    scr2 = kb.sb("mb_scr2", [128, 2, T], F32)
    t_scr = scr2
    abar_sb = [scr2[:, 0, :], scr2[:, 1, :]]
    bu_sb = [kb.sb("mb_bu", [128, T], BF16) for _ in range(2)]
    h2_sb = kb.sb("mb_h", [128, 2, T], F32)
    yrow_sb = kb.sb("mb_yrow", [1, T], F32)
    hstate = kb.sb("mb_hstate", [128, 32], F32)
    xck_b = kb.sb("mb_xckb", [128, 2, T], BF16)
    x_bufs = [x_ck, xck_b[:, :, :]]
    st_h0 = P.comp("gpsimd", lambda eng: eng.memset(hstate[:, :], 0.0),
                   deps=[st_cse, z_stages[-1]], name=f"{name}_h0")
    ones_col_b = ones_col_bf if ones_col_bf is not None else kb.consts["ones_col_bf"]
    ones_row_b = kb.consts["ones_row_bf"]

    ar_in = kb.dram(kb.u("mb_arin"), [8, T], F32)
    ar_out = kb.dram(kb.u("mb_arout"), [8, T], F32)
    dtb_ps = kb.ps("mb_dtbps", [128, T], F32)
    y_ps = kb.ps("mb_yps", [1, T], F32)

    prev_ck = {}
    sc_p = {}
    hb_p = {}
    hc_p = {}
    red_p = {}
    bu_p = {}
    ab_p = {}
    bbcm_p = {}
    bbc_prev = [None]
    bu_of_bbcbuf = {}
    st_yev_prev = None
    last_yst = None
    ldx_stages = {}
    u_stages = {}
    for ck in range(NCK):
        if ck == 0:
            ldx_stages[0] = load(
                kb, x_bufs[0],
                xs_a2a_out.ap()[0:256, :].rearrange("(g p) t -> p g t", p=128),
                deps=[st_a2a] + og_stages + z_stages, name=f"{name}_ldx0")
        st_ldx = ldx_stages[ck]
        st_lddt = load(kb, dtraw_sb[:, :],
                       bcd_ag_out.ap()[ck * 17 + 16:ck * 17 + 17, :],
                       deps=[st_bag] + prev_ck.get("dtraw_readers", []) +
                            ([st_cse] if ck == 0 else []),
                       name=f"{name}_lddt{ck}", engine="sync")
        st_dtb = P.comp("tensor",
                        (lambda: lambda eng: [
                            eng.matmul(dtb_ps[:, n0:min(n0 + 512, T)], ones_row,
                                       dtraw_sb[:, n0:min(n0 + 512, T)],
                                       start=True, stop=True)
                            for n0 in range(0, T, 512)][-1])(),
                        deps=[st_lddt, st_h0] + prev_ck.get("dtb_readers", []),
                        name=f"{name}_dtb{ck}")
        st_dts = []
        t1s = []
        for g in range(2):
            st_t1 = P.comp("vector",
                           (lambda g=g: lambda eng:
                            eng.tensor_scalar(t_scr[:, g, :], dtb_ps[:, :],
                                              wdt_sb[:, g:g + 1], bdt_sb[:, g:g + 1],
                                              ALU.mult, ALU.add))(),
                           deps=[st_dtb, st_w] +
                                ([sc_p[g]] if g in sc_p else []),
                           name=f"{name}_dt1c{ck}g{g}")
            t1s.append(st_t1)
            st_t2 = P.comp("scalar",
                           (lambda g=g: lambda eng:
                            eng.activation(dt_ck[:, g, :], t_scr[:, g, :], AF.Abs))(),
                           deps=[st_t1] +
                                ([ab_p[g]] if g in ab_p else []) +
                                prev_ck.get("uck_readers", []) +
                                ([st_ldx] if ck == 0 else []),
                           name=f"{name}_dt2c{ck}g{g}")
            st_t3 = P.comp("scalar",
                           (lambda g=g: lambda eng:
                            eng.activation(dt_ck[:, g, :], dt_ck[:, g, :], AF.Exp,
                                           scale=-1.0))(),
                           deps=[st_t2], name=f"{name}_dt3c{ck}g{g}")
            st_t4 = P.comp("vector",
                           (lambda g=g: lambda eng:
                            eng.tensor_single_scalar(dt_ck[:, g, :], dt_ck[:, g, :],
                                                     1.0, ALU.add))(),
                           deps=[st_t3], name=f"{name}_dt4c{ck}g{g}")
            st_t5 = P.comp("scalar",
                           (lambda g=g: lambda eng:
                            eng.activation(dt_ck[:, g, :], dt_ck[:, g, :], AF.Ln))(),
                           deps=[st_t4], name=f"{name}_dt5c{ck}g{g}")
            st_t6 = P.comp("vector",
                           (lambda g=g: lambda eng: (
                             eng.tensor_scalar_max(t_scr[:, g, :], t_scr[:, g, :], 0.0),
                             eng.tensor_add(dt_ck[:, g, :], dt_ck[:, g, :],
                                            t_scr[:, g, :]))[-1])(),
                           deps=[st_t5], name=f"{name}_dt6c{ck}g{g}")
            st_dts.append(st_t6)
        st_u = P.comp("vector",
                      (lambda ck=ck: lambda eng:
                       eng.tensor_mul(u_ck, dt_ck, x_bufs[ck % 2]))(),
                      deps=st_dts + [st_ldx] +
                           [x for x in (bu_p.get(0), bu_p.get(1)) if x],
                      name=f"{name}_u{ck}")
        u_stages[ck] = st_u
        if ck + 1 < NCK:
            ldx_stages[ck + 1] = load(
                kb, x_bufs[(ck + 1) % 2],
                xs_a2a_out.ap()[(ck + 1) * 256:(ck + 2) * 256, :].rearrange(
                    "(g p) t -> p g t", p=128),
                deps=[st_a2a] +
                     ([u_stages[ck - 1]] if ck - 1 in u_stages else
                      z_stages[-1:] + [st_cse]),
                name=f"{name}_ldx{ck + 1}")

        bbc_st = {}

        def _emit_bn(n, ck=ck, t1s=t1s):
            lbuf = n % 2
            st_lb = P.dma("gpsimd",
                          (lambda n=n, ck=ck, lbuf=lbuf: lambda eng: [
                              eng.dma_start(
                                  out=brow_sb[lbuf][:, :],
                                  in_=bcd_ag_out.ap()[ck * 17 + n:ck * 17 + n + 1, :])
                          ])(),
                          deps=[st_bag] +
                               ([bbcm_p[lbuf]] if lbuf in bbcm_p else []) +
                               ([st_cse] if (ck == 0 and n < 2) else []),
                          n_dma=1, name=f"{name}_lbc{ck}n{n}")
            st_bbcm = P.comp("tensor",
                             (lambda lbuf=lbuf: lambda eng: [
                                 eng.matmul(dtb_ps[:, n0:min(n0 + 512, T)],
                                            ones_row_b,
                                            brow_sb[lbuf][:, n0:min(n0 + 512, T)],
                                            start=True, stop=True)
                                 for n0 in range(0, T, 512)][-1])(),
                             deps=[st_lb] +
                                  ([t1s[1]] if n == 0 else []) +
                                  ([bbc_prev[0]] if bbc_prev[0] else []),
                             name=f"{name}_bbm{ck}n{n}")
            bbcm_p[lbuf] = st_bbcm
            nbuf = n % 2
            st_bbc = P.comp("scalar",
                            (lambda nbuf=nbuf: lambda eng:
                             eng.copy(bbc_sb[nbuf][:, :], dtb_ps[:, :]))(),
                            deps=[st_bbcm] +
                                 ([bu_of_bbcbuf[nbuf]] if nbuf in bu_of_bbcbuf
                                  else []),
                            name=f"{name}_bbc{ck}n{n}")
            bbc_prev[0] = st_bbc
            bbc_st[n] = st_bbc

        ab_st = {}

        def _emit_ab(s, ck=ck, st_dts=st_dts):
            g = s % 2
            buf = s % 2
            st = P.comp("scalar",
                        (lambda g=g, s=s, buf=buf: lambda eng:
                         eng.activation(abar_sb[buf], dt_ck[:, g, :],
                                        AF.Exp, scale=acol_sb[:, s:s + 1]))(),
                        deps=[st_dts[g], st_a2] +
                             ([sc_p[buf]] if buf in sc_p else []),
                        name=f"{name}_ab{ck}s{s}")
            ab_st[s] = st
            ab_p[g] = st

        _emit_bn(0)
        _emit_ab(0)
        _emit_ab(1)
        _emit_bn(1)
        for n in range(16):
            st_bbc = bbc_st[n]
            nbuf = n % 2
            for g in range(2):
                s = n * 2 + g
                buf = s % 2
                st_ab = ab_st[s]
                st_bu = P.comp("gpsimd",
                               (lambda g=g, buf=buf, nbuf=nbuf: lambda eng:
                                eng.tensor_mul(bu_sb[buf][:, :], u_ck[:, g, :],
                                               bbc_sb[nbuf][:, :]))(),
                               deps=[st_u, st_bbc] +
                                    ([sc_p[buf]] if buf in sc_p else []),
                               name=f"{name}_bu{ck}s{s}")
                st_sc = P.comp("vector",
                               (lambda s=s, buf=buf: lambda eng:
                                eng.tensor_tensor_scan(
                                    h2_sb[:, buf, :], abar_sb[buf],
                                    bu_sb[buf][:, :], hstate[:, s:s + 1],
                                    ALU.mult, ALU.add))(),
                               deps=[st_ab, st_bu, st_h0] +
                                    ([red_p[buf]] if buf in red_p else []) +
                                    ([hc_p[buf]] if buf in hc_p else []),
                               name=f"{name}_sc{ck}s{s}")
                st_hc = P.comp("vector",
                               (lambda s=s, buf=buf: lambda eng:
                                eng.tensor_copy(hstate[:, s:s + 1],
                                                h2_sb[:, buf, T - 1:T]))(),
                               deps=[st_sc], name=f"{name}_hc{ck}s{s}")
                st_red = P.comp("tensor",
                                (lambda s=s, buf=buf, first=(s == 0): lambda eng: [
                                    eng.matmul(y_ps[:, n0:min(n0 + 512, T)],
                                               ones_col,
                                               h2_sb[:, buf, n0:min(n0 + 512, T)],
                                               start=first, stop=(s == 31))
                                    for n0 in range(0, T, 512)][-1])(),
                                deps=[st_sc] +
                                     ([st_yev_prev] if (s == 0 and st_yev_prev) else []),
                                name=f"{name}_red{ck}s{s}")
                sc_p[buf] = st_sc
                hc_p[buf] = st_hc
                red_p[buf] = st_red
                bu_p[buf] = st_bu
                bu_of_bbcbuf[nbuf] = st_bu
            if n + 1 < 16:
                _emit_ab(2 * n + 2)
                _emit_ab(2 * n + 3)
            if n + 2 < 16:
                _emit_bn(n + 2)
        st_yev = P.comp("scalar",
                        lambda eng: eng.copy(yrow_sb[:, :], y_ps[:, :]),
                        deps=[red_p[1]], name=f"{name}_yev{ck}")
        st_yst = store(kb, ar_in.ap()[ck:ck + 1, :], yrow_sb[:, :],
                       deps=[st_yev], name=f"{name}_yst{ck}")
        st_yev_prev = st_yev
        last_yst = st_yst
        prev_ck = {
            "xck_readers": [st_u],
            "dtraw_readers": [st_dtb],
            "dtb_readers": [bbc_prev[0]],
            "uck_readers": [st_u],
        }
    st_ar = P.coll(
        lambda eng: eng.collective_compute(
            "AllReduce", ALU.add, replica_groups=[list(range(R))],
            ins=[ar_in.ap().opt()], outs=[ar_out.ap().opt()]),
        deps=[last_yst], name=f"{name}_ar")
    kb.free(dtb_ps, y_ps)
    kb.free(xck_b, hstate, yrow_sb, h2_sb, bu_sb[1], bu_sb[0], scr2,
            bbc_sb[1], bbc_sb[0], brow_sb[1], brow_sb[0], dtraw_sb)

    yall_sb = kb.sb("mb_yall", [8, T], F32)
    yscal_sb = kb.sb("mb_yscal", [1, T], F32)
    st_ya = load(kb, yall_sb[:, :], ar_out.ap(), deps=[st_ar], name=f"{name}_ya")
    ysel_ps = kb.ps("mb_yselps", [1, T], F32)
    st_ysel = P.comp("tensor",
                     (lambda: lambda eng: [
                         eng.matmul(ysel_ps[:, n0:min(n0 + 512, T)], tmask_sb[:, :],
                                    yall_sb[:, n0:min(n0 + 512, T)],
                                    start=True, stop=True)
                         for n0 in range(0, T, 512)][-1])(),
                     deps=[st_ya, st_w], name=f"{name}_ysel")
    st_ysc = P.comp("vector",
                    lambda eng: (
                        eng.tensor_mul(yscal_sb[:, :], ysel_ps[:, :], csum_sb[:, :]),
                        eng.tensor_single_scalar(yscal_sb[:, :], yscal_sb[:, :],
                                                 1.0 / 2048.0, ALU.mult))[-1],
                    deps=[st_ysel, st_cse], name=f"{name}_ysc")
    kb.free(ysel_ps)
    yb_ps = kb.ps("mb_ybps", [128, T], F32)
    st_yb = P.comp("tensor",
                   (lambda: lambda eng: [
                       eng.matmul(yb_ps[:, n0:min(n0 + 512, T)], ones_row,
                                  yscal_sb[:, n0:min(n0 + 512, T)],
                                  start=True, stop=True)
                       for n0 in range(0, T, 512)][-1])(),
                   deps=[st_ysc], name=f"{name}_yb")

    xs_in = [kb.sb("mb_xsin", [128, T], BF16) for _ in range(2)]
    zz_in = [kb.sb("mb_zzin", [128, T], BF16) for _ in range(2)]
    ts_bf = kb.sb("mb_tsbf", [128, T], BF16)
    y3_stages = []
    prev_ld = {}
    for og in range(16):
        buf = og % 2
        st_lx = load(kb, xs_in[buf][:, :],
                     xs_a2a_in.ap()[og * 128:(og + 1) * 128, :],
                     deps=[st_yb] + prev_ld.get(buf, []), name=f"{name}_c1x{og}")
        st_lz = load(kb, zz_in[buf][:, :],
                     z_stage.ap()[og * 128:(og + 1) * 128, :],
                     deps=[st_yb] + prev_ld.get(buf, []), name=f"{name}_c1z{og}",
                     engine="sync")
        st_m1 = P.comp("vector",
                       (lambda og=og, buf=buf: lambda eng:
                        eng.scalar_tensor_tensor(ts_bf[:, :], xs_in[buf][:, :],
                                                 dcol_sb[:, og:og + 1], yb_ps[:, :],
                                                 ALU.mult, ALU.add))(),
                       deps=[st_lx, st_w] + y3_stages[-1:], name=f"{name}_m1o{og}")
        st_sz = P.comp("scalar",
                       (lambda buf=buf: lambda eng:
                        eng.activation(zz_in[buf][:, :], zz_in[buf][:, :], AF.Silu))(),
                       deps=[st_lz], name=f"{name}_szo{og}")
        st_m2 = P.comp("vector",
                       (lambda buf=buf: lambda eng:
                        eng.tensor_mul(zz_in[buf][:, :], ts_bf[:, :],
                                       zz_in[buf][:, :]))(),
                       deps=[st_m1, st_sz], name=f"{name}_m2o{og}")
        st_y3s = store(kb, y3_stage.ap()[og * 128:(og + 1) * 128, :],
                       zz_in[buf][:, :], deps=[st_m2], name=f"{name}_y3s{og}")
        prev_ld[buf] = [st_m2, st_y3s]
        y3_stages.append(st_y3s)
    kb.free(yb_ps)

    wout_sb2 = kb.sb("mb_wout2", [128, 16, 256], BF16)
    mo_sb = xln
    mo_ps2 = [kb.ps("mb_mops", [128, T], F32) for _ in range(2)]
    mo_evs = []
    prev_pass = []
    for ps_i in range(4):
        mgs = (2 * ps_i, 2 * ps_i + 1)
        st_lw = P.dma("gpsimd",
                      (lambda ps_i=ps_i: lambda eng: [eng.dma_start(
                          out=wout_sb2[:, :, :],
                          in_=W_out[:, ps_i * 256:(ps_i + 1) * 256].rearrange(
                              "(g p) m -> p g m", p=128))])(),
                      deps=y3_stages + prev_pass, n_dma=1,
                      name=f"{name}_ldwo{ps_i}")
        mm_stages = []
        prev_y3ld = {}
        for kg in range(16):
            buf = kg % 2
            st_ly = load(kb, xs_in[buf][:, :],
                         y3_stage.ap()[kg * 128:(kg + 1) * 128, :],
                         deps=y3_stages + prev_y3ld.get(buf, []) + prev_pass,
                         name=f"{name}_c2y{ps_i}k{kg}")
            def emit_mo(eng, kg=kg, buf=buf):
                last = None
                for j in range(2):
                    eng.ldweights(wout_sb2[:, kg, j * 128:(j + 1) * 128])
                    for n0 in range(0, T, 512):
                        last = eng.matmul(mo_ps2[j][:, n0:min(n0 + 512, T)],
                                          wout_sb2[:, kg, j * 128:(j + 1) * 128],
                                          xs_in[buf][:, n0:min(n0 + 512, T)],
                                          start=(kg == 0), stop=(kg == 15))
                return last
            st_mo = P.comp("tensor", emit_mo,
                           deps=[st_ly, st_lw] + prev_pass, name=f"{name}_mo{ps_i}k{kg}")
            mm_stages.append(st_mo)
            prev_y3ld[buf] = [st_mo]
        evs = []
        for j in range(2):
            st_ev = P.comp("scalar",
                           (lambda j=j, mg=mgs[j]: lambda eng:
                            eng.copy(mo_sb[:, mg, :], mo_ps2[j][:, :]))(),
                           deps=[mm_stages[-1]], name=f"{name}_moev{ps_i}_{j}")
            evs.append(st_ev)
            mo_evs.append(st_ev)
        prev_pass = evs
    kb.free(mo_ps2[1], mo_ps2[0])

    kb.free(wout_sb2, ts_bf, zz_in[1], zz_in[0], xs_in[1], xs_in[0],
            yscal_sb, yall_sb)
    mo_tiles = [mo_sb[:, g, :] for g in range(8)]
    ln_fin2 = emit_ln(kb, mo_tiles, None, T, resid_tiles=x_tiles,
                      deps=mo_evs, ones_col=ones_col, ones_row=ones_row,
                      ones_col_bf=ones_col_bf, name=f"{name}_lnmo")
    kb.free(csum_sb, cw_sb, cbb_sb, wx_sb, wdt_sb, bdt_sb, acol_sb, dcol_sb,
            selc_sb, hmask_sb, tmask_sb, xln)
    return ln_fin2



def emit_rwkv(kb, x_tiles, T, WrT, WkT, WvT, mur_cols, muk_cols, halo_mask,
              ones_col, ones_row, ones_col_bf, deps=(), name="rw"):
    P = kb.prog
    R = 8

    mur_sb = kb.sb("rw_mur", [128, 8], F32)
    muk_sb = kb.sb("rw_muk", [128, 8], F32)
    hmask_sb = kb.sb("rw_hmask", [8, 1], F32)
    st_w0 = P.dma("gpsimd",
                  lambda eng: [
                      eng.dma_start(out=mur_sb[:, :], in_=mur_cols),
                      eng.dma_start(out=muk_sb[:, :], in_=muk_cols),
                      eng.dma_start(out=hmask_sb[:, :], in_=halo_mask),
                  ], deps=list(deps), n_dma=3, name=f"{name}_ldc")

    h_ag_in = kb.dram(kb.u("rw_hin"), [1024, 1], F32)
    h_ag_out = kb.dram(kb.u("rw_hout"), [R * 1024, 1], F32)
    h_bounce = kb.dram(kb.u("rw_hb"), [1, 1024], F32)
    hall_sb = kb.sb("rw_hall", [8, 1024], F32)
    hprev_sb = kb.sb("rw_hprev", [128, 8], F32)
    hrow_sb = kb.sb("rw_hrow", [128, 8, 1], F32)
    st_hst = P.dma("gpsimd",
                   lambda eng: [
                       eng.dma_start(
                           out=h_ag_in.ap()[g * 128:(g + 1) * 128, :],
                           in_=x_tiles[g][:, T - 1:T])
                       for g in range(8)],
                   deps=list(deps), n_dma=8, name=f"{name}_hst")
    st_hag = P.coll(
        lambda eng: eng.collective_compute(
            "AllGather", ALU.bypass, replica_groups=[list(range(R))],
            ins=[h_ag_in.ap().opt()], outs=[h_ag_out.ap().opt()]),
        deps=[st_hst], name=f"{name}_hag")
    st_hld = load(kb, hall_sb[:, :],
                  h_ag_out.ap().rearrange("(r q) c -> r (q c)", r=8),
                  deps=[st_hag], name=f"{name}_hld")
    hsel_ps = kb.ps("rw_hselps", [1, 1024], F32)
    st_hsel = P.comp("tensor",
                     (lambda: lambda eng: [
                         eng.matmul(hsel_ps[:, n0:n0 + 512], hmask_sb[:, :],
                                    hall_sb[:, n0:n0 + 512], start=True, stop=True)
                         for n0 in range(0, 1024, 512)][-1])(),
                     deps=[st_hld, st_w0], name=f"{name}_hsel")
    hsel_sb = kb.sb("rw_hselsb", [1, 1024], F32)
    st_hsev = P.comp("vector",
                     lambda eng: eng.tensor_copy(hsel_sb[:, :], hsel_ps[:, :]),
                     deps=[st_hsel], name=f"{name}_hsev")
    st_hbo = store(kb, h_bounce.ap(), hsel_sb[:, :], deps=[st_hsev],
                   name=f"{name}_hbo")
    def emit_hbi(eng):
        with kb.nc.allow_non_contiguous_dma(reason="tiny 4KB halo gather"):
            return [eng.dma_start(
                out=hrow_sb[:, :, :],
                in_=h_bounce.ap().rearrange("a (g p) -> (a p) g ()", p=128))]
    st_hbi = P.dma("gpsimd", emit_hbi, deps=[st_hbo], n_dma=1, name=f"{name}_hbi")
    kb.free(hsel_sb)
    kb.free(hsel_ps)

    hs_ps = kb.ps("rw_hsps", [1, 2], F32)
    hsq_sb = kb.sb("rw_hsq", [128, 8], F32)
    st_hsq = P.comp("scalar",
                    lambda eng: eng.activation(hsq_sb[:, :],
                                               hrow_sb[:, :, :].rearrange("p g c -> p (g c)"),
                                               AF.Square),
                    deps=[st_hbi], name=f"{name}_hsq")

    def emit_hstats(eng):
        last = None
        for g in range(8):
            eng.matmul(hs_ps[:, 0:1], ones_col, hrow_sb[:, g, :],
                       start=(g == 0), stop=(g == 7))
        for g in range(8):
            last = eng.matmul(hs_ps[:, 1:2], ones_col, hsq_sb[:, g:g + 1],
                              start=(g == 0), stop=(g == 7))
        return last
    st_hss = P.comp("tensor", emit_hstats, deps=[st_hbi, st_hsq],
                    name=f"{name}_hss")
    hst_sb = kb.sb("rw_hst", [1, 4], F32)
    st_hm = P.comp("vector",
                   lambda eng: eng.tensor_single_scalar(
                       hst_sb[:, 0:2], hs_ps[:, :], 1.0 / 1024.0, ALU.mult),
                   deps=[st_hss], name=f"{name}_hm")
    st_hv = P.comp("vector",
                   lambda eng: (
                       eng.tensor_mul(hst_sb[:, 2:3], hst_sb[:, 0:1], hst_sb[:, 0:1]),
                       eng.tensor_sub(hst_sb[:, 1:2], hst_sb[:, 1:2], hst_sb[:, 2:3]),
                       eng.tensor_single_scalar(hst_sb[:, 1:2], hst_sb[:, 1:2],
                                                1e-5, ALU.add))[-1],
                   deps=[st_hm], name=f"{name}_hv")
    st_hs2 = P.comp("scalar",
                    lambda eng: eng.activation(hst_sb[:, 1:2], hst_sb[:, 1:2], AF.Sqrt),
                    deps=[st_hv], name=f"{name}_hs2")
    st_hr = P.comp("vector",
                   lambda eng: eng.reciprocal(hst_sb[:, 1:2], hst_sb[:, 1:2]),
                   deps=[st_hs2], name=f"{name}_hr")
    hb_ps = kb.ps("rw_hbps", [128, 2], F32)
    st_hbb = P.comp("tensor",
                    lambda eng: eng.matmul(hb_ps[:, :], ones_row, hst_sb[:, 0:2],
                                           start=True, stop=True),
                    deps=[st_hr], name=f"{name}_hbb")
    hpb_sb = kb.sb("rw_hpb", [128, 2], F32)
    st_hbe = P.comp("vector",
                    lambda eng: eng.tensor_copy(hpb_sb[:, :], hb_ps[:, :]),
                    deps=[st_hbb], name=f"{name}_hbe")
    kb.free(hb_ps, hs_ps)
    st_hln = P.comp("vector",
                    lambda eng: eng.tensor_scalar(
                        hprev_sb[:, :], hrow_sb[:, :, :].rearrange("p g c -> p (g c)"),
                        hpb_sb[:, 0:1], hpb_sb[:, 1:2], ALU.subtract, ALU.mult),
                    deps=[st_hbe], name=f"{name}_hln")

    xr_sb = kb.sb("rw_xr", [128, 8, T], BF16)
    xk_sb = kb.sb("rw_xk", [128, 8, T], BF16)
    dscr = kb.sb("rw_dscr", [128, T], BF16)
    xln4 = kb.sb("rw_xln4", [128, 8, T], BF16)
    xln_tiles = [xln4[:, g, :] for g in range(8)]
    ln_fin = emit_ln(kb, x_tiles, xln_tiles, T, deps=deps,
                     ones_col=ones_col, ones_row=ones_row, name=f"{name}_ln4")
    shift_stages = []
    prev_d = []
    for g in range(8):
        st_d = P.comp("vector",
                      (lambda g=g: lambda eng:
                       eng.tensor_sub(dscr[:, 1:T], xln4[:, g, 1:T],
                                      xln4[:, g, 0:T - 1]))(),
                      deps=ln_fin + prev_d, name=f"{name}_d{g}")
        st_d0 = P.comp("vector",
                       (lambda g=g: lambda eng:
                        eng.tensor_sub(dscr[:, 0:1], xln4[:, g, 0:1],
                                       hprev_sb[:, g:g + 1]))(),
                       deps=[st_d, st_hln], name=f"{name}_d0{g}")
        st_xr = P.comp("vector",
                       (lambda g=g: lambda eng: (
                           eng.scalar_tensor_tensor(
                               xr_sb[:, g, 1:T], dscr[:, 1:T], mur_sb[:, g:g + 1],
                               xln4[:, g, 0:T - 1], ALU.mult, ALU.add),
                           eng.scalar_tensor_tensor(
                               xk_sb[:, g, 1:T], dscr[:, 1:T], muk_sb[:, g:g + 1],
                               xln4[:, g, 0:T - 1], ALU.mult, ALU.add))[-1])(),
                       deps=[st_d0, st_w0], name=f"{name}_xrk{g}")
        st_xkp = st_xr
        st_xr0 = P.comp("vector",
                        (lambda g=g: lambda eng: (
                            eng.scalar_tensor_tensor(
                                xr_sb[:, g, 0:1], dscr[:, 0:1], mur_sb[:, g:g + 1],
                                hprev_sb[:, g:g + 1], ALU.mult, ALU.add),
                            eng.scalar_tensor_tensor(
                                xk_sb[:, g, 0:1], dscr[:, 0:1], muk_sb[:, g:g + 1],
                                hprev_sb[:, g:g + 1], ALU.mult, ALU.add))[-1])(),
                        deps=[st_xr, st_xkp], name=f"{name}_xrk0{g}")
        prev_d = [st_xr0, st_xkp]
        shift_stages.append(st_xr0)
    kb.free(xln4)

    wk_sb = kb.sb("rw_wk", [128, 8, 1024], BF16)
    st_wk = P.dma("gpsimd",
                  lambda eng: [eng.dma_start(
                      out=wk_sb[:, :, :],
                      in_=WkT.rearrange("(g p) m -> p g m", p=128))],
                  deps=shift_stages, n_dma=1, name=f"{name}_ldwk")
    k2_sb = kb.sb("rw_k2", [128, 8, T], BF16)
    kp_ps2 = [kb.ps("rw_kps", [128, T], F32) for _ in range(2)]
    kr_p = {}
    k2_stages = []
    for mg in range(8):
        kb_ = mg % 2

        def emit_k(eng, mg=mg, kb_=kb_):
            last = None
            for kg in range(8):
                eng.ldweights(wk_sb[:, kg, mg * 128:(mg + 1) * 128])
                for n0 in range(0, T, 512):
                    last = eng.matmul(kp_ps2[kb_][:, n0:min(n0 + 512, T)],
                                      wk_sb[:, kg, mg * 128:(mg + 1) * 128],
                                      xk_sb[:, kg, n0:min(n0 + 512, T)],
                                      start=(kg == 0), stop=(kg == 7))
            return last
        st_km = P.comp("tensor", emit_k,
                       deps=shift_stages + [st_wk] +
                            ([kr_p[kb_]] if kb_ in kr_p else []),
                       name=f"{name}_km{mg}")
        st_kr = P.comp("vector",
                       (lambda mg=mg, kb_=kb_: lambda eng:
                        eng.tensor_scalar_max(k2_sb[:, mg, :],
                                              kp_ps2[kb_][:, :], 0.0))(),
                       deps=[st_km], name=f"{name}_kr{mg}")
        st_k2 = P.comp("vector",
                       (lambda mg=mg: lambda eng:
                        eng.tensor_mul(k2_sb[:, mg, :], k2_sb[:, mg, :],
                                       k2_sb[:, mg, :]))(),
                       deps=[st_kr], name=f"{name}_k2{mg}")
        kr_p[kb_] = st_kr
        k2_stages.append(st_k2)
    kb.free(kp_ps2[1], kp_ps2[0])

    rw_sb = xk_sb
    wv_sb = kb.sb("rw_wv", [128, 8, 1024], BF16)
    st_wv = P.dma("gpsimd",
                  lambda eng: [
                      eng.dma_start(out=wk_sb[:, :, :],
                                    in_=WrT.rearrange("(g p) m -> p g m", p=128)),
                      eng.dma_start(out=wv_sb[:, :, :],
                                    in_=WvT.rearrange("(g p) m -> p g m", p=128)),
                  ], deps=k2_stages + [st_km], n_dma=2, name=f"{name}_ldwv")
    wr_sb = wk_sb
    rs_scr = kb.sb("rw_rs", [128, T], BF16)
    r_ps = kb.ps("rw_rps", [128, T], F32)
    v_ps = kb.ps("rw_vps", [128, T], F32)
    prev_rw = []
    prev_sg = []
    rw_stages = []
    for mg in range(8):
        def emit_r(eng, mg=mg):
            last = None
            for kg in range(8):
                eng.ldweights(wr_sb[:, kg, mg * 128:(mg + 1) * 128])
                for n0 in range(0, T, 512):
                    last = eng.matmul(r_ps[:, n0:min(n0 + 512, T)],
                                      wr_sb[:, kg, mg * 128:(mg + 1) * 128],
                                      xr_sb[:, kg, n0:min(n0 + 512, T)],
                                      start=(kg == 0), stop=(kg == 7))
            return last

        def emit_v(eng, mg=mg):
            last = None
            for kg in range(8):
                eng.ldweights(wv_sb[:, kg, mg * 128:(mg + 1) * 128])
                for n0 in range(0, T, 512):
                    last = eng.matmul(v_ps[:, n0:min(n0 + 512, T)],
                                      wv_sb[:, kg, mg * 128:(mg + 1) * 128],
                                      k2_sb[:, kg, n0:min(n0 + 512, T)],
                                      start=(kg == 0), stop=(kg == 7))
            return last
        st_rm = P.comp("tensor", emit_r, deps=shift_stages + [st_wv] + prev_sg,
                       name=f"{name}_rm{mg}")
        st_vm = P.comp("tensor", emit_v, deps=k2_stages + [st_wv] + prev_rw,
                       name=f"{name}_vm{mg}")
        st_sg = P.comp("scalar",
                       (lambda: lambda eng:
                        eng.activation(rs_scr[:, :], r_ps[:, :], AF.Sigmoid))(),
                       deps=[st_rm], name=f"{name}_sg{mg}")
        st_rw = P.comp("vector",
                       (lambda mg=mg: lambda eng:
                        eng.tensor_mul(rw_sb[:, mg, :], rs_scr[:, :], v_ps[:, :]))(),
                       deps=[st_sg, st_vm], name=f"{name}_rw{mg}")
        prev_rw = [st_rw]
        prev_sg = [st_sg]
        rw_stages.append(st_rw)
    kb.free(v_ps, r_ps)

    kb.free(rs_scr, wv_sb, k2_sb, wk_sb, dscr)
    rw_tiles = [rw_sb[:, g, :] for g in range(8)]
    ln_fin2 = emit_ln(kb, rw_tiles, None, T, resid_tiles=x_tiles,
                      deps=rw_stages, ones_col=ones_col, ones_row=ones_row,
                      ones_col_bf=ones_col_bf, name=f"{name}_lnrw")
    kb.free(xr_sb, xk_sb,
            hst_sb, hsq_sb, hpb_sb, hrow_sb, hprev_sb,
            hall_sb, hmask_sb, muk_sb, mur_sb)
    return ln_fin2

"""Driver: FUSED single-launch megakernel."""
import numpy as np
import ml_dtypes
from concourse.bass_utils import run_bass_kernel_spmd

T = 2048
S = 16384


def _build_fused():
    import os
    DBGF = os.environ.get("DBGF")
    kb = KB()
    x = kb.inp("x", [1024, T], F32)
    WqT = kb.inp("WqT", [1024, 128], F32)
    WkT = kb.inp("WkT", [1024, 128], F32)
    WvT = kb.inp("WvT", [1024, 128], F32)
    maskf = kb.inp("mask", [128, 128], F32)
    identf_i = kb.inp("identf", [128, 128], F32)
    WoT = kb.inp("WoT", [1024, 1024], F32)
    gate_wT = kb.inp("gate_wT", [1024, 16], F32)
    gate_b = kb.inp("gate_b", [1, 16], F32)
    W13 = kb.inp("W13", [18, 1024, 1024], F32)
    W2 = kb.inp("W2", [18, 512, 1024], F32)
    b13 = kb.inp("b13", [128, 8, 18], F32)
    be2 = kb.inp("be2", [16, 1024], F32)
    bs2 = kb.inp("bs2", [1, 1024], F32)
    sel18 = kb.inp("sel18", [18, 18 * 128], F32)
    W_in = kb.inp("W_in", [1024, 4096], F32)
    cw = kb.inp("cw", [128, 16, 4], F32)
    cb = kb.inp("cb", [128, 16], F32)
    W_x = kb.inp("W_x", [2048, 33], F32)
    wdt = kb.inp("wdt", [128, 2], F32)
    bdt = kb.inp("bdt", [128, 2], F32)
    alog = kb.inp("alog", [128, 32], F32)
    dcol = kb.inp("dcol", [128, 16], F32)
    W_out = kb.inp("W_out", [2048, 1024], F32)
    selC = kb.inp("selC", [33, 1], F32)
    hmask = kb.inp("hmask", [8, 1], F32)
    tmask = kb.inp("tmask", [8, 1], F32)
    WrT = kb.inp("WrT", [1024, 1024], F32)
    WkT2 = kb.inp("WkT2", [1024, 1024], F32)
    WvT2 = kb.inp("WvT2", [1024, 1024], F32)
    mur = kb.inp("mur", [128, 8], F32)
    muk = kb.inp("muk", [128, 8], F32)
    y = kb.out("y", [1024, T], F32)

    x_sb = kb.sb("x", [128, 8, T], F32)
    ones_col, ones_row, ones_col_bf, st_c = make_consts(kb)
    identf = kb.sb("identf", [128, 128], F32)
    mask_sb = kb.sb("mask", [128, 128], F32)
    ident_bf = kb.sb("identbf", [128, 128], BF16)
    ones_bf = kb.sb("onesbf", [128, 128], BF16)
    st_ob = kb.prog.comp("gpsimd", lambda eng: eng.memset(ones_bf[:, :], 1.0),
                         name="onesbf")
    st_ld = load(kb, x_sb[:, :, :], x.rearrange("(g p) t -> p g t", p=128),
                 name="ld_x")
    st_m = load(kb, mask_sb[:, :], maskf, name="ld_mask")
    st_i = load(kb, identf[:, :], identf_i, name="ld_i")
    st_ib = kb.prog.dma("gpsimd",
                        lambda eng: [eng.dma_start(out=ident_bf[:, :], in_=identf_i)],
                        n_dma=1, name="ld_identbf")
    x_tiles = [x_sb[:, g, :] for g in range(8)]

    out_attn = kb.sb("oattn", [128, S], BF16)
    at_fin = emit_attn(kb, x_tiles, T, S, WqT, WkT, WvT,
                       mask_sb[:, :], ident_bf[:, :],
                       ones_col, ones_row, ones_bf[:, :], out_attn[:, :],
                       deps=[st_ld, st_c, st_m, st_i, st_ib, st_ob], name="at")
    at_a2a_in = kb.dram(kb.u("at_a2ain"), [8 * 128, T], BF16)
    at_a2a_out = kb.dram(kb.u("at_a2aout"), [8 * 128, T], BF16)
    st_ast = kb.prog.dma("gpsimd",
                         lambda eng: [
                             eng.dma_start(out=at_a2a_in.ap()[j * 128:(j + 1) * 128, :],
                                           in_=out_attn[:, j * T:(j + 1) * T])
                             for j in range(8)],
                         deps=at_fin, n_dma=8, name="at_a2ast")

    def emit_a2a(eng):
        from concourse.aot_env import allow_ici_all_to_all
        with allow_ici_all_to_all():
            return eng.collective_compute(
                "AllToAll", ALU.bypass, replica_groups=[list(range(8))],
                ins=[at_a2a_in.ap().opt()], outs=[at_a2a_out.ap().opt()])
    st_a2a = kb.prog.coll(emit_a2a, deps=[st_ast], name="at_a2a")
    kb.free(out_attn, ones_bf, ident_bf, mask_sb)

    if DBGF:
        d_attn = kb.out("d_attn", [1024, T], BF16)
        st_da = kb.prog.dma("gpsimd",
                            lambda eng: [eng.dma_start(out=d_attn, in_=at_a2a_out.ap())],
                            deps=[st_a2a], n_dma=1, name="d_attn")
    lr_fin = emit_l0rest(kb, x_tiles, T, at_a2a_out.ap(), WoT, gate_wT, gate_b,
                         W13, W2, b13, be2, bs2, sel18, None, identf[:, :],
                         ones_col, ones_row, ones_col_bf,
                         deps=[st_a2a], name="lr")

    if DBGF:
        d_lr = kb.out("d_lr", [1024, T], F32)
        for g in range(8):
            store(kb, d_lr[g * 128:(g + 1) * 128, :], x_tiles[g], deps=lr_fin,
                  name=f"d_lr{g}")
    mb_fin = emit_mamba(kb, x_tiles, T, S, W_in, cw, cb, W_x, wdt, bdt, alog,
                        dcol, W_out, selC, hmask, tmask, ones_col, ones_row,
                        ones_col_bf=ones_col_bf, deps=lr_fin, name="mb")

    if DBGF:
        d_mb = kb.out("d_mb", [1024, T], F32)
        for g in range(8):
            store(kb, d_mb[g * 128:(g + 1) * 128, :], x_tiles[g], deps=mb_fin,
                  name=f"d_mb{g}")
    rw_fin = emit_rwkv(kb, x_tiles, T, WrT, WkT2, WvT2, mur, muk, hmask,
                       ones_col, ones_row, ones_col_bf, deps=mb_fin, name="rw")

    store(kb, y.rearrange("(g p) t -> p g t", p=128), x_sb[:, :, :],
          deps=rw_fin, name="st_y")
    kb.finish()
    return kb


def _prep_inputs(inp):
    x_in = np.ascontiguousarray(np.asarray(inp["L0_attn_x"])[0])
    Wq = np.asarray(inp["L0_attn_Wq"]); Wk = np.asarray(inp["L0_attn_Wk"])
    Wv = np.asarray(inp["L0_attn_Wv"]); Wo = np.asarray(inp["L0_attn_Wo"])
    mask = np.triu(np.ones((128, 128), np.float32))
    ident = np.eye(128, dtype=np.float32)
    gate_w = np.asarray(inp["L0_ffn_gate_w"])
    gate_b = np.asarray(inp["L0_ffn_gate_b"]).reshape(1, 16)
    We1 = np.asarray(inp["L0_ffn_We1"]); We2 = np.asarray(inp["L0_ffn_We2"])
    We3 = np.asarray(inp["L0_ffn_We3"])
    be1 = np.asarray(inp["L0_ffn_be1"]); be2 = np.asarray(inp["L0_ffn_be2"])
    be3 = np.asarray(inp["L0_ffn_be3"])
    ws1 = np.asarray(inp["L0_ffn_ws1"]); ws2 = np.asarray(inp["L0_ffn_ws2"])
    ws3 = np.asarray(inp["L0_ffn_ws3"])
    bs1 = np.asarray(inp["L0_ffn_bs1"])
    bs2 = np.asarray(inp["L0_ffn_bs2"]).reshape(1, 1024)
    bs3 = np.asarray(inp["L0_ffn_bs3"])
    W13 = np.zeros((18, 1024, 1024), np.float32)
    W2a = np.zeros((18, 512, 1024), np.float32)
    b13 = np.zeros((128, 8, 18), np.float32)
    for e in range(16):
        W13[e] = np.concatenate([We1[e], We3[e]], axis=1)
        W2a[e] = We2[e]
        b13[:, 0:4, e] = be1[e].reshape(4, 128).T
        b13[:, 4:8, e] = be3[e].reshape(4, 128).T
    for i, e in enumerate((16, 17)):
        hsl = slice(i * 512, (i + 1) * 512)
        W13[e] = np.concatenate([ws1[:, hsl], ws3[:, hsl]], axis=1)
        W2a[e] = ws2[hsl, :]
        b13[:, 0:4, e] = bs1[hsl].reshape(4, 128).T
        b13[:, 4:8, e] = bs3[hsl].reshape(4, 128).T
    sel18 = np.zeros((18, 18 * 128), np.float32)
    for e in range(18):
        sel18[e, e * 128:(e + 1) * 128] = 1.0
    W_in = np.asarray(inp["L1_attn_W_in"])
    conv_w = np.asarray(inp["L1_attn_conv_w"])
    conv_b = np.asarray(inp["L1_attn_conv_b"])
    W_x = np.asarray(inp["L1_attn_W_x"])
    W_dt = np.asarray(inp["L1_attn_W_dt"])
    b_dt = np.asarray(inp["L1_attn_b_dt"])
    A_log = np.asarray(inp["L1_attn_A_log"])
    D_par = np.asarray(inp["L1_attn_D_param"])
    W_out = np.asarray(inp["L1_attn_W_out"])
    selC = np.zeros((33, 1), np.float32)
    selC[16:32] = 1.0
    cw_cols = np.zeros((128, 16, 4), np.float32)
    cb_cols = np.zeros((128, 16), np.float32)
    d_cols = np.zeros((128, 16), np.float32)
    for og in range(16):
        cw_cols[:, og, :] = conv_w[og * 128:(og + 1) * 128, :]
        cb_cols[:, og] = conv_b[og * 128:(og + 1) * 128]
        d_cols[:, og] = D_par[og * 128:(og + 1) * 128]
    Wr = np.asarray(inp["L1_ffn_W_r"]); Wkk = np.asarray(inp["L1_ffn_W_k"])
    Wvv = np.asarray(inp["L1_ffn_W_v"])
    mur_c = np.asarray(inp["L1_ffn_mu_r"]).reshape(8, 128).T.copy()
    muk_c = np.asarray(inp["L1_ffn_mu_k"]).reshape(8, 128).T.copy()

    in_maps = []
    for c in range(8):
        sl = slice(c * 128, (c + 1) * 128)
        wdt_c = np.zeros((128, 2), np.float32)
        bdt_c = np.zeros((128, 2), np.float32)
        alog_c = np.zeros((128, 32), np.float32)
        for g in range(2):
            dd = slice(c * 256 + g * 128, c * 256 + (g + 1) * 128)
            wdt_c[:, g] = W_dt[0, dd]
            bdt_c[:, g] = b_dt[dd]
            for n in range(16):
                alog_c[:, n * 2 + g] = A_log[dd, n]
        hmask = np.zeros((8, 1), np.float32)
        if c > 0:
            hmask[c - 1, 0] = 1.0
        tmask = np.zeros((8, 1), np.float32)
        tmask[c, 0] = 1.0
        in_maps.append({
            "x": np.ascontiguousarray(x_in[c * T:(c + 1) * T].T),
            "WqT": np.ascontiguousarray(Wq.T[:, sl]),
            "WkT": np.ascontiguousarray(Wk.T[:, sl]),
            "WvT": np.ascontiguousarray(Wv.T[:, sl]),
            "mask": mask, "identf": ident,
            "WoT": np.ascontiguousarray(Wo.T),
            "gate_wT": np.ascontiguousarray(gate_w.T),
            "gate_b": gate_b, "W13": W13, "W2": W2a, "b13": b13,
            "be2": be2, "bs2": bs2, "sel18": sel18,
            "W_in": W_in, "cw": cw_cols, "cb": cb_cols, "W_x": W_x,
            "wdt": wdt_c, "bdt": bdt_c, "alog": alog_c, "dcol": d_cols,
            "W_out": W_out, "selC": selC, "hmask": hmask, "tmask": tmask,
            "WrT": np.ascontiguousarray(Wr.T),
            "WkT2": np.ascontiguousarray(Wkk.T),
            "WvT2": np.ascontiguousarray(Wvv.T),
            "mur": mur_c, "muk": muk_c,
        })
    return in_maps


def kernel(**inp):
    kb = _build_fused()
    in_maps = _prep_inputs(inp)
    res = run_bass_kernel_spmd(kb.nc, in_maps, core_ids=list(range(8)))
    out = np.concatenate([res.results[c]["y"].T for c in range(8)], axis=0)
    return out[None].astype(np.float32)


# version 12
# speedup vs baseline: 1.2837x; 1.0011x over previous
import numpy as np
from contextlib import ExitStack

import concourse.bass as bass
import concourse.mybir as mybir

F32 = mybir.dt.float32
BF16 = mybir.dt.bfloat16
U32 = mybir.dt.uint32
AF = mybir.ActivationFunctionType
ALU = mybir.AluOpType
AX = mybir.AxisListType

ENGINES = ("sync", "tensor", "vector", "scalar", "gpsimd")

DMA_ENGINES = ("sync", "gpsimd", "scalar", "vector")


class Stage:
    __slots__ = ("idx", "engine", "emit", "deps", "n_inc", "kind", "sem", "val",
                 "name")

    def __init__(self, idx, engine, emit, deps, n_inc, kind, name):
        self.idx = idx
        self.engine = engine
        self.emit = emit
        self.deps = deps
        self.n_inc = n_inc
        self.kind = kind
        self.sem = None
        self.val = None
        self.name = name


class Prog:
    def __init__(self):
        self.stages = []

    def comp(self, engine, emit, deps=(), name=""):
        s = Stage(len(self.stages), engine, emit, tuple(deps), 1, "compute", name)
        self.stages.append(s)
        return s

    def dma(self, engine, emit, deps=(), n_dma=1, name=""):
        s = Stage(len(self.stages), engine, emit, tuple(deps), 16 * n_dma, "dma", name)
        self.stages.append(s)
        return s

    def coll(self, emit, deps=(), name=""):
        s = Stage(len(self.stages), "gpsimd", emit, tuple(deps), 1, "collective", name)
        self.stages.append(s)
        return s

    def build(self, nc, ctx, n_dma_sems=8):
        csem = {e: ctx.enter_context(nc.semaphore(f"c_{e}")) for e in ENGINES}
        dsem = {e: [ctx.enter_context(nc.semaphore(f"d_{e}{i}"))
                    for i in range(n_dma_sems)] for e in DMA_ENGINES}
        run_c = {e: 0 for e in ENGINES}
        run_d = {e: [0] * n_dma_sems for e in DMA_ENGINES}
        rot = {e: 0 for e in DMA_ENGINES}
        coll_sems = []
        for s in self.stages:
            for d in s.deps:
                assert d.idx < s.idx, f"dep ordering: {d.name} -> {s.name}"
            if s.kind == "compute":
                s.sem = csem[s.engine]
                run_c[s.engine] += s.n_inc
                s.val = run_c[s.engine]
            elif s.kind == "dma":
                i = rot[s.engine]
                rot[s.engine] = (i + 1) % n_dma_sems
                s.sem = dsem[s.engine][i]
                run_d[s.engine][i] += s.n_inc
                s.val = run_d[s.engine][i]
            else:
                h = ctx.enter_context(nc.semaphore(f"cc_{len(coll_sems)}"))
                coll_sems.append(h)
                s.sem = h
                s.val = 1

        block = ctx.enter_context(nc.Block())

        def engine_body(ename):
            def body(eng):
                for s in self.stages:
                    if s.engine != ename:
                        continue
                    seen = set()
                    for d in s.deps:
                        key = (id(d.sem), d.val)
                        if key in seen:
                            continue
                        seen.add(key)
                        eng.wait_ge(d.sem, d.val)
                    if s.kind == "compute":
                        last = s.emit(eng)
                        assert last is not None, f"stage {s.name} returned None"
                        last.then_inc(s.sem, 1)
                    elif s.kind == "dma":
                        instrs = s.emit(eng)
                        assert len(instrs) * 16 == s.n_inc, \
                            f"stage {s.name}: {len(instrs)} DMAs != declared {s.n_inc//16}"
                        for i in instrs:
                            i.then_inc(s.sem, 16)
                    else:
                        instr = s.emit(eng)
                        instr.then_inc(s.sem)
            return body

        used = {s.engine for s in self.stages}
        if "sync" in used:
            block.sync(engine_body("sync"))
        if "tensor" in used:
            block.tensor(engine_body("tensor"))
        if "vector" in used:
            block.vector(engine_body("vector"))
        if "scalar" in used:
            block.scalar(engine_body("scalar"))
        if "gpsimd" in used:
            block.gpsimd(engine_body("gpsimd"))


class KB:

    def __init__(self, name="k"):
        self.nc = bass.Bass("TRN2", target_bir_lowering=False)
        self.ctx = ExitStack()
        self.prog = Prog()
        self.inputs = {}
        self.outputs = {}
        self._uniq = 0
        self._cms = {}
        self._alloc_seq = {}

    def u(self, base):
        self._uniq += 1
        return f"{base}_{self._uniq}"

    def inp(self, name, shape, dtype=F32):
        ap = self.nc.dram_tensor(name, list(shape), dtype, kind="ExternalInput").ap()
        self.inputs[name] = ap
        return ap

    def out(self, name, shape, dtype=F32):
        ap = self.nc.dram_tensor(name, list(shape), dtype, kind="ExternalOutput").ap()
        self.outputs[name] = ap
        return ap

    def dram(self, name, shape, dtype=F32):
        return self.nc.dram_tensor(name, list(shape), dtype)

    def sb(self, name, shape, dtype=F32):
        cm = self.nc.sbuf_tensor(self.u(name), list(shape), dtype)
        h = cm.__enter__()
        self._cms[id(h)] = cm
        self._alloc_seq[id(h)] = len(self._alloc_seq)
        return h

    def ps(self, name, shape, dtype=F32):
        cm = self.nc.psum_tensor(self.u(name), list(shape), dtype)
        h = cm.__enter__()
        self._cms[id(h)] = cm
        self._alloc_seq[id(h)] = len(self._alloc_seq)
        return h

    def free(self, *handles):
        hs = sorted(handles, key=lambda h: -self._alloc_seq[id(h)])
        for h in hs:
            cm = self._cms.pop(id(h))
            cm.__exit__(None, None, None)

    def psum_scope(self):
        return ExitStack()

    def finish(self):
        self.prog.build(self.nc, self.ctx)
        for hid in sorted(self._cms, key=lambda i: -self._alloc_seq[i]):
            self._cms[hid].__exit__(None, None, None)
        self._cms.clear()
        self.ctx.close()
        return self.nc



def load(kb, dst_sb, src_ap, deps=(), name="load", engine="sync"):
    def emit(eng):
        return [eng.dma_start(out=dst_sb, in_=src_ap)]
    return kb.prog.dma(engine, emit, deps=deps, n_dma=1, name=name)


def store(kb, dst_ap, src_sb, deps=(), name="store", engine="gpsimd"):
    def emit(eng):
        return [eng.dma_start(out=dst_ap, in_=src_sb)]
    return kb.prog.dma(engine, emit, deps=deps, n_dma=1, name=name)


def matmul_fm(kb, out_ps, lhsT_sb_tiles, rhs_sb_tiles, deps=(), name="mm",
              n_chunk=512, start=True, stop=True, extra=()):
    def emit(eng):
        last = None
        K = len(lhsT_sb_tiles)
        N = rhs_sb_tiles[0].shape[-1]
        for ki in range(K):
            lhsT = lhsT_sb_tiles[ki]
            rhs = rhs_sb_tiles[ki]
            bf = lhsT.dtype != F32
            if bf:
                eng.ldweights(lhsT)
            for n0 in range(0, N, n_chunk):
                n1 = min(N, n0 + n_chunk)
                last = eng.matmul(
                    out_ps[:, n0:n1], lhsT, rhs[:, n0:n1],
                    start=(start and ki == 0), stop=(stop and ki == K - 1))
        return last
    return kb.prog.comp("tensor", emit, deps=deps, name=name)


def emit_ln(kb, x_tiles, out_tiles, T, resid_tiles=None, eps=1e-5, deps=(),
            ones_col=None, ones_row=None, ones_col_bf=None, name="ln"):
    nG = len(x_tiles)
    D = nG * 128
    sx_ps = kb.ps("ln_sx", [1, T], F32)
    sq_ps = kb.ps("ln_sq", [1, T], F32)
    sqg_t = kb.sb("ln_sqg", [128, T], F32)
    sqg = [sqg_t, sqg_t]
    mean_row = kb.sb("ln_mean", [1, T], F32)
    work_row = kb.sb("ln_work", [1, T], F32)
    msq_row = work_row
    var_row = work_row
    srow = work_row
    rstd_row = work_row
    sq_stages = [None, None]
    mmx_stages = []
    mmq_stages = []
    for g in range(nG):
        buf = sqg[g % 2]
        sq_deps = list(deps)
        if g >= 1:
            sq_deps.append(mmq_stages[g - 1])
        st_sq = kb.prog.comp(
            "scalar",
            (lambda b=buf, xg=x_tiles[g]: lambda eng:
                eng.activation(b[:, :], xg, AF.Square))(),
            deps=sq_deps, name=f"{name}_sq{g}")
        sq_stages[g % 2] = st_sq
        oc = ones_col_bf if (x_tiles[g].dtype == BF16 and ones_col_bf is not None) else ones_col
        st_mx = matmul_fm(kb, sx_ps[:, :], [oc], [x_tiles[g]],
                          deps=list(deps), name=f"{name}_mmx{g}",
                          start=(g == 0), stop=(g == nG - 1))
        st_mq = matmul_fm(kb, sq_ps[:, :], [ones_col], [buf[:, :]],
                          deps=[st_sq], name=f"{name}_mmq{g}",
                          start=(g == 0), stop=(g == nG - 1))
        mmx_stages.append(st_mx)
        mmq_stages.append(st_mq)

    st_mean = kb.prog.comp(
        "vector",
        lambda eng: eng.tensor_single_scalar(mean_row[:, :], sx_ps[:, :], 1.0 / D, ALU.mult),
        deps=[mmx_stages[-1]], name=f"{name}_mean")
    st_msq = kb.prog.comp(
        "vector",
        lambda eng: eng.tensor_mul(msq_row[:, :], mean_row[:, :], mean_row[:, :]),
        deps=[st_mean], name=f"{name}_msq")
    st_var = kb.prog.comp(
        "vector",
        lambda eng: eng.scalar_tensor_tensor(
            var_row[:, :], sq_ps[:, :], 1.0 / D, msq_row[:, :], ALU.mult, ALU.subtract),
        deps=[st_msq, mmq_stages[-1]], name=f"{name}_var")
    st_ve = kb.prog.comp(
        "vector",
        lambda eng: eng.tensor_single_scalar(var_row[:, :], var_row[:, :], eps, ALU.add),
        deps=[st_var], name=f"{name}_vareps")
    st_s = kb.prog.comp(
        "scalar",
        lambda eng: eng.activation(srow[:, :], var_row[:, :], AF.Sqrt),
        deps=[st_ve], name=f"{name}_sqrt")
    st_r = kb.prog.comp(
        "vector",
        lambda eng: eng.reciprocal(rstd_row[:, :], srow[:, :]),
        deps=[st_s], name=f"{name}_recip")
    kb.free(sx_ps, sq_ps)
    mean_b = kb.ps("ln_meanb", [128, T], F32)
    rstd_b = kb.ps("ln_rstdb", [128, T], F32)
    st_bm = matmul_fm(kb, mean_b[:, :], [ones_row], [mean_row[:, :]],
                      deps=[st_mean, st_var], name=f"{name}_bm")
    st_br = matmul_fm(kb, rstd_b[:, :], [ones_row], [rstd_row[:, :]],
                      deps=[st_r], name=f"{name}_br")

    fin = []
    tmp = kb.sb("ln_t", [128, T], F32 if resid_tiles is not None else out_tiles[0].dtype)
    if resid_tiles is None:
        for g in range(nG):
            war = [fin[-1]] if fin else []
            st_t = kb.prog.comp(
                "vector",
                (lambda tg=tmp, xg=x_tiles[g]: lambda eng:
                    eng.tensor_sub(tg[:, :], xg, mean_b[:, :]))(),
                deps=[st_bm] + list(deps) + war, name=f"{name}_sub{g}")
            st_o = kb.prog.comp(
                "vector",
                (lambda og=out_tiles[g], tg=tmp: lambda eng:
                    eng.tensor_mul(og, tg[:, :], rstd_b[:, :]))(),
                deps=[st_t, st_br], name=f"{name}_mul{g}")
            fin.append(st_o)
    else:
        tmp_bufs = [tmp, sqg_t]
        adds = {}
        for g in range(nG):
            tb = tmp_bufs[g % 2]
            war = ([adds[g - 2]] if g - 2 in adds else [])
            st_t = kb.prog.comp(
                "vector",
                (lambda tg=tb, xg=x_tiles[g]: lambda eng:
                    eng.tensor_sub(tg[:, :], xg, mean_b[:, :]))(),
                deps=[st_bm] + list(deps) + war +
                     ([mmq_stages[-1]] if g == 1 else []),
                name=f"{name}_sub{g}")
            st_u = kb.prog.comp(
                "vector",
                (lambda tg=tb: lambda eng:
                    eng.tensor_mul(tg[:, :], tg[:, :], rstd_b[:, :]))(),
                deps=[st_t, st_br], name=f"{name}_mul{g}")
            st_a = kb.prog.comp(
                "gpsimd",
                (lambda rg=resid_tiles[g], tg=tb: lambda eng:
                    eng.tensor_add(rg, rg, tg[:, :]))(),
                deps=[st_u], name=f"{name}_add{g}")
            adds[g] = st_a
            fin.append(st_a)
    kb.free(tmp, mean_b, rstd_b, sqg_t, mean_row, work_row)
    return fin


def make_consts(kb):
    ones_col = kb.sb("ones_col", [128, 1], F32)
    ones_row = kb.sb("ones_row", [1, 128], F32)
    ones_col_bf = kb.sb("ones_col_bf", [128, 1], BF16)
    ones_row_bf = kb.sb("ones_row_bf", [1, 128], BF16)

    def emit(eng):
        eng.memset(ones_col[:, :], 1.0)
        eng.memset(ones_col_bf[:, :], 1.0)
        eng.memset(ones_row_bf[:, :], 1.0)
        return eng.memset(ones_row[:, :], 1.0)
    st = kb.prog.comp("gpsimd", emit, name="consts")
    kb.consts = {"ones_row_bf": ones_row_bf[:, :]}
    return ones_col[:, :], ones_row[:, :], ones_col_bf[:, :], st



def emit_attn(kb, x_tiles, T, S, WqT, WkT, WvT, mask_tri, ident_bf,
              ones_col, ones_row, ones_bf, out_attn, deps=(), name="attn",
              xln_out_tiles=None):
    P = kb.prog
    EPS = 1e-8
    import os
    phase = os.environ.get("ATTN_PHASE", "full")

    xln = kb.sb("at_xln", [128, 8, T], BF16)
    xln_tiles = [xln[:, g, :] for g in range(8)]
    ln_fin = emit_ln(kb, x_tiles, xln_tiles, T, deps=deps,
                     ones_col=ones_col, ones_row=ones_row, name=f"{name}_ln1")

    ag_in = kb.dram(kb.u("at_agin"), [1024, T], BF16)
    ag_out = kb.dram(kb.u("at_agout"), [8 * 1024, T], BF16)
    st_stage = store(kb, ag_in.ap().rearrange("(g p) t -> p g t", p=128),
                     xln[:, :, :], deps=ln_fin, name=f"{name}_stage")
    st_ag = P.coll(
        lambda eng: eng.collective_compute(
            "AllGather", ALU.bypass, replica_groups=[list(range(8))],
            ins=[ag_in.ap().opt()], outs=[ag_out.ap().opt()]),
        deps=[st_stage], name=f"{name}_ag")

    wq = kb.sb("at_wq", [128, 8, 128], BF16)
    wk = kb.sb("at_wk", [128, 8, 128], BF16)
    wv = kb.sb("at_wv", [128, 8, 128], BF16)
    st_w = P.dma(
        "gpsimd",
        lambda eng: [
            eng.dma_start(out=wq[:, :, :], in_=WqT.rearrange("(g p) m -> p g m", p=128)),
            eng.dma_start(out=wk[:, :, :], in_=WkT.rearrange("(g p) m -> p g m", p=128)),
            eng.dma_start(out=wv[:, :, :], in_=WvT.rearrange("(g p) m -> p g m", p=128)),
        ], n_dma=3, name=f"{name}_ldw")

    KV = kb.sb("at_kv", [128, 192], F32)
    KVbf = kb.sb("at_kvbf", [128, 192], BF16)
    st_kv0 = P.comp("gpsimd", lambda eng: eng.memset(KV[:, :], 0.0),
                    name=f"{name}_kv0")

    xblk = kb.sb("at_xblk", [128, 8, T], BF16)
    qf = kb.sb("at_q", [128, T], BF16)
    kf = kb.sb("at_k", [128, T], BF16)
    vf = kb.sb("at_v", [128, T], BF16)
    e_scr = kb.sb("at_escr", [128, T], F32)
    p_scr = kb.sb("at_pscr", [128, T], F32)
    vtm = [kb.sb("at_vtm", [128, 192], BF16) for _ in range(2)]
    ktm = [kb.sb("at_ktm", [128, 128], BF16) for _ in range(2)]
    ssb = [kb.sb("at_ssb", [128, 128], BF16) for _ in range(2)]
    rec_sb = kb.sb("at_rec", [128, 128], F32)

    if os.environ.get("ATTN_MEMSET", "1") == "1":
        st_ones_v = P.comp(
            "gpsimd",
            lambda eng: (eng.memset(vtm[0][:, 128:192], 1.0),
                         eng.memset(vtm[1][:, 128:192], 1.0),
                         eng.memset(out_attn[:, :], 0.0))[-1],
            name=f"{name}_vones")
    else:
        st_ones_v = P.comp(
            "gpsimd",
            lambda eng: (eng.memset(vtm[0][:, 128:192], 1.0),
                         eng.memset(vtm[1][:, 128:192], 1.0))[-1],
            name=f"{name}_vones")

    tp_k_t = kb.ps("at_tpk", [128, 128], BF16)
    tp_v_t = kb.ps("at_tpv", [128, 128], BF16)
    tp_k = tp_k_t[:, :]
    tp_v = tp_v_t[:, :]
    PW = T if os.environ.get("PROJ_FULL") else min(T, 512)
    pps = kb.ps("at_pp", [128, PW], F32)
    if phase != "proj":
        num_ps_t = kb.ps("at_num", [128, 128], F32)
        den_ps_t = kb.ps("at_den", [128, 128], F32)
        stp_t = kb.ps("at_stp", [128, 128], F32)
        stp_t2 = kb.ps("at_stp2", [128, 128], F32)
        kvd_t = kb.ps("at_kvd", [128, 192], F32)
        num_ps = num_ps_t[:, :]
        den_ps = den_ps_t[:, :]
        stp2 = [stp_t[:, :], stp_t2[:, :]]
        kvd = kvd_t[:, :]

    n_blocks = int(os.environ.get("ATTN_BLOCKS", S // T))
    n_chunks_per_blk = T // 128
    last_stages = []
    prev_blk_tail = []
    prev = None

    for r in range(n_blocks):
        st_x = load(kb, xblk[:, :, :],
                    ag_out.ap()[r * 1024:(r + 1) * 1024, :].rearrange(
                        "(g p) t -> p g t", p=128),
                    deps=[st_ag] + prev_blk_tail, name=f"{name}_x{r}")
        proj = {}
        prev_pps_readers = []
        prev_elu_done = []
        W = PW
        for wname, wsb, osb, do_elu in (("q", wq, qf, True), ("k", wk, kf, True),
                                        ("v", wv, vf, False)):
            st_os = []
            for hf in range(T // W):
                hsl2 = slice(hf * W, (hf + 1) * W)
                st_mm = matmul_fm(kb, pps[:, :],
                                  [wsb[:, g, :] for g in range(8)],
                                  [xblk[:, g, hsl2] for g in range(8)],
                                  deps=[st_x, st_w] + prev_pps_readers,
                                  name=f"{name}_{wname}mm{r}_{hf}")
                if do_elu:
                    st_min = P.comp("vector",
                                    (lambda h2=hsl2: lambda eng: eng.tensor_scalar_min(
                                        e_scr[:, h2], pps[:, :], 0.0))(),
                                    deps=[st_mm] + prev_elu_done,
                                    name=f"{name}_{wname}min{r}_{hf}")
                    st_exp = P.comp("scalar",
                                    (lambda h2=hsl2: lambda eng: eng.activation(
                                        e_scr[:, h2], e_scr[:, h2], AF.Exp))(),
                                    deps=[st_min], name=f"{name}_{wname}exp{r}_{hf}")
                    st_max = P.comp("vector",
                                    (lambda h2=hsl2: lambda eng: eng.tensor_scalar_max(
                                        p_scr[:, h2], pps[:, :], 0.0))(),
                                    deps=[st_mm] + prev_elu_done,
                                    name=f"{name}_{wname}max{r}_{hf}")
                    st_o = P.comp("vector",
                                  (lambda o=osb, h2=hsl2: lambda eng: eng.tensor_add(
                                      o[:, h2], e_scr[:, h2], p_scr[:, h2]))(),
                                  deps=[st_exp, st_max],
                                  name=f"{name}_{wname}elu{r}_{hf}")
                    prev_pps_readers = [st_min, st_max]
                    prev_elu_done = [st_o]
                else:
                    st_o = P.comp("scalar",
                                  (lambda o=osb, h2=hsl2: lambda eng: eng.copy(
                                      o[:, h2], pps[:, :]))(),
                                  deps=[st_mm], name=f"{name}_vcp{r}_{hf}")
                    prev_pps_readers = [st_o]
                st_os.append(st_o)
            proj[wname] = st_os[-1]

        if phase == "proj":
            st_cp = P.comp("vector",
                           (lambda r=r: lambda eng: eng.tensor_copy(
                               out_attn[:, r * T:(r + 1) * T], qf[:, :]))(),
                           deps=[proj["q"], proj["k"], proj["v"]],
                           name=f"{name}_projcp{r}")
            prev_blk_tail = [st_cp]
            last_stages = [st_cp]
            continue
        blk_readers = []
        for c in range(n_chunks_per_blk):
            gi = r * n_chunks_per_blk + c
            sl = slice(c * 128, (c + 1) * 128)
            gsl = slice(gi * 128, (gi + 1) * 128)
            buf = gi % 2

            dep_tp = [proj["k"], proj["v"]]
            if prev is not None:
                dep_tp = dep_tp + [prev["ek"], prev["ev"], prev["kv_add"]]
                if prev["prev2_tm_readers"]:
                    dep_tp = dep_tp + prev["prev2_tm_readers"]
            st_tk = P.comp("tensor",
                           (lambda s=sl: lambda eng:
                            eng.transpose(tp_k[:, :], kf[:, s], ident_bf))(),
                           deps=dep_tp, name=f"{name}_tk{gi}")
            st_tv = P.comp("tensor",
                           (lambda s=sl: lambda eng:
                            eng.transpose(tp_v[:, :], vf[:, s], ident_bf))(),
                           deps=dep_tp, name=f"{name}_tv{gi}")
            if phase == "tmo":
                prev = {"kv_add": st_tv, "kvbf_readers": [st_tk, st_tv],
                        "tm_readers": [st_tk, st_tv], "prev2_tm_readers":
                        (prev["tm_readers"] if prev is not None else []),
                        "ssb_readers": [st_tk, st_tv], "msks": [st_tk, st_tv],
                        "out_stages": [st_tk, st_tv], "ek": st_tk, "ev": st_tv}
                blk_readers.extend([st_tk, st_tv])
                continue
            st_ek = P.comp("scalar",
                           (lambda d=ktm[buf]: lambda eng:
                            eng.copy(d[:, :], tp_k[:, :]))(),
                           deps=[st_tk], name=f"{name}_ek{gi}")
            st_ev = P.comp("scalar",
                           (lambda d=vtm[buf]: lambda eng:
                            eng.copy(d[:, 0:128], tp_v[:, :]))(),
                           deps=[st_tv, st_ones_v], name=f"{name}_ev{gi}")

            if phase == "tm":
                prev = {"kv_add": st_ev, "kvbf_readers": [st_ek, st_ev],
                        "tm_readers": [st_ek, st_ev], "prev2_tm_readers":
                        (prev["tm_readers"] if prev is not None else []),
                        "ssb_readers": [st_ek, st_ek], "msks": [st_ek, st_ev],
                        "out_stages": [st_ek, st_ev], "ek": st_ek, "ev": st_ev}
                blk_readers.extend([st_ek, st_ev])
                continue
            def emit_st(eng, s=sl):
                eng.ldweights(kf[0:64, s])
                eng.matmul(stp2[0], kf[0:64, s], qf[0:64, s],
                           start=True, stop=True)
                eng.ldweights(kf[64:128, s], tile_position=(64, 0))
                return eng.matmul(stp2[1], kf[64:128, s], qf[64:128, s],
                                  start=True, stop=True, tile_position=(64, 0))
            st_stmm = P.comp("tensor", emit_st,
                             deps=[proj["k"], proj["q"]] +
                                  ([prev["msk"]] if prev is not None else []),
                             name=f"{name}_st{gi}")
            st_msk = P.comp("vector",
                            lambda eng: (
                                eng.tensor_mul(ssb[0][:, :], stp2[0], mask_tri),
                                eng.tensor_mul(ssb[1][:, :], stp2[1], mask_tri))[-1],
                            deps=[st_stmm] +
                                 ([prev["nd"]] if prev is not None else []),
                            name=f"{name}_msk{gi}")

            if gi > 0:
                st_kvbf = P.comp("scalar",
                                 lambda eng: eng.copy(KVbf[:, :], KV[:, :]),
                                 deps=[prev["kv_add"], prev["nd"]],
                                 name=f"{name}_kvbf{gi}")
            else:
                st_kvbf = None

            first = (gi == 0)

            def emit_nd(eng, s=sl, first=first, buf=buf):
                last = None
                for h in range(2):
                    hsl = slice(h * 64, (h + 1) * 64)
                    tp0 = (0, 64) if h == 1 else None
                    tpq = (64, 64) if h == 1 else None
                    eng.ldweights(vtm[buf][:, h * 64:(h + 1) * 64],
                                  tile_position=tp0)
                    eng.matmul(num_ps[hsl, :], vtm[buf][:, h * 64:(h + 1) * 64],
                               ssb[h][:, :], start=True, stop=first,
                               tile_position=tp0)
                    if not first:
                        eng.ldweights(KVbf[hsl, h * 64:(h + 1) * 64],
                                      tile_position=tpq)
                        eng.matmul(num_ps[hsl, :], KVbf[hsl, h * 64:(h + 1) * 64],
                                   qf[hsl, s], start=False, stop=True,
                                   tile_position=tpq)
                    eng.ldweights(ones_bf[:, 0:64], tile_position=tp0)
                    last = eng.matmul(den_ps[hsl, :], ones_bf[:, 0:64],
                                      ssb[h][:, :], start=True, stop=first,
                                      tile_position=tp0)
                    if not first:
                        eng.ldweights(KVbf[hsl, 128:192], tile_position=tpq)
                        last = eng.matmul(den_ps[hsl, :], KVbf[hsl, 128:192],
                                          qf[hsl, s], start=False, stop=True,
                                          tile_position=tpq)
                return last
            deps_nd = [st_msk, st_ev, proj["q"]]
            if st_kvbf is not None:
                deps_nd.append(st_kvbf)
            if prev is not None:
                deps_nd.extend(prev["out_stages"])
            st_nd = P.comp("tensor", emit_nd, deps=deps_nd,
                           name=f"{name}_nd{gi}")
            st_rec = P.comp("vector",
                            lambda eng: (
                                eng.tensor_single_scalar(
                                    rec_sb[:, :], den_ps[:, :], EPS, ALU.add),
                                eng.reciprocal(rec_sb[:, :], rec_sb[:, :]))[-1],
                            deps=[st_nd], name=f"{name}_rec{gi}")
            st_out = P.comp("vector",
                            (lambda s=gsl: lambda eng:
                             eng.tensor_mul(out_attn[:, s], num_ps[:, :],
                                            rec_sb[:, :]))(),
                            deps=[st_nd, st_rec], name=f"{name}_out{gi}")
            na_out = [st_out]

            kv_deps = [st_ek, st_ev, st_kv0]
            if prev is not None:
                kv_deps.append(prev["kv_add"])

            st_kvd = P.comp("tensor",
                            (lambda buf=buf: lambda eng: (
                                eng.ldweights(ktm[buf][:, :]),
                                eng.matmul(kvd[:, :], ktm[buf][:, :], vtm[buf][:, :],
                                           start=True, stop=True))[-1])(),
                            deps=kv_deps, name=f"{name}_kvd{gi}")
            st_kvadd = P.comp("vector",
                              lambda eng: eng.tensor_add(KV[:, :], KV[:, :], kvd[:, :]),
                              deps=[st_kvd] + ([st_kvbf] if st_kvbf is not None else []),
                              name=f"{name}_kvadd{gi}")

            prev2_tm = prev["tm_readers"] if prev is not None else []
            prev = {
                "kv_add": st_kvadd,
                "kvbf_readers": [st_nd],
                "tm_readers": [st_nd, st_kvd],
                "prev2_tm_readers": prev2_tm,
                "msk": st_msk,
                "nd": st_nd,
                "out_stages": [st_rec, st_out],
                "ek": st_ek,
                "ev": st_ev,
            }
            blk_readers.extend(na_out)
        prev_blk_tail = blk_readers[-4:] + [prev["kv_add"]]
        last_stages = blk_readers[-2:]
    kb.free(pps, num_ps_t, den_ps_t, stp_t, stp_t2, kvd_t, tp_k_t, tp_v_t)
    kb.free(rec_sb, ssb[1], ssb[0], ktm[1], ktm[0], vtm[1], vtm[0],
            p_scr, e_scr, vf, kf, qf, xblk, KVbf, KV, wv, wk, wq, xln)
    return last_stages


def emit_l0rest(kb, x_tiles, T, attn_src, WoT, gate_wT, gate_b, W13_all, W2_all,
                b13_cols, be2, bs2_row, sel18, mask128, identf,
                ones_col, ones_row, ones_col_bf, deps=(), name="l0r",
                dbg_outs=None):
    P = kb.prog
    NE = 18
    NC = T // 512

    wo_out = kb.sb("lr_woout", [128, 8, T], BF16)
    wo_sb = kb.sb("lr_wo", [128, 8, 1024], BF16)
    attn_sb = kb.sb("lr_attn", [128, 8, T], BF16)
    st_a = load(kb, attn_sb[:, :, :], attn_src.rearrange("(g p) t -> p g t", p=128),
                deps=deps, name=f"{name}_lda")
    st_w = P.dma("gpsimd",
                 lambda eng: [eng.dma_start(out=wo_sb[:, :, :],
                                            in_=WoT.rearrange("(g p) m -> p g m", p=128))],
                 deps=list(deps), n_dma=1, name=f"{name}_ldwo")
    WOW = min(T, 1024)
    pp = kb.ps("lr_pp", [128, WOW], F32)
    ev_prev = []
    mm_prev = []
    for hf in range(T // WOW):
        h2 = slice(hf * WOW, (hf + 1) * WOW)
        for mg in range(8):
            st_mm = matmul_fm(kb, pp[:, :],
                              [wo_sb[:, kg, mg * 128:(mg + 1) * 128] for kg in range(8)],
                              [attn_sb[:, kg, h2] for kg in range(8)],
                              deps=[st_a, st_w] + ev_prev, name=f"{name}_womm{hf}_{mg}")
            st_ev = P.comp("scalar",
                           (lambda mg=mg, h2=h2: lambda eng:
                            eng.copy(wo_out[:, mg, h2], pp[:, :]))(),
                           deps=[st_mm], name=f"{name}_woev{hf}_{mg}")
            ev_prev = [st_ev]
            mm_prev.append(st_ev)
    kb.free(pp)
    kb.free(attn_sb, wo_sb)
    wo_tiles = [wo_out[:, g, :] for g in range(8)]
    ln2_fin = emit_ln(kb, wo_tiles, None, T, resid_tiles=x_tiles,
                      deps=mm_prev, ones_col=ones_col, ones_row=ones_row,
                      ones_col_bf=ones_col_bf, name=f"{name}_ln2")
    kb.free(wo_out)

    y_sb = kb.sb("lr_y", [128, 8, T], F32)
    xln2 = kb.sb("lr_xln2", [128, 8, T], BF16)
    xln2_tiles = [xln2[:, g, :] for g in range(8)]
    lnm_fin = emit_ln(kb, x_tiles, xln2_tiles, T, deps=ln2_fin,
                      ones_col=ones_col, ones_row=ones_row, name=f"{name}_lnm")

    sel_sb = kb.sb("lr_sel", [18, NE * 128], BF16)
    be2_sb = kb.sb("lr_be2", [16, 1024], BF16)
    bs2c_sb = kb.sb("lr_bs2c", [128, 8], F32)
    b13_sb = kb.sb("lr_b13", [128, 8, NE], F32)
    comb_sb = kb.sb("lr_comb", [18, T], BF16)
    gw_sb = kb.sb("lr_gw", [128, 8, 16], BF16)
    gb_sb = kb.sb("lr_gb", [1, 16], F32)
    st_gw = P.dma("gpsimd",
                  lambda eng: [
                      eng.dma_start(out=gw_sb[:, :, :],
                                    in_=gate_wT.rearrange("(g p) m -> p g m", p=128)),
                      eng.dma_start(out=gb_sb[:, :], in_=gate_b),
                      eng.dma_start(out=sel_sb[:, :], in_=sel18),
                      eng.dma_start(out=be2_sb[:, :], in_=be2),
                      eng.dma_start(out=bs2c_sb[:, :],
                                    in_=bs2_row.rearrange("a (g p) -> (a p) g", p=128),
                                    allow_slow_non_contiguous=True),
                      eng.dma_start(out=b13_sb[:, :, :], in_=b13_cols),
                  ], deps=list(lnm_fin), n_dma=6, name=f"{name}_ldg")

    sc_sb = kb.sb("lr_scsb", [16, T], BF16)
    identbf_g = kb.sb("lr_idbf", [128, 128], BF16)
    gbb_sb = kb.sb("lr_gbbsb", [128, 16], F32)
    sc_ps = kb.ps("lr_scps", [16, T], F32)
    st_scmm = matmul_fm(kb, sc_ps[:, :],
                        [gw_sb[:, kg, :] for kg in range(8)],
                        xln2_tiles,
                        deps=lnm_fin + [st_gw], name=f"{name}_scmm")
    st_scev = P.comp("vector",
                     lambda eng: eng.tensor_copy(sc_sb[:, :], sc_ps[:, :]),
                     deps=[st_scmm], name=f"{name}_scev")
    st_idbf = P.comp("vector",
                     lambda eng: eng.tensor_copy(identbf_g[:, :], identf),
                     deps=list(deps), name=f"{name}_idbf")
    kb.free(sc_ps)

    gbb_ps = kb.ps("lr_gbb", [128, 16], F32)
    st_gbb = matmul_fm(kb, gbb_ps[:, :], [ones_row], [gb_sb[:, :]],
                       deps=[st_gw, st_scev], name=f"{name}_gbb")
    st_gbbe = P.comp("vector",
                     lambda eng: eng.tensor_copy(gbb_sb[:, :], gbb_ps[:, :]),
                     deps=[st_gbb], name=f"{name}_gbbe")
    kb.free(gbb_ps)

    st_comb1 = P.comp("gpsimd", lambda eng: eng.memset(comb_sb[:, :], 1.0),
                      deps=list(deps), name=f"{name}_comb1")

    ntile = T // 128
    sc_all = kb.sb("lr_scall", [128, ntile, 16], F32)
    sm_all = kb.sb("lr_small", [128, ntile, 16], F32)
    s3_all = kb.sb("lr_s3all", [128, ntile, 16], F32)
    red_a = kb.sb("lr_reda", [128, ntile, 2], F32)
    gs_a = kb.sb("lr_gsa", [128, ntile, 8], F32)
    pr_a = kb.sb("lr_pra", [128, ntile, 24], F32)
    keep_a = kb.sb("lr_keepa", [128, ntile, 8], F32)
    tp_ps = kb.ps("lr_tpps", [128, 16], BF16)
    ct_ps = kb.ps("lr_ctps", [16, 128], F32)

    tfin = []
    prev_cp = None
    for t in range(ntile):
        tsl = slice(t * 128, (t + 1) * 128)
        st_tp = P.comp("tensor",
                       (lambda ts=tsl: lambda eng:
                        eng.transpose(tp_ps[:, :], sc_sb[:, ts], identbf_g[0:16, 0:16]))(),
                       deps=[st_scev, st_gbbe, st_idbf] +
                            ([prev_cp] if prev_cp else []),
                       name=f"{name}_gtp{t}")
        st_cp = P.comp("vector",
                       (lambda t=t: lambda eng:
                        eng.tensor_copy(sc_all[:, t, :], tp_ps[:, :]))(),
                       deps=[st_tp], name=f"{name}_gcp{t}")
        prev_cp = st_cp
        tfin.append(st_cp)

    def stage(nm, fn, deps):
        return P.comp("vector", fn, deps=deps, name=f"{name}_{nm}")

    st1 = stage("gmax", lambda eng: eng.tensor_reduce(
        red_a[:, :, 0:1], sc_all[:, :, :], AX.X, ALU.max), tfin)
    st2 = stage("gsub", lambda eng: eng.tensor_tensor(
        sm_all[:, :, :], sc_all[:, :, :],
        red_a[:, :, 0:1].broadcast_to([128, ntile, 16]), ALU.subtract), [st1])
    st3 = P.comp("scalar", lambda eng: eng.activation(
        sm_all[:, :, :], sm_all[:, :, :], AF.Exp), deps=[st2],
        name=f"{name}_gexp")
    st4 = stage("gsum", lambda eng: eng.tensor_reduce(
        red_a[:, :, 1:2], sm_all[:, :, :], AX.X, ALU.add), [st3])
    st5 = stage("grec", lambda eng: eng.reciprocal(
        red_a[:, :, 1:2], red_a[:, :, 1:2]), [st4])
    st6 = stage("gnrm", lambda eng: eng.tensor_tensor(
        sm_all[:, :, :], sm_all[:, :, :],
        red_a[:, :, 1:2].broadcast_to([128, ntile, 16]), ALU.mult), [st5])
    st7 = stage("gs2", lambda eng: eng.tensor_tensor(
        s3_all[:, :, :], sm_all[:, :, :],
        gbb_sb[:, :].rearrange("p e -> p () e").broadcast_to([128, ntile, 16]),
        ALU.add), [st6, st_gbbe])

    pairs = [(0, 1), (0, 2), (0, 3), (1, 2), (1, 3), (2, 3)]

    def emit_pairsum(eng):
        v4 = s3_all[:, :, :].rearrange("p t (a b) -> p t a b", a=4)
        for i, (a, b) in enumerate(pairs):
            eng.tensor_add(pr_a[:, :, 4 * i:4 * i + 4], v4[:, :, :, a], v4[:, :, :, b])
        return eng.tensor_copy(red_a[:, :, 0:1], red_a[:, :, 0:1])
    st8 = stage("gps", emit_pairsum, [st7])

    def emit_gsmax(eng):
        eng.tensor_max(gs_a[:, :, 0:4], pr_a[:, :, 0:4], pr_a[:, :, 4:8])
        eng.tensor_max(gs_a[:, :, 4:8], pr_a[:, :, 8:12], pr_a[:, :, 12:16])
        return eng.tensor_max(pr_a[:, :, 16:20], pr_a[:, :, 16:20], pr_a[:, :, 20:24])
    st9 = stage("ggm0", emit_gsmax, [st8])

    def emit_gsmax2(eng):
        eng.tensor_max(gs_a[:, :, 0:4], gs_a[:, :, 0:4], gs_a[:, :, 4:8])
        return eng.tensor_copy(red_a[:, :, 0:1], red_a[:, :, 0:1])
    st10 = stage("ggm1", emit_gsmax2, [st9])
    st11 = stage("ggm2", lambda eng: eng.tensor_max(
        gs_a[:, :, 0:4], gs_a[:, :, 0:4], pr_a[:, :, 16:20]), [st10])

    def emit_pairmin(eng):
        for i, (a, b) in enumerate(pairs):
            eng.tensor_tensor(pr_a[:, :, i:i + 1], gs_a[:, :, a:a + 1],
                              gs_a[:, :, b:b + 1], ALU.min)
        return eng.tensor_copy(red_a[:, :, 0:1], red_a[:, :, 0:1])
    st12 = stage("gpm", emit_pairmin, [st11])

    def emit_sm2(eng):
        eng.tensor_max(pr_a[:, :, 6:7], pr_a[:, :, 0:1], pr_a[:, :, 1:2])
        eng.tensor_max(pr_a[:, :, 7:8], pr_a[:, :, 2:3], pr_a[:, :, 3:4])
        return eng.tensor_max(pr_a[:, :, 8:9], pr_a[:, :, 4:5], pr_a[:, :, 5:6])
    st13 = stage("gsm2a", emit_sm2, [st12])
    st14 = stage("gsm2b", lambda eng: eng.tensor_max(
        pr_a[:, :, 6:7], pr_a[:, :, 6:7], pr_a[:, :, 7:8]), [st13])
    st15 = stage("gsm2c", lambda eng: eng.tensor_max(
        pr_a[:, :, 6:7], pr_a[:, :, 6:7], pr_a[:, :, 8:9]), [st14])
    st16 = stage("gkeep", lambda eng: eng.tensor_tensor(
        keep_a[:, :, 0:4], gs_a[:, :, 0:4],
        pr_a[:, :, 6:7].broadcast_to([128, ntile, 4]), ALU.is_ge), [st15])

    def emit_s3(eng):
        v4 = s3_all[:, :, :].rearrange("p t (a b) -> p t a b", a=4)
        kb4 = keep_a[:, :, 0:4].rearrange("p t a -> p t a ()").broadcast_to(
            [128, ntile, 4, 4])
        eng.tensor_tensor(v4, v4, kb4, ALU.mult)
        eng.tensor_scalar(keep_a[:, :, 4:8], keep_a[:, :, 0:4], 1e30, 1e30,
                          ALU.mult, ALU.subtract)
        return eng.tensor_copy(red_a[:, :, 0:1], red_a[:, :, 0:1])
    st17 = stage("gs3", emit_s3, [st16])
    st18 = stage("gs3b", lambda eng: eng.tensor_tensor(
        s3_all[:, :, :].rearrange("p t (a b) -> p t a b", a=4),
        s3_all[:, :, :].rearrange("p t (a b) -> p t a b", a=4),
        keep_a[:, :, 4:8].rearrange("p t a -> p t a ()").broadcast_to(
            [128, ntile, 4, 4]), ALU.add), [st17])
    st19 = stage("gm1", lambda eng: eng.tensor_reduce(
        red_a[:, :, 0:1], s3_all[:, :, :], AX.X, ALU.max), [st18])
    st20 = stage("geq", lambda eng: eng.tensor_tensor(
        pr_a[:, :, 0:16], s3_all[:, :, :],
        red_a[:, :, 0:1].broadcast_to([128, ntile, 16]), ALU.is_equal), [st19])
    st21 = stage("gs4", lambda eng: eng.scalar_tensor_tensor(
        pr_a[:, :, 0:16], pr_a[:, :, 0:16], -1e30, s3_all[:, :, :],
        ALU.mult, ALU.add), [st20])
    st22 = stage("gm2", lambda eng: eng.tensor_reduce(
        red_a[:, :, 1:2], pr_a[:, :, 0:16], AX.X, ALU.max), [st21])
    st23 = stage("gsel", lambda eng: eng.tensor_tensor(
        pr_a[:, :, 0:16], s3_all[:, :, :],
        red_a[:, :, 1:2].broadcast_to([128, ntile, 16]), ALU.is_ge), [st22])
    st24 = stage("gcomb", lambda eng: eng.tensor_mul(
        sm_all[:, :, :], sm_all[:, :, :], pr_a[:, :, 0:16]), [st23])

    comb_done = []
    prev_ce = None
    for t in range(ntile):
        tsl = slice(t * 128, (t + 1) * 128)
        st_ct = P.comp("tensor",
                       (lambda t=t: lambda eng:
                        eng.transpose(ct_ps[:, :], sm_all[:, t, :], identf))(),
                       deps=[st24] + ([prev_ce] if prev_ce else []),
                       name=f"{name}_gct{t}")
        st_ce = P.comp("vector",
                       (lambda ts=tsl: lambda eng:
                        eng.tensor_copy(comb_sb[0:16, ts], ct_ps[:, :][0:16, :]))(),
                       deps=[st_ct, st_comb1], name=f"{name}_gce{t}")
        comb_done.append(st_ce)
        prev_ce = st_ce
    kb.free(tp_ps, ct_ps)
    kb.free(keep_a, pr_a, gs_a, red_a, s3_all, sm_all, sc_all)
    kb.free(gbb_sb, identbf_g, sc_sb, gb_sb, gw_sb)

    if dbg_outs and "comb" in dbg_outs:
        store(kb, dbg_outs["comb"], comb_sb[:, :], deps=comb_done,
              name=f"{name}_dbgcomb")
    if dbg_outs and "x1" in dbg_outs:
        for g in range(8):
            store(kb, dbg_outs["x1"][g * 128:(g + 1) * 128, :], x_tiles[g],
                  deps=lnm_fin, name=f"{name}_dbgx1_{g}")
    xspill = kb.dram(kb.u("lr_xspill"), [6 * 128, T], F32)
    st_xsp = P.dma("gpsimd",
                   lambda eng: [eng.dma_start(
                       out=xspill.ap()[g * 128:(g + 1) * 128, :], in_=x_tiles[g])
                       for g in range(6)],
                   deps=list(ln2_fin), n_dma=6, name=f"{name}_xsp")
    t3_01 = x_tiles[0].bitcast(BF16).rearrange("p (a t) -> p a t", a=2)
    t3_23 = x_tiles[1].bitcast(BF16).rearrange("p (a t) -> p a t", a=2)
    t3v = [t3_01[:, 0, :], t3_01[:, 1, :], t3_23[:, 0, :], t3_23[:, 1, :]]
    z_v2 = x_tiles[2].bitcast(BF16).rearrange("p (a t) -> p a t", a=2)
    zv = [z_v2[:, 0, :], z_v2[:, 1, :]]
    s1_v2 = x_tiles[3].bitcast(BF16).rearrange("p (a t) -> p a t", a=2)
    s1v = [s1_v2[:, 0, :], s1_v2[:, 1, :]]
    comb_bc = x_tiles[4].bitcast(BF16).rearrange("p (a t) -> p a t", a=2)[:, 0, :]

    moe_ps = kb.ps("lr_moeps", [128, 8, 512], F32)
    pgA = moe_ps[:, 0:4, :].rearrange("p a t -> p (a t)")
    pgB = moe_ps[:, 4:8, :].rearrange("p a t -> p (a t)")

    seed_done = []
    seed_prev = None
    for c in range(NC):
        csl = slice(c * 512, (c + 1) * 512)

        def emit_seed(eng, csl=csl):
            last = None
            for mg in range(8):
                msl = slice(mg * 128, (mg + 1) * 128)
                eng.ldweights(be2_sb[:, msl])
                last = eng.matmul(moe_ps[:, mg, :], be2_sb[:, msl],
                                  comb_sb[0:16, csl], start=True, stop=True)
            return last
        st_seed = P.comp("tensor", emit_seed,
                         deps=comb_done + [st_gw] +
                              ([seed_prev] if seed_prev else []),
                         name=f"{name}_seed{c}")

        def emit_sev(eng, csl=csl):
            last = None
            for mg in range(8):
                last = eng.activation(y_sb[:, mg, csl], moe_ps[:, mg, :],
                                      AF.Identity, bias=bs2c_sb[:, mg:mg + 1])
            return last
        st_sev = P.comp("scalar", emit_sev, deps=[st_seed],
                        name=f"{name}_seedev{c}")
        seed_prev = st_sev
        seed_done.append(st_sev)

    w13_sb = [kb.sb("lr_w13", [128, 8, 1024], BF16) for _ in range(2)]
    if 2 * T >= 4096:
        w2_sb = x_tiles[5].bitcast(BF16).rearrange("p (a m) -> p a m", a=4)
        w2_alloc = None
    else:
        w2_alloc = kb.sb("lr_w2", [128, 4, 1024], BF16)
        w2_sb = w2_alloc[:, :, :]
    w13_reader = [None, None]
    w2_reader = None
    hp_prev = None
    hp_par_prev = [None, None]
    last_w2 = None
    zev_prev = [None, None]
    acc_last = [None] * 8
    pool_accs = []
    for e in range(NE):
        buf = e % 2
        st_we = P.dma("gpsimd",
                      (lambda e=e, buf=buf: lambda eng: [
                          eng.dma_start(out=w13_sb[buf][:, :, :],
                                        in_=W13_all[e].rearrange(
                                            "(g p) m -> p g m", p=128))])(),
                      deps=([w13_reader[buf]] if w13_reader[buf]
                            else list(comb_done)),
                      n_dma=1, name=f"{name}_we{e}")
        st_w2ld = P.dma("gpsimd",
                        (lambda e=e: lambda eng: [
                            eng.dma_start(out=w2_sb[:, :, :],
                                          in_=W2_all[e].rearrange(
                                              "(g p) m -> p g m", p=128))])(),
                        deps=[st_xsp] +
                             ([w2_reader] if w2_reader else list(comb_done)),
                        n_dma=1, name=f"{name}_w2ld{e}")
        st_cb = P.comp("tensor",
                       (lambda e=e: lambda eng: (
                           eng.ldweights(sel_sb[:, e * 128:(e + 1) * 128]),
                           [eng.matmul(pgA[:, c0:min(c0 + 512, T)],
                                       sel_sb[:, e * 128:(e + 1) * 128],
                                       comb_sb[:, c0:min(c0 + 512, T)],
                                       start=True, stop=True)
                            for c0 in range(0, T, 512)])[-1][-1])(),
                       deps=comb_done + [st_gw] +
                            ([zev_prev[0]] if zev_prev[0] else seed_done),
                       name=f"{name}_cb{e}")
        st_cbe = P.comp("scalar",
                        lambda eng: eng.copy(comb_bc, pgA[:, 0:T]),
                        deps=[st_cb, st_xsp] +
                             ([hp_prev] if hp_prev else []),
                        name=f"{name}_cbe{e}")
        evA_prev = None
        evB_prev = None
        hps = []
        for ig in range(4):
            sp = ig % 2

            def emit_mmA(eng, ig=ig, buf=buf):
                last = None
                for kg in range(8):
                    eng.ldweights(w13_sb[buf][:, kg, ig * 128:(ig + 1) * 128])
                    for c0 in range(0, T, 512):
                        last = eng.matmul(pgA[:, c0:min(c0 + 512, T)],
                                          w13_sb[buf][:, kg, ig * 128:(ig + 1) * 128],
                                          xln2[:, kg, c0:min(c0 + 512, T)],
                                          start=(kg == 0), stop=(kg == 7))
                return last
            st_mmA = P.comp("tensor", emit_mmA,
                            deps=[st_we] +
                                 ([evA_prev] if evA_prev else [st_cbe]),
                            name=f"{name}_mA{e}i{ig}")

            def emit_mmB(eng, ig=ig, buf=buf):
                last = None
                for kg in range(8):
                    eng.ldweights(w13_sb[buf][:, kg, (4 + ig) * 128:(5 + ig) * 128])
                    for c0 in range(0, T, 512):
                        last = eng.matmul(pgB[:, c0:min(c0 + 512, T)],
                                          w13_sb[buf][:, kg, (4 + ig) * 128:(5 + ig) * 128],
                                          xln2[:, kg, c0:min(c0 + 512, T)],
                                          start=(kg == 0), stop=(kg == 7))
                return last
            st_mmB = P.comp("tensor", emit_mmB,
                            deps=[st_we] +
                                 ([evB_prev] if evB_prev else
                                  ([zev_prev[1]] if zev_prev[1] else seed_done)),
                            name=f"{name}_mB{e}i{ig}")
            st_evA = P.comp("scalar",
                            (lambda ig=ig, sp=sp, e=e: lambda eng:
                             eng.activation(s1v[sp], pgA[:, 0:T], AF.Silu,
                                            bias=b13_sb[:, ig, e:e + 1]))(),
                            deps=[st_mmA, st_xsp] +
                                 ([hps[ig - 2]] if ig >= 2 else
                                  ([hp_par_prev[sp]] if hp_par_prev[sp] else [])),
                            name=f"{name}_evA{e}i{ig}")
            st_evB = P.comp("scalar",
                            (lambda ig=ig, e=e: lambda eng:
                             eng.activation(t3v[ig], pgB[:, 0:T], AF.Identity,
                                            bias=b13_sb[:, 4 + ig, e:e + 1]))(),
                            deps=[st_mmB, st_xsp] +
                                 ([last_w2] if last_w2 else []),
                            name=f"{name}_evB{e}i{ig}")
            st_hp = P.comp("vector",
                           (lambda ig=ig, sp=sp: lambda eng: (
                               eng.tensor_mul(t3v[ig], s1v[sp], t3v[ig]),
                               eng.tensor_mul(t3v[ig], t3v[ig], comb_bc))[-1])(),
                           deps=[st_evA, st_evB, st_cbe],
                           name=f"{name}_hp{e}i{ig}")
            hps.append(st_hp)
            evA_prev = st_evA
            evB_prev = st_evB
        zev_now = [None, None]
        for og2 in range(8):
            wb = og2 % 2
            pg = pgA if wb == 0 else pgB

            def emit_w2(eng, og2=og2, pg=pg):
                last = None
                for kg in range(4):
                    eng.ldweights(w2_sb[:, kg, og2 * 128:(og2 + 1) * 128])
                    for c0 in range(0, T, 512):
                        last = eng.matmul(pg[:, c0:min(c0 + 512, T)],
                                          w2_sb[:, kg, og2 * 128:(og2 + 1) * 128],
                                          t3v[kg][:, c0:min(c0 + 512, T)],
                                          start=(kg == 0), stop=(kg == 3))
                return last
            w2deps = [st_w2ld] + hps
            if zev_now[wb] is not None:
                w2deps.append(zev_now[wb])
            elif og2 == 0:
                w2deps.append(evA_prev)
            elif og2 == 1:
                w2deps.append(evB_prev)
            st_w2 = P.comp("tensor", emit_w2, deps=w2deps,
                           name=f"{name}_w2e{e}o{og2}")
            st_zev = P.comp("scalar",
                            (lambda wb=wb, pg=pg: lambda eng:
                             eng.copy(zv[wb], pg[:, 0:T]))(),
                            deps=[st_w2, st_xsp] +
                                 ([acc_last[og2 - 2]] if og2 >= 2 else
                                  [a for a in (acc_last[6], acc_last[7])
                                   if a is not None][-1:]),
                            name=f"{name}_zev{e}o{og2}")
            st_acc = P.comp("gpsimd" if og2 % 2 == 0 else "vector",
                            (lambda og2=og2, wb=wb: lambda eng:
                             eng.tensor_add(y_sb[:, og2, :], y_sb[:, og2, :],
                                            zv[wb]))(),
                            deps=[st_zev] +
                                 ([acc_last[og2]] if acc_last[og2]
                                  else seed_done),
                            name=f"{name}_acc{e}o{og2}")
            zev_now[wb] = st_zev
            acc_last[og2] = st_acc
            pool_accs.append(st_acc)
            last_w2 = st_w2
        zev_prev = zev_now
        hp_prev = hps[3]
        hp_par_prev = [hps[2], hps[3]]
        w13_reader[buf] = st_mmB
        w2_reader = last_w2
    kb.free(moe_ps)

    if dbg_outs and "y" in dbg_outs:
        for g in range(8):
            store(kb, dbg_outs["y"][g * 128:(g + 1) * 128, :], y_sb[:, g, :],
                  deps=[a for a in acc_last if a is not None],
                  name=f"{name}_dbgy_{g}")
    st_xrl = P.dma("gpsimd",
                   lambda eng: [eng.dma_start(
                       out=x_tiles[g], in_=xspill.ap()[g * 128:(g + 1) * 128, :])
                       for g in range(6)],
                   deps=[last_w2] + hps + [a for a in acc_last if a is not None],
                   n_dma=6, name=f"{name}_xrl")
    kb.free(xln2, sel_sb, be2_sb, bs2c_sb, b13_sb, comb_sb,
            w13_sb[0], w13_sb[1],
            *([w2_alloc] if w2_alloc is not None else []))
    y_tiles = [y_sb[:, g, :] for g in range(8)]
    ln4_fin = emit_ln(kb, y_tiles, None, T, resid_tiles=x_tiles,
                      deps=[a for a in acc_last if a is not None] + [st_xrl],
                      ones_col=ones_col, ones_row=ones_row, name=f"{name}_lnmo")
    kb.free(y_sb)
    return ln4_fin



def emit_mamba(kb, x_tiles, T, S, W_in, conv_w_cols, conv_b_cols, W_x, wdt_col,
               bdt_col, alog_cols, d_cols, W_out, selC, halo_mask, tok_mask,
               ones_col, ones_row, ones_col_bf=None, deps=(), name="mb"):
    P = kb.prog
    NCK = 8
    R = 8

    cw_sb = kb.sb("mb_cw", [128, 16, 4], F32)
    cbb_sb = kb.sb("mb_cb", [128, 16], F32)
    wx_sb = kb.sb("mb_wx", [128, 16, 33], BF16)
    wdt_sb = kb.sb("mb_wdt", [128, 2], F32)
    bdt_sb = kb.sb("mb_bdt", [128, 2], F32)
    acol_sb = kb.sb("mb_acol", [128, 32], F32)
    dcol_sb = kb.sb("mb_dcol", [128, 16], F32)
    selc_sb = kb.sb("mb_selc", [33, 1], F32)
    hmask_sb = kb.sb("mb_hmask", [8, 1], BF16)
    tmask_sb = kb.sb("mb_tmask", [8, 1], F32)
    st_w = P.dma("gpsimd",
                 lambda eng: [
                     eng.dma_start(out=cw_sb[:, :, :], in_=conv_w_cols),
                     eng.dma_start(out=cbb_sb[:, :], in_=conv_b_cols),
                     eng.dma_start(out=wx_sb[:, :, :],
                                   in_=W_x.rearrange("(g p) m -> p g m", p=128)),
                     eng.dma_start(out=wdt_sb[:, :], in_=wdt_col),
                     eng.dma_start(out=bdt_sb[:, :], in_=bdt_col),
                     eng.dma_start(out=acol_sb[:, :], in_=alog_cols),
                     eng.dma_start(out=dcol_sb[:, :], in_=d_cols),
                     eng.dma_start(out=selc_sb[:, :], in_=selC),
                     eng.dma_start(out=hmask_sb[:, :], in_=halo_mask),
                     eng.dma_start(out=tmask_sb[:, :], in_=tok_mask),
                 ], deps=list(deps), n_dma=10, name=f"{name}_ldw")
    st_a = P.comp("scalar",
                  lambda eng: eng.activation(acol_sb[:, :], acol_sb[:, :], AF.Exp),
                  deps=[st_w], name=f"{name}_aexp")
    st_a2 = P.comp("vector",
                   lambda eng: eng.tensor_single_scalar(acol_sb[:, :], acol_sb[:, :],
                                                        -1.0, ALU.mult),
                   deps=[st_a], name=f"{name}_aneg")

    xln = kb.sb("mb_xln", [128, 8, T], BF16)
    csum_sb = kb.sb("mb_csum", [1, T], F32)
    xln_tiles = [xln[:, g, :] for g in range(8)]
    ln_fin = emit_ln(kb, x_tiles, xln_tiles, T, deps=deps,
                     ones_col=ones_col, ones_row=ones_row, name=f"{name}_ln3")

    xs_a2a_in = kb.dram(kb.u("mb_a2ain"), [2048, T], BF16)
    xs_a2a_out = kb.dram(kb.u("mb_a2aout"), [2048, T], BF16)
    z_stage = kb.dram(kb.u("mb_zstage"), [2048, T], BF16)
    y3_stage = kb.dram(kb.u("mb_y3stage"), [2048, T], BF16)
    win_sb = kb.sb("mb_win", [128, 8, 2048], BF16)
    st_wa = P.dma("gpsimd",
                  lambda eng: [eng.dma_start(
                      out=win_sb[:, :, :],
                      in_=W_in[:, 0:2048].rearrange("(g p) m -> p g m", p=128))],
                  deps=ln_fin, n_dma=1, name=f"{name}_ldwa")

    halo_ag_in = kb.dram(kb.u("mb_haloin"), [2048, 4], BF16)
    halo_ag_out = kb.dram(kb.u("mb_haloout"), [R * 2048, 4], BF16)
    halo_bounce = kb.dram(kb.u("mb_halob"), [1, 2048 * 4], BF16)
    halo_sb = kb.sb("mb_halo", [128, 16, 4], F32)
    halo_all = kb.sb("mb_haloall", [8, 2048 * 4], BF16)
    hsel_sb = kb.sb("mb_hselsb", [1, 2048 * 4], BF16)
    hz_ps = kb.ps("mb_hzps", [128, 4], F32)
    hz_sb = kb.sb("mb_hzsb", [128, 16, 4], BF16)

    halo_stages = []
    st_prev = None
    for og in range(16):
        def emit_hz(eng, og=og):
            last = None
            for kg in range(8):
                eng.ldweights(win_sb[:, kg, og * 128:(og + 1) * 128])
                last = eng.matmul(hz_ps[:, :], win_sb[:, kg, og * 128:(og + 1) * 128],
                                  xln[:, kg, T - 4:T], start=(kg == 0), stop=(kg == 7))
            return last
        st_hz = P.comp("tensor", emit_hz,
                       deps=ln_fin + [st_wa] + ([st_prev] if st_prev else []),
                       name=f"{name}_hz{og}")
        st_he = P.comp("vector",
                       (lambda og=og: lambda eng:
                        eng.tensor_copy(hz_sb[:, og, :], hz_ps[:, :]))(),
                       deps=[st_hz], name=f"{name}_he{og}")
        st_prev = st_he
        halo_stages.append(st_he)
    st_hst = store(kb, halo_ag_in.ap().rearrange("(g p) c -> p g c", p=128),
                   hz_sb[:, :, :], deps=halo_stages, name=f"{name}_hstage")
    st_hag = P.coll(
        lambda eng: eng.collective_compute(
            "AllGather", ALU.bypass, replica_groups=[list(range(R))],
            ins=[halo_ag_in.ap().opt()], outs=[halo_ag_out.ap().opt()]),
        deps=[st_hst], name=f"{name}_hag")
    st_hld = load(kb, halo_all[:, :],
                  halo_ag_out.ap().rearrange("(r q) c -> r (q c)", r=8),
                  deps=[st_hag], name=f"{name}_hld")
    hsel_ps = kb.ps("mb_hselps", [1, 2048], F32)
    st_hsev = None
    for rr in range(4):
        def emit_hsel(eng, rr=rr):
            last = None
            eng.ldweights(hmask_sb[:, :])
            for n0 in range(0, 2048, 512):
                last = eng.matmul(hsel_ps[:, n0:n0 + 512], hmask_sb[:, :],
                                  halo_all[:, rr * 2048 + n0:rr * 2048 + n0 + 512],
                                  start=True, stop=True)
            return last
        st_hsel = P.comp("tensor", emit_hsel,
                         deps=[st_hld, st_w] + ([st_hsev] if st_hsev else []),
                         name=f"{name}_hsel{rr}")
        st_hsev = P.comp("vector",
                         (lambda rr=rr: lambda eng: eng.tensor_copy(
                             hsel_sb[:, rr * 2048:(rr + 1) * 2048], hsel_ps[:, :]))(),
                         deps=[st_hsel], name=f"{name}_hsev{rr}")
    st_hbo = store(kb, halo_bounce.ap(), hsel_sb[:, :], deps=[st_hsev],
                   name=f"{name}_hbo")
    st_hbi = P.dma("gpsimd",
                   lambda eng: [eng.dma_start(
                       out=halo_sb[:, :, :],
                       in_=halo_bounce.ap().rearrange("a (g p c) -> (a p) g c",
                                                      p=128, c=4))],
                   deps=[st_hbo], n_dma=1, name=f"{name}_hbi")
    kb.free(hz_sb, hsel_sb, halo_all)
    kb.free(hz_ps, hsel_ps)

    xs_og = [kb.sb("mb_xsog", [128, T], BF16) for _ in range(2)]
    xzf_sb = kb.sb("mb_xzf", [128, T], F32)
    acc_sb = kb.sb("mb_acc", [128, T], F32)
    corr_sb = kb.sb("mb_corr", [128, 4], F32)
    bcd_ps = kb.ps("mb_bcdps", [33, T], F32)
    xz_ps = kb.ps("mb_xzps", [128, T], F32)
    og_stages = []
    z_stages = []
    bcd_mms = []
    xs_stores = []
    prev_readers = []
    prev_xzf_readers = []
    prev_st = {}
    st_wz = None
    st_bcd = None
    for og in range(32):
        is_x = og < 16
        buf = og % 2
        if og == 16:
            st_wz = P.dma("gpsimd",
                          lambda eng: [eng.dma_start(
                              out=win_sb[:, :, :],
                              in_=W_in[:, 2048:4096].rearrange(
                                  "(g p) m -> p g m", p=128))],
                          deps=[og_stages[-1], bcd_mms[-1], halo_stages[-1]],
                          n_dma=1, name=f"{name}_ldwz")

        def emit_xz(eng, og=og):
            last = None
            for kg in range(8):
                eng.ldweights(win_sb[:, kg, (og % 16) * 128:(og % 16 + 1) * 128])
                for n0 in range(0, T, 512):
                    last = eng.matmul(xz_ps[:, n0:min(n0 + 512, T)],
                                      win_sb[:, kg, (og % 16) * 128:(og % 16 + 1) * 128],
                                      xln[:, kg, n0:min(n0 + 512, T)],
                                      start=(kg == 0), stop=(kg == 7))
            return last
        st_mm = P.comp("tensor", emit_xz,
                       deps=ln_fin + [st_wa] + prev_readers +
                            ([st_wz] if og >= 16 else []) +
                            ([st_hsev, st_hbi] if og == 0 else []),
                       name=f"{name}_xz{og}")
        if is_x:
            st_xze = P.comp("scalar",
                            (lambda: lambda eng:
                             eng.copy(xzf_sb[:, :], xz_ps[:, :]))(),
                            deps=[st_mm] + prev_xzf_readers,
                            name=f"{name}_xze{og}")
            st_cv = P.comp("vector",
                           (lambda og=og: lambda eng: (
                               eng.tensor_scalar(acc_sb[:, :], xzf_sb[:, :],
                                                 cw_sb[:, og, 3:4], None, ALU.mult),
                               eng.scalar_tensor_tensor(
                                   acc_sb[:, 1:T], xzf_sb[:, 0:T - 1],
                                   cw_sb[:, og, 2:3], acc_sb[:, 1:T],
                                   ALU.mult, ALU.add),
                               eng.scalar_tensor_tensor(
                                   acc_sb[:, 2:T], xzf_sb[:, 0:T - 2],
                                   cw_sb[:, og, 1:2], acc_sb[:, 2:T],
                                   ALU.mult, ALU.add),
                               eng.scalar_tensor_tensor(
                                   acc_sb[:, 3:T], xzf_sb[:, 0:T - 3],
                                   cw_sb[:, og, 0:1], acc_sb[:, 3:T],
                                   ALU.mult, ALU.add))[-1])(),
                           deps=[st_xze, st_w], name=f"{name}_cv{og}")
            st_c1 = P.comp("vector",
                           (lambda og=og: lambda eng:
                            eng.tensor_scalar(corr_sb[:, 0:3], halo_sb[:, og, 1:4],
                                              cw_sb[:, og, 0:1], None, ALU.mult))(),
                           deps=[st_hbi, st_w] + ([og_stages[-1]] if og > 0 else []),
                           name=f"{name}_c1o{og}")
            st_c2 = P.comp("vector",
                           (lambda og=og: lambda eng:
                            eng.scalar_tensor_tensor(
                                corr_sb[:, 0:2], halo_sb[:, og, 2:4],
                                cw_sb[:, og, 1:2], corr_sb[:, 0:2],
                                ALU.mult, ALU.add))(),
                           deps=[st_c1], name=f"{name}_c2o{og}")
            st_c3 = P.comp("vector",
                           (lambda og=og: lambda eng:
                            eng.scalar_tensor_tensor(
                                corr_sb[:, 0:1], halo_sb[:, og, 3:4],
                                cw_sb[:, og, 2:3], corr_sb[:, 0:1],
                                ALU.mult, ALU.add))(),
                           deps=[st_c2], name=f"{name}_c3o{og}")
            st_c4 = P.comp("vector",
                           (lambda: lambda eng:
                            eng.tensor_add(acc_sb[:, 0:3], acc_sb[:, 0:3],
                                           corr_sb[:, 0:3]))(),
                           deps=[st_c3, st_cv], name=f"{name}_c4o{og}")
            st_si = P.comp("scalar",
                           (lambda og=og, buf=buf: lambda eng:
                            eng.activation(xs_og[buf][:, :], acc_sb[:, :], AF.Silu,
                                           bias=cbb_sb[:, og:og + 1]))(),
                           deps=[st_c4] + prev_st.get(buf, []),
                           name=f"{name}_si{og}")
            st_so = store(kb, xs_a2a_in.ap()[og * 128:(og + 1) * 128, :],
                          xs_og[buf][:, :], deps=[st_si], name=f"{name}_so{og}")
            xs_stores.append(st_so)
            st_bcd = P.comp("tensor",
                            (lambda og=og, buf=buf: lambda eng: (
                                eng.ldweights(wx_sb[:, og, :]),
                                [eng.matmul(bcd_ps[:, n0:min(n0 + 512, T)],
                                            wx_sb[:, og, :],
                                            xs_og[buf][:, n0:min(n0 + 512, T)],
                                            start=(og == 0), stop=(og == 15))
                                 for n0 in range(0, T, 512)][-1])[-1])(),
                            deps=[st_si, st_w] +
                                 ([bcd_mms[-1]] if bcd_mms else []),
                            name=f"{name}_bcdp{og}")
            bcd_mms.append(st_bcd)
            prev_st[buf] = [st_so, st_bcd]
            prev_readers = [st_xze]
            prev_xzf_readers = [st_cv]
            og_stages.append(st_si)
        else:
            st_cp = P.comp("scalar",
                           (lambda og=og, buf=buf: lambda eng:
                            eng.copy(xs_og[buf][:, :], xz_ps[:, :]))(),
                           deps=[st_mm] + prev_st.get(buf, []),
                           name=f"{name}_zcp{og}")
            st_so = store(kb, z_stage.ap()[(og - 16) * 128:(og - 15) * 128, :],
                          xs_og[buf][:, :], deps=[st_cp], name=f"{name}_zso{og}")
            prev_st[buf] = [st_so]
            prev_readers = [st_cp]
            z_stages.append(st_so)
    kb.free(xz_ps)
    kb.free(corr_sb, acc_sb, xzf_sb, xs_og[1], xs_og[0], halo_sb, win_sb)

    bcd_sb = kb.sb("mb_bcdsb", [33, T], F32)
    st_bcde = P.comp("vector",
                     lambda eng: eng.tensor_copy(bcd_sb[:, :], bcd_ps[:, :]),
                     deps=[bcd_mms[-1], z_stages[-1]], name=f"{name}_bcde")
    kb.free(bcd_ps)
    csum_ps = kb.ps("mb_csps", [1, T], F32)
    st_cs = P.comp("tensor",
                   (lambda: lambda eng: [
                       eng.matmul(csum_ps[:, n0:min(n0 + 512, T)], selc_sb[:, :],
                                  bcd_sb[:, n0:min(n0 + 512, T)],
                                  start=True, stop=True)
                       for n0 in range(0, T, 512)][-1])(),
                   deps=[st_bcde, st_w], name=f"{name}_cs")
    st_cse = P.comp("vector",
                    lambda eng: eng.tensor_copy(csum_sb[:, :], csum_ps[:, :]),
                    deps=[st_cs], name=f"{name}_cse")
    kb.free(csum_ps)

    bcd_ag_in = kb.dram(kb.u("mb_bcdin"), [17, T], F32)
    bcd_ag_out = kb.dram(kb.u("mb_bcdout"), [R * 17, T], F32)
    st_bc_st = P.dma("gpsimd",
                     lambda eng: [
                         eng.dma_start(out=bcd_ag_in.ap()[0:16, :],
                                       in_=bcd_sb[0:16, :]),
                         eng.dma_start(out=bcd_ag_in.ap()[16:17, :],
                                       in_=bcd_sb[32:33, :]),
                     ], deps=[st_bcde], n_dma=2, name=f"{name}_bcst")

    def emit_a2a(eng):
        from concourse.aot_env import allow_ici_all_to_all
        with allow_ici_all_to_all():
            return eng.collective_compute(
                "AllToAll", ALU.bypass, replica_groups=[list(range(R))],
                ins=[xs_a2a_in.ap().opt()], outs=[xs_a2a_out.ap().opt()])
    st_a2a = P.coll(emit_a2a, deps=xs_stores, name=f"{name}_a2a")
    st_bag = P.coll(
        lambda eng: eng.collective_compute(
            "AllGather", ALU.bypass, replica_groups=[list(range(R))],
            ins=[bcd_ag_in.ap().opt()], outs=[bcd_ag_out.ap().opt()]),
        deps=[st_bc_st], name=f"{name}_bag")
    kb.free(bcd_sb)

    x_ck = xln[:, 0:2, :]
    u_ck = xln[:, 2:4, :]
    dt_ck = xln[:, 4:8, :].rearrange("p a t -> p (a t)").bitcast(F32).rearrange(
        "p (g t) -> p g t", g=2)
    dtraw_sb = kb.sb("mb_dtraw", [1, T], F32)
    brow_sb = [kb.sb("mb_brow", [1, T], BF16) for _ in range(2)]
    bbc_sb = [kb.sb("mb_bbc", [128, T], BF16) for _ in range(2)]
    scr2 = kb.sb("mb_scr2", [128, 2, T], F32)
    t_scr = scr2
    abar_sb = [scr2[:, 0, :], scr2[:, 1, :]]
    bu_sb = [kb.sb("mb_bu", [128, T], BF16) for _ in range(2)]
    h2_sb = kb.sb("mb_h", [128, 2, T], F32)
    yrow_sb = kb.sb("mb_yrow", [1, T], F32)
    hstate = kb.sb("mb_hstate", [128, 32], F32)
    xck_b = kb.sb("mb_xckb", [128, 2, T], BF16)
    x_bufs = [x_ck, xck_b[:, :, :]]
    st_h0 = P.comp("gpsimd", lambda eng: eng.memset(hstate[:, :], 0.0),
                   deps=[st_cse, z_stages[-1]], name=f"{name}_h0")
    ones_col_b = ones_col_bf if ones_col_bf is not None else kb.consts["ones_col_bf"]
    ones_row_b = kb.consts["ones_row_bf"]

    ar_in = kb.dram(kb.u("mb_arin"), [8, T], F32)
    ar_out = kb.dram(kb.u("mb_arout"), [8, T], F32)
    dtb_ps = kb.ps("mb_dtbps", [128, T], F32)
    y_ps = kb.ps("mb_yps", [1, T], F32)

    prev_ck = {}
    sc_p = {}
    hb_p = {}
    hc_p = {}
    red_p = {}
    bu_p = {}
    ab_p = {}
    bbcm_p = {}
    bbc_prev = [None]
    bu_of_bbcbuf = {}
    st_yev_prev = None
    last_yst = None
    ldx_stages = {}
    u_stages = {}
    for ck in range(NCK):
        if ck == 0:
            ldx_stages[0] = load(
                kb, x_bufs[0],
                xs_a2a_out.ap()[0:256, :].rearrange("(g p) t -> p g t", p=128),
                deps=[st_a2a] + og_stages + z_stages, name=f"{name}_ldx0")
        st_ldx = ldx_stages[ck]
        st_lddt = load(kb, dtraw_sb[:, :],
                       bcd_ag_out.ap()[ck * 17 + 16:ck * 17 + 17, :],
                       deps=[st_bag] + prev_ck.get("dtraw_readers", []) +
                            ([st_cse] if ck == 0 else []),
                       name=f"{name}_lddt{ck}", engine="sync")
        st_dtb = P.comp("tensor",
                        (lambda: lambda eng: [
                            eng.matmul(dtb_ps[:, n0:min(n0 + 512, T)], ones_row,
                                       dtraw_sb[:, n0:min(n0 + 512, T)],
                                       start=True, stop=True)
                            for n0 in range(0, T, 512)][-1])(),
                        deps=[st_lddt, st_h0] + prev_ck.get("dtb_readers", []),
                        name=f"{name}_dtb{ck}")
        st_dts = []
        t1s = []
        for g in range(2):
            st_t1 = P.comp("vector",
                           (lambda g=g: lambda eng:
                            eng.tensor_scalar(t_scr[:, g, :], dtb_ps[:, :],
                                              wdt_sb[:, g:g + 1], bdt_sb[:, g:g + 1],
                                              ALU.mult, ALU.add))(),
                           deps=[st_dtb, st_w] +
                                ([sc_p[g]] if g in sc_p else []),
                           name=f"{name}_dt1c{ck}g{g}")
            t1s.append(st_t1)
            st_t2 = P.comp("scalar",
                           (lambda g=g: lambda eng:
                            eng.activation(dt_ck[:, g, :], t_scr[:, g, :], AF.Abs))(),
                           deps=[st_t1] +
                                ([ab_p[g]] if g in ab_p else []) +
                                prev_ck.get("uck_readers", []) +
                                ([st_ldx] if ck == 0 else []),
                           name=f"{name}_dt2c{ck}g{g}")
            st_t3 = P.comp("scalar",
                           (lambda g=g: lambda eng:
                            eng.activation(dt_ck[:, g, :], dt_ck[:, g, :], AF.Exp,
                                           scale=-1.0))(),
                           deps=[st_t2], name=f"{name}_dt3c{ck}g{g}")
            st_t4 = P.comp("vector",
                           (lambda g=g: lambda eng:
                            eng.tensor_single_scalar(dt_ck[:, g, :], dt_ck[:, g, :],
                                                     1.0, ALU.add))(),
                           deps=[st_t3], name=f"{name}_dt4c{ck}g{g}")
            st_t5 = P.comp("scalar",
                           (lambda g=g: lambda eng:
                            eng.activation(dt_ck[:, g, :], dt_ck[:, g, :], AF.Ln))(),
                           deps=[st_t4], name=f"{name}_dt5c{ck}g{g}")
            st_t6 = P.comp("vector",
                           (lambda g=g: lambda eng: (
                             eng.tensor_scalar_max(t_scr[:, g, :], t_scr[:, g, :], 0.0),
                             eng.tensor_add(dt_ck[:, g, :], dt_ck[:, g, :],
                                            t_scr[:, g, :]))[-1])(),
                           deps=[st_t5], name=f"{name}_dt6c{ck}g{g}")
            st_dts.append(st_t6)
        st_u = P.comp("vector",
                      (lambda ck=ck: lambda eng:
                       eng.tensor_mul(u_ck, dt_ck, x_bufs[ck % 2]))(),
                      deps=st_dts + [st_ldx] +
                           [x for x in (bu_p.get(0), bu_p.get(1)) if x],
                      name=f"{name}_u{ck}")
        u_stages[ck] = st_u
        if ck + 1 < NCK:
            ldx_stages[ck + 1] = load(
                kb, x_bufs[(ck + 1) % 2],
                xs_a2a_out.ap()[(ck + 1) * 256:(ck + 2) * 256, :].rearrange(
                    "(g p) t -> p g t", p=128),
                deps=[st_a2a] +
                     ([u_stages[ck - 1]] if ck - 1 in u_stages else
                      z_stages[-1:] + [st_cse]),
                name=f"{name}_ldx{ck + 1}")

        bbc_st = {}

        def _emit_bn(n, ck=ck, t1s=t1s):
            lbuf = n % 2
            st_lb = P.dma("gpsimd",
                          (lambda n=n, ck=ck, lbuf=lbuf: lambda eng: [
                              eng.dma_start(
                                  out=brow_sb[lbuf][:, :],
                                  in_=bcd_ag_out.ap()[ck * 17 + n:ck * 17 + n + 1, :])
                          ])(),
                          deps=[st_bag] +
                               ([bbcm_p[lbuf]] if lbuf in bbcm_p else []) +
                               ([st_cse] if (ck == 0 and n < 2) else []),
                          n_dma=1, name=f"{name}_lbc{ck}n{n}")
            st_bbcm = P.comp("tensor",
                             (lambda lbuf=lbuf: lambda eng: [
                                 eng.matmul(dtb_ps[:, n0:min(n0 + 512, T)],
                                            ones_row_b,
                                            brow_sb[lbuf][:, n0:min(n0 + 512, T)],
                                            start=True, stop=True)
                                 for n0 in range(0, T, 512)][-1])(),
                             deps=[st_lb] +
                                  ([t1s[1]] if n == 0 else []) +
                                  ([bbc_prev[0]] if bbc_prev[0] else []),
                             name=f"{name}_bbm{ck}n{n}")
            bbcm_p[lbuf] = st_bbcm
            nbuf = n % 2
            st_bbc = P.comp("scalar",
                            (lambda nbuf=nbuf: lambda eng:
                             eng.copy(bbc_sb[nbuf][:, :], dtb_ps[:, :]))(),
                            deps=[st_bbcm] +
                                 ([bu_of_bbcbuf[nbuf]] if nbuf in bu_of_bbcbuf
                                  else []),
                            name=f"{name}_bbc{ck}n{n}")
            bbc_prev[0] = st_bbc
            bbc_st[n] = st_bbc

        ab_st = {}

        def _emit_ab(s, ck=ck, st_dts=st_dts):
            g = s % 2
            buf = s % 2
            st = P.comp("scalar",
                        (lambda g=g, s=s, buf=buf: lambda eng:
                         eng.activation(abar_sb[buf], dt_ck[:, g, :],
                                        AF.Exp, scale=acol_sb[:, s:s + 1]))(),
                        deps=[st_dts[g], st_a2] +
                             ([sc_p[buf]] if buf in sc_p else []),
                        name=f"{name}_ab{ck}s{s}")
            ab_st[s] = st
            ab_p[g] = st

        _emit_bn(0)
        _emit_ab(0)
        _emit_ab(1)
        _emit_bn(1)
        for n in range(16):
            st_bbc = bbc_st[n]
            nbuf = n % 2
            for g in range(2):
                s = n * 2 + g
                buf = s % 2
                st_ab = ab_st[s]
                st_bu = P.comp("gpsimd",
                               (lambda g=g, buf=buf, nbuf=nbuf: lambda eng:
                                eng.tensor_mul(bu_sb[buf][:, :], u_ck[:, g, :],
                                               bbc_sb[nbuf][:, :]))(),
                               deps=[st_u, st_bbc] +
                                    ([sc_p[buf]] if buf in sc_p else []),
                               name=f"{name}_bu{ck}s{s}")
                st_sc = P.comp("vector",
                               (lambda s=s, buf=buf: lambda eng:
                                eng.tensor_tensor_scan(
                                    h2_sb[:, buf, :], abar_sb[buf],
                                    bu_sb[buf][:, :], hstate[:, s:s + 1],
                                    ALU.mult, ALU.add))(),
                               deps=[st_ab, st_bu, st_h0] +
                                    ([red_p[buf]] if buf in red_p else []) +
                                    ([hc_p[buf]] if buf in hc_p else []),
                               name=f"{name}_sc{ck}s{s}")
                st_hc = P.comp("vector",
                               (lambda s=s, buf=buf: lambda eng:
                                eng.tensor_copy(hstate[:, s:s + 1],
                                                h2_sb[:, buf, T - 1:T]))(),
                               deps=[st_sc], name=f"{name}_hc{ck}s{s}")
                st_red = P.comp("tensor",
                                (lambda s=s, buf=buf, first=(s == 0): lambda eng: [
                                    eng.matmul(y_ps[:, n0:min(n0 + 512, T)],
                                               ones_col,
                                               h2_sb[:, buf, n0:min(n0 + 512, T)],
                                               start=first, stop=(s == 31))
                                    for n0 in range(0, T, 512)][-1])(),
                                deps=[st_sc] +
                                     ([st_yev_prev] if (s == 0 and st_yev_prev) else []),
                                name=f"{name}_red{ck}s{s}")
                sc_p[buf] = st_sc
                hc_p[buf] = st_hc
                red_p[buf] = st_red
                bu_p[buf] = st_bu
                bu_of_bbcbuf[nbuf] = st_bu
            if n + 1 < 16:
                _emit_ab(2 * n + 2)
                _emit_ab(2 * n + 3)
            if n + 2 < 16:
                _emit_bn(n + 2)
        st_yev = P.comp("scalar",
                        lambda eng: eng.copy(yrow_sb[:, :], y_ps[:, :]),
                        deps=[red_p[1]], name=f"{name}_yev{ck}")
        st_yst = store(kb, ar_in.ap()[ck:ck + 1, :], yrow_sb[:, :],
                       deps=[st_yev], name=f"{name}_yst{ck}")
        st_yev_prev = st_yev
        last_yst = st_yst
        prev_ck = {
            "xck_readers": [st_u],
            "dtraw_readers": [st_dtb],
            "dtb_readers": [bbc_prev[0]],
            "uck_readers": [st_u],
        }
    st_ar = P.coll(
        lambda eng: eng.collective_compute(
            "AllReduce", ALU.add, replica_groups=[list(range(R))],
            ins=[ar_in.ap().opt()], outs=[ar_out.ap().opt()]),
        deps=[last_yst], name=f"{name}_ar")
    kb.free(dtb_ps, y_ps)
    kb.free(xck_b, hstate, yrow_sb, h2_sb, bu_sb[1], bu_sb[0], scr2,
            bbc_sb[1], bbc_sb[0], brow_sb[1], brow_sb[0], dtraw_sb)

    yall_sb = kb.sb("mb_yall", [8, T], F32)
    yscal_sb = kb.sb("mb_yscal", [1, T], F32)
    st_ya = load(kb, yall_sb[:, :], ar_out.ap(), deps=[st_ar], name=f"{name}_ya")
    ysel_ps = kb.ps("mb_yselps", [1, T], F32)
    st_ysel = P.comp("tensor",
                     (lambda: lambda eng: [
                         eng.matmul(ysel_ps[:, n0:min(n0 + 512, T)], tmask_sb[:, :],
                                    yall_sb[:, n0:min(n0 + 512, T)],
                                    start=True, stop=True)
                         for n0 in range(0, T, 512)][-1])(),
                     deps=[st_ya, st_w], name=f"{name}_ysel")
    st_ysc = P.comp("vector",
                    lambda eng: (
                        eng.tensor_mul(yscal_sb[:, :], ysel_ps[:, :], csum_sb[:, :]),
                        eng.tensor_single_scalar(yscal_sb[:, :], yscal_sb[:, :],
                                                 1.0 / 2048.0, ALU.mult))[-1],
                    deps=[st_ysel, st_cse], name=f"{name}_ysc")
    kb.free(ysel_ps)
    yb_ps = kb.ps("mb_ybps", [128, T], F32)
    st_yb = P.comp("tensor",
                   (lambda: lambda eng: [
                       eng.matmul(yb_ps[:, n0:min(n0 + 512, T)], ones_row,
                                  yscal_sb[:, n0:min(n0 + 512, T)],
                                  start=True, stop=True)
                       for n0 in range(0, T, 512)][-1])(),
                   deps=[st_ysc], name=f"{name}_yb")

    xs_in = [kb.sb("mb_xsin", [128, T], BF16) for _ in range(2)]
    zz_in = [kb.sb("mb_zzin", [128, T], BF16) for _ in range(2)]
    ts_bf = kb.sb("mb_tsbf", [128, T], BF16)
    y3_stages = []
    prev_ld = {}
    for og in range(16):
        buf = og % 2
        st_lx = load(kb, xs_in[buf][:, :],
                     xs_a2a_in.ap()[og * 128:(og + 1) * 128, :],
                     deps=[st_yb] + prev_ld.get(buf, []), name=f"{name}_c1x{og}")
        st_lz = load(kb, zz_in[buf][:, :],
                     z_stage.ap()[og * 128:(og + 1) * 128, :],
                     deps=[st_yb] + prev_ld.get(buf, []), name=f"{name}_c1z{og}",
                     engine="sync")
        st_m1 = P.comp("vector",
                       (lambda og=og, buf=buf: lambda eng:
                        eng.scalar_tensor_tensor(ts_bf[:, :], xs_in[buf][:, :],
                                                 dcol_sb[:, og:og + 1], yb_ps[:, :],
                                                 ALU.mult, ALU.add))(),
                       deps=[st_lx, st_w] + y3_stages[-1:], name=f"{name}_m1o{og}")
        st_sz = P.comp("scalar",
                       (lambda buf=buf: lambda eng:
                        eng.activation(zz_in[buf][:, :], zz_in[buf][:, :], AF.Silu))(),
                       deps=[st_lz], name=f"{name}_szo{og}")
        st_m2 = P.comp("vector",
                       (lambda buf=buf: lambda eng:
                        eng.tensor_mul(zz_in[buf][:, :], ts_bf[:, :],
                                       zz_in[buf][:, :]))(),
                       deps=[st_m1, st_sz], name=f"{name}_m2o{og}")
        st_y3s = store(kb, y3_stage.ap()[og * 128:(og + 1) * 128, :],
                       zz_in[buf][:, :], deps=[st_m2], name=f"{name}_y3s{og}")
        prev_ld[buf] = [st_m2, st_y3s]
        y3_stages.append(st_y3s)
    kb.free(yb_ps)

    wout_sb2 = kb.sb("mb_wout2", [128, 16, 256], BF16)
    mo_sb = xln
    mo_ps2 = [kb.ps("mb_mops", [128, T], F32) for _ in range(2)]
    mo_evs = []
    prev_pass = []
    for ps_i in range(4):
        mgs = (2 * ps_i, 2 * ps_i + 1)
        st_lw = P.dma("gpsimd",
                      (lambda ps_i=ps_i: lambda eng: [eng.dma_start(
                          out=wout_sb2[:, :, :],
                          in_=W_out[:, ps_i * 256:(ps_i + 1) * 256].rearrange(
                              "(g p) m -> p g m", p=128))])(),
                      deps=y3_stages + prev_pass, n_dma=1,
                      name=f"{name}_ldwo{ps_i}")
        mm_stages = []
        prev_y3ld = {}
        for kg in range(16):
            buf = kg % 2
            st_ly = load(kb, xs_in[buf][:, :],
                         y3_stage.ap()[kg * 128:(kg + 1) * 128, :],
                         deps=y3_stages + prev_y3ld.get(buf, []) + prev_pass,
                         name=f"{name}_c2y{ps_i}k{kg}")
            def emit_mo(eng, kg=kg, buf=buf):
                last = None
                for j in range(2):
                    eng.ldweights(wout_sb2[:, kg, j * 128:(j + 1) * 128])
                    for n0 in range(0, T, 512):
                        last = eng.matmul(mo_ps2[j][:, n0:min(n0 + 512, T)],
                                          wout_sb2[:, kg, j * 128:(j + 1) * 128],
                                          xs_in[buf][:, n0:min(n0 + 512, T)],
                                          start=(kg == 0), stop=(kg == 15))
                return last
            st_mo = P.comp("tensor", emit_mo,
                           deps=[st_ly, st_lw] + prev_pass, name=f"{name}_mo{ps_i}k{kg}")
            mm_stages.append(st_mo)
            prev_y3ld[buf] = [st_mo]
        evs = []
        for j in range(2):
            st_ev = P.comp("scalar",
                           (lambda j=j, mg=mgs[j]: lambda eng:
                            eng.copy(mo_sb[:, mg, :], mo_ps2[j][:, :]))(),
                           deps=[mm_stages[-1]], name=f"{name}_moev{ps_i}_{j}")
            evs.append(st_ev)
            mo_evs.append(st_ev)
        prev_pass = evs
    kb.free(mo_ps2[1], mo_ps2[0])

    kb.free(wout_sb2, ts_bf, zz_in[1], zz_in[0], xs_in[1], xs_in[0],
            yscal_sb, yall_sb)
    mo_tiles = [mo_sb[:, g, :] for g in range(8)]
    ln_fin2 = emit_ln(kb, mo_tiles, None, T, resid_tiles=x_tiles,
                      deps=mo_evs, ones_col=ones_col, ones_row=ones_row,
                      ones_col_bf=ones_col_bf, name=f"{name}_lnmo")
    kb.free(csum_sb, cw_sb, cbb_sb, wx_sb, wdt_sb, bdt_sb, acol_sb, dcol_sb,
            selc_sb, hmask_sb, tmask_sb, xln)
    return ln_fin2



def emit_rwkv(kb, x_tiles, T, WrT, WkT, WvT, mur_cols, muk_cols, halo_mask,
              ones_col, ones_row, ones_col_bf, deps=(), name="rw"):
    P = kb.prog
    R = 8

    mur_sb = kb.sb("rw_mur", [128, 8], F32)
    muk_sb = kb.sb("rw_muk", [128, 8], F32)
    hmask_sb = kb.sb("rw_hmask", [8, 1], F32)
    st_w0 = P.dma("gpsimd",
                  lambda eng: [
                      eng.dma_start(out=mur_sb[:, :], in_=mur_cols),
                      eng.dma_start(out=muk_sb[:, :], in_=muk_cols),
                      eng.dma_start(out=hmask_sb[:, :], in_=halo_mask),
                  ], deps=list(deps), n_dma=3, name=f"{name}_ldc")

    h_ag_in = kb.dram(kb.u("rw_hin"), [1024, 1], F32)
    h_ag_out = kb.dram(kb.u("rw_hout"), [R * 1024, 1], F32)
    h_bounce = kb.dram(kb.u("rw_hb"), [1, 1024], F32)
    hall_sb = kb.sb("rw_hall", [8, 1024], F32)
    hprev_sb = kb.sb("rw_hprev", [128, 8], F32)
    hrow_sb = kb.sb("rw_hrow", [128, 8, 1], F32)
    st_hst = P.dma("gpsimd",
                   lambda eng: [
                       eng.dma_start(
                           out=h_ag_in.ap()[g * 128:(g + 1) * 128, :],
                           in_=x_tiles[g][:, T - 1:T])
                       for g in range(8)],
                   deps=list(deps), n_dma=8, name=f"{name}_hst")
    st_hag = P.coll(
        lambda eng: eng.collective_compute(
            "AllGather", ALU.bypass, replica_groups=[list(range(R))],
            ins=[h_ag_in.ap().opt()], outs=[h_ag_out.ap().opt()]),
        deps=[st_hst], name=f"{name}_hag")
    st_hld = load(kb, hall_sb[:, :],
                  h_ag_out.ap().rearrange("(r q) c -> r (q c)", r=8),
                  deps=[st_hag], name=f"{name}_hld")
    hsel_ps = kb.ps("rw_hselps", [1, 1024], F32)
    st_hsel = P.comp("tensor",
                     (lambda: lambda eng: [
                         eng.matmul(hsel_ps[:, n0:n0 + 512], hmask_sb[:, :],
                                    hall_sb[:, n0:n0 + 512], start=True, stop=True)
                         for n0 in range(0, 1024, 512)][-1])(),
                     deps=[st_hld, st_w0], name=f"{name}_hsel")
    hsel_sb = kb.sb("rw_hselsb", [1, 1024], F32)
    st_hsev = P.comp("vector",
                     lambda eng: eng.tensor_copy(hsel_sb[:, :], hsel_ps[:, :]),
                     deps=[st_hsel], name=f"{name}_hsev")
    st_hbo = store(kb, h_bounce.ap(), hsel_sb[:, :], deps=[st_hsev],
                   name=f"{name}_hbo")
    def emit_hbi(eng):
        with kb.nc.allow_non_contiguous_dma(reason="tiny 4KB halo gather"):
            return [eng.dma_start(
                out=hrow_sb[:, :, :],
                in_=h_bounce.ap().rearrange("a (g p) -> (a p) g ()", p=128))]
    st_hbi = P.dma("gpsimd", emit_hbi, deps=[st_hbo], n_dma=1, name=f"{name}_hbi")
    kb.free(hsel_sb)
    kb.free(hsel_ps)

    hs_ps = kb.ps("rw_hsps", [1, 2], F32)
    hsq_sb = kb.sb("rw_hsq", [128, 8], F32)
    st_hsq = P.comp("scalar",
                    lambda eng: eng.activation(hsq_sb[:, :],
                                               hrow_sb[:, :, :].rearrange("p g c -> p (g c)"),
                                               AF.Square),
                    deps=[st_hbi], name=f"{name}_hsq")

    def emit_hstats(eng):
        last = None
        for g in range(8):
            eng.matmul(hs_ps[:, 0:1], ones_col, hrow_sb[:, g, :],
                       start=(g == 0), stop=(g == 7))
        for g in range(8):
            last = eng.matmul(hs_ps[:, 1:2], ones_col, hsq_sb[:, g:g + 1],
                              start=(g == 0), stop=(g == 7))
        return last
    st_hss = P.comp("tensor", emit_hstats, deps=[st_hbi, st_hsq],
                    name=f"{name}_hss")
    hst_sb = kb.sb("rw_hst", [1, 4], F32)
    st_hm = P.comp("vector",
                   lambda eng: eng.tensor_single_scalar(
                       hst_sb[:, 0:2], hs_ps[:, :], 1.0 / 1024.0, ALU.mult),
                   deps=[st_hss], name=f"{name}_hm")
    st_hv = P.comp("vector",
                   lambda eng: (
                       eng.tensor_mul(hst_sb[:, 2:3], hst_sb[:, 0:1], hst_sb[:, 0:1]),
                       eng.tensor_sub(hst_sb[:, 1:2], hst_sb[:, 1:2], hst_sb[:, 2:3]),
                       eng.tensor_single_scalar(hst_sb[:, 1:2], hst_sb[:, 1:2],
                                                1e-5, ALU.add))[-1],
                   deps=[st_hm], name=f"{name}_hv")
    st_hs2 = P.comp("scalar",
                    lambda eng: eng.activation(hst_sb[:, 1:2], hst_sb[:, 1:2], AF.Sqrt),
                    deps=[st_hv], name=f"{name}_hs2")
    st_hr = P.comp("vector",
                   lambda eng: eng.reciprocal(hst_sb[:, 1:2], hst_sb[:, 1:2]),
                   deps=[st_hs2], name=f"{name}_hr")
    hb_ps = kb.ps("rw_hbps", [128, 2], F32)
    st_hbb = P.comp("tensor",
                    lambda eng: eng.matmul(hb_ps[:, :], ones_row, hst_sb[:, 0:2],
                                           start=True, stop=True),
                    deps=[st_hr], name=f"{name}_hbb")
    hpb_sb = kb.sb("rw_hpb", [128, 2], F32)
    st_hbe = P.comp("vector",
                    lambda eng: eng.tensor_copy(hpb_sb[:, :], hb_ps[:, :]),
                    deps=[st_hbb], name=f"{name}_hbe")
    kb.free(hb_ps, hs_ps)
    st_hln = P.comp("vector",
                    lambda eng: eng.tensor_scalar(
                        hprev_sb[:, :], hrow_sb[:, :, :].rearrange("p g c -> p (g c)"),
                        hpb_sb[:, 0:1], hpb_sb[:, 1:2], ALU.subtract, ALU.mult),
                    deps=[st_hbe], name=f"{name}_hln")

    xr_sb = kb.sb("rw_xr", [128, 8, T], BF16)
    xk_sb = kb.sb("rw_xk", [128, 8, T], BF16)
    dscr = kb.sb("rw_dscr", [128, T], BF16)
    xln4 = kb.sb("rw_xln4", [128, 8, T], BF16)
    xln_tiles = [xln4[:, g, :] for g in range(8)]
    ln_fin = emit_ln(kb, x_tiles, xln_tiles, T, deps=deps,
                     ones_col=ones_col, ones_row=ones_row, name=f"{name}_ln4")
    shift_stages = []
    prev_d = []
    for g in range(8):
        st_d = P.comp("vector",
                      (lambda g=g: lambda eng:
                       eng.tensor_sub(dscr[:, 1:T], xln4[:, g, 1:T],
                                      xln4[:, g, 0:T - 1]))(),
                      deps=ln_fin + prev_d, name=f"{name}_d{g}")
        st_d0 = P.comp("vector",
                       (lambda g=g: lambda eng:
                        eng.tensor_sub(dscr[:, 0:1], xln4[:, g, 0:1],
                                       hprev_sb[:, g:g + 1]))(),
                       deps=[st_d, st_hln], name=f"{name}_d0{g}")
        st_xr = P.comp("vector",
                       (lambda g=g: lambda eng: (
                           eng.scalar_tensor_tensor(
                               xr_sb[:, g, 1:T], dscr[:, 1:T], mur_sb[:, g:g + 1],
                               xln4[:, g, 0:T - 1], ALU.mult, ALU.add),
                           eng.scalar_tensor_tensor(
                               xk_sb[:, g, 1:T], dscr[:, 1:T], muk_sb[:, g:g + 1],
                               xln4[:, g, 0:T - 1], ALU.mult, ALU.add))[-1])(),
                       deps=[st_d, st_w0], name=f"{name}_xrk{g}")
        st_xkp = st_xr
        st_xr0 = P.comp("vector",
                        (lambda g=g: lambda eng: (
                            eng.scalar_tensor_tensor(
                                xr_sb[:, g, 0:1], dscr[:, 0:1], mur_sb[:, g:g + 1],
                                hprev_sb[:, g:g + 1], ALU.mult, ALU.add),
                            eng.scalar_tensor_tensor(
                                xk_sb[:, g, 0:1], dscr[:, 0:1], muk_sb[:, g:g + 1],
                                hprev_sb[:, g:g + 1], ALU.mult, ALU.add))[-1])(),
                        deps=[st_xr, st_xkp, st_d0], name=f"{name}_xrk0{g}")
        prev_d = [st_xr0, st_xkp]
        shift_stages.append(st_xr0)
    kb.free(xln4)

    wk_sb = kb.sb("rw_wk", [128, 8, 1024], BF16)
    st_wk = P.dma("gpsimd",
                  lambda eng: [eng.dma_start(
                      out=wk_sb[:, :, :],
                      in_=WkT.rearrange("(g p) m -> p g m", p=128))],
                  deps=shift_stages, n_dma=1, name=f"{name}_ldwk")
    k2_sb = kb.sb("rw_k2", [128, 8, T], BF16)
    kp_ps2 = [kb.ps("rw_kps", [128, T], F32) for _ in range(2)]
    kr_p = {}
    k2_stages = []
    for mg in range(8):
        kb_ = mg % 2

        def emit_k(eng, mg=mg, kb_=kb_):
            last = None
            for kg in range(8):
                eng.ldweights(wk_sb[:, kg, mg * 128:(mg + 1) * 128])
                for n0 in range(0, T, 512):
                    last = eng.matmul(kp_ps2[kb_][:, n0:min(n0 + 512, T)],
                                      wk_sb[:, kg, mg * 128:(mg + 1) * 128],
                                      xk_sb[:, kg, n0:min(n0 + 512, T)],
                                      start=(kg == 0), stop=(kg == 7))
            return last
        st_km = P.comp("tensor", emit_k,
                       deps=shift_stages + [st_wk] +
                            ([kr_p[kb_]] if kb_ in kr_p else []),
                       name=f"{name}_km{mg}")
        st_kr = P.comp("vector",
                       (lambda mg=mg, kb_=kb_: lambda eng:
                        eng.tensor_scalar_max(k2_sb[:, mg, :],
                                              kp_ps2[kb_][:, :], 0.0))(),
                       deps=[st_km], name=f"{name}_kr{mg}")
        st_k2 = P.comp("vector",
                       (lambda mg=mg: lambda eng:
                        eng.tensor_mul(k2_sb[:, mg, :], k2_sb[:, mg, :],
                                       k2_sb[:, mg, :]))(),
                       deps=[st_kr], name=f"{name}_k2{mg}")
        kr_p[kb_] = st_kr
        k2_stages.append(st_k2)
    kb.free(kp_ps2[1], kp_ps2[0])

    rw_sb = xk_sb
    wv_sb = kb.sb("rw_wv", [128, 8, 1024], BF16)
    st_wv = P.dma("gpsimd",
                  lambda eng: [
                      eng.dma_start(out=wk_sb[:, :, :],
                                    in_=WrT.rearrange("(g p) m -> p g m", p=128)),
                      eng.dma_start(out=wv_sb[:, :, :],
                                    in_=WvT.rearrange("(g p) m -> p g m", p=128)),
                  ], deps=k2_stages + [st_km], n_dma=2, name=f"{name}_ldwv")
    wr_sb = wk_sb
    rs_scr = kb.sb("rw_rs", [128, T], BF16)
    r_ps = kb.ps("rw_rps", [128, T], F32)
    v_ps = kb.ps("rw_vps", [128, T], F32)
    prev_rw = []
    prev_sg = []
    rw_stages = []
    for mg in range(8):
        def emit_r(eng, mg=mg):
            last = None
            for kg in range(8):
                eng.ldweights(wr_sb[:, kg, mg * 128:(mg + 1) * 128])
                for n0 in range(0, T, 512):
                    last = eng.matmul(r_ps[:, n0:min(n0 + 512, T)],
                                      wr_sb[:, kg, mg * 128:(mg + 1) * 128],
                                      xr_sb[:, kg, n0:min(n0 + 512, T)],
                                      start=(kg == 0), stop=(kg == 7))
            return last

        def emit_v(eng, mg=mg):
            last = None
            for kg in range(8):
                eng.ldweights(wv_sb[:, kg, mg * 128:(mg + 1) * 128])
                for n0 in range(0, T, 512):
                    last = eng.matmul(v_ps[:, n0:min(n0 + 512, T)],
                                      wv_sb[:, kg, mg * 128:(mg + 1) * 128],
                                      k2_sb[:, kg, n0:min(n0 + 512, T)],
                                      start=(kg == 0), stop=(kg == 7))
            return last
        st_rm = P.comp("tensor", emit_r, deps=shift_stages + [st_wv] + prev_sg,
                       name=f"{name}_rm{mg}")
        st_vm = P.comp("tensor", emit_v, deps=k2_stages + [st_wv] + prev_rw,
                       name=f"{name}_vm{mg}")
        st_sg = P.comp("scalar",
                       (lambda: lambda eng:
                        eng.activation(rs_scr[:, :], r_ps[:, :], AF.Sigmoid))(),
                       deps=[st_rm], name=f"{name}_sg{mg}")
        st_rw = P.comp("vector",
                       (lambda mg=mg: lambda eng:
                        eng.tensor_mul(rw_sb[:, mg, :], rs_scr[:, :], v_ps[:, :]))(),
                       deps=[st_sg, st_vm], name=f"{name}_rw{mg}")
        prev_rw = [st_rw]
        prev_sg = [st_sg]
        rw_stages.append(st_rw)
    kb.free(v_ps, r_ps)

    kb.free(rs_scr, wv_sb, k2_sb, wk_sb, dscr)
    rw_tiles = [rw_sb[:, g, :] for g in range(8)]
    ln_fin2 = emit_ln(kb, rw_tiles, None, T, resid_tiles=x_tiles,
                      deps=rw_stages, ones_col=ones_col, ones_row=ones_row,
                      ones_col_bf=ones_col_bf, name=f"{name}_lnrw")
    kb.free(xr_sb, xk_sb,
            hst_sb, hsq_sb, hpb_sb, hrow_sb, hprev_sb,
            hall_sb, hmask_sb, muk_sb, mur_sb)
    return ln_fin2

"""Driver: FUSED single-launch megakernel."""
import numpy as np
import ml_dtypes
from concourse.bass_utils import run_bass_kernel_spmd

T = 2048
S = 16384


def _build_fused():
    import os
    DBGF = os.environ.get("DBGF")
    kb = KB()
    x = kb.inp("x", [1024, T], F32)
    WqT = kb.inp("WqT", [1024, 128], F32)
    WkT = kb.inp("WkT", [1024, 128], F32)
    WvT = kb.inp("WvT", [1024, 128], F32)
    maskf = kb.inp("mask", [128, 128], F32)
    identf_i = kb.inp("identf", [128, 128], F32)
    WoT = kb.inp("WoT", [1024, 1024], F32)
    gate_wT = kb.inp("gate_wT", [1024, 16], F32)
    gate_b = kb.inp("gate_b", [1, 16], F32)
    W13 = kb.inp("W13", [18, 1024, 1024], F32)
    W2 = kb.inp("W2", [18, 512, 1024], F32)
    b13 = kb.inp("b13", [128, 8, 18], F32)
    be2 = kb.inp("be2", [16, 1024], F32)
    bs2 = kb.inp("bs2", [1, 1024], F32)
    sel18 = kb.inp("sel18", [18, 18 * 128], F32)
    W_in = kb.inp("W_in", [1024, 4096], F32)
    cw = kb.inp("cw", [128, 16, 4], F32)
    cb = kb.inp("cb", [128, 16], F32)
    W_x = kb.inp("W_x", [2048, 33], F32)
    wdt = kb.inp("wdt", [128, 2], F32)
    bdt = kb.inp("bdt", [128, 2], F32)
    alog = kb.inp("alog", [128, 32], F32)
    dcol = kb.inp("dcol", [128, 16], F32)
    W_out = kb.inp("W_out", [2048, 1024], F32)
    selC = kb.inp("selC", [33, 1], F32)
    hmask = kb.inp("hmask", [8, 1], F32)
    tmask = kb.inp("tmask", [8, 1], F32)
    WrT = kb.inp("WrT", [1024, 1024], F32)
    WkT2 = kb.inp("WkT2", [1024, 1024], F32)
    WvT2 = kb.inp("WvT2", [1024, 1024], F32)
    mur = kb.inp("mur", [128, 8], F32)
    muk = kb.inp("muk", [128, 8], F32)
    y = kb.out("y", [1024, T], F32)

    x_sb = kb.sb("x", [128, 8, T], F32)
    ones_col, ones_row, ones_col_bf, st_c = make_consts(kb)
    identf = kb.sb("identf", [128, 128], F32)
    mask_sb = kb.sb("mask", [128, 128], F32)
    ident_bf = kb.sb("identbf", [128, 128], BF16)
    ones_bf = kb.sb("onesbf", [128, 128], BF16)
    st_ob = kb.prog.comp("gpsimd", lambda eng: eng.memset(ones_bf[:, :], 1.0),
                         name="onesbf")
    st_ld = load(kb, x_sb[:, :, :], x.rearrange("(g p) t -> p g t", p=128),
                 name="ld_x")
    st_m = load(kb, mask_sb[:, :], maskf, name="ld_mask")
    st_i = load(kb, identf[:, :], identf_i, name="ld_i")
    st_ib = kb.prog.dma("gpsimd",
                        lambda eng: [eng.dma_start(out=ident_bf[:, :], in_=identf_i)],
                        n_dma=1, name="ld_identbf")
    x_tiles = [x_sb[:, g, :] for g in range(8)]

    out_attn = kb.sb("oattn", [128, S], BF16)
    at_fin = emit_attn(kb, x_tiles, T, S, WqT, WkT, WvT,
                       mask_sb[:, :], ident_bf[:, :],
                       ones_col, ones_row, ones_bf[:, :], out_attn[:, :],
                       deps=[st_ld, st_c, st_m, st_i, st_ib, st_ob], name="at")
    at_a2a_in = kb.dram(kb.u("at_a2ain"), [8 * 128, T], BF16)
    at_a2a_out = kb.dram(kb.u("at_a2aout"), [8 * 128, T], BF16)
    st_ast = kb.prog.dma("gpsimd",
                         lambda eng: [
                             eng.dma_start(out=at_a2a_in.ap()[j * 128:(j + 1) * 128, :],
                                           in_=out_attn[:, j * T:(j + 1) * T])
                             for j in range(8)],
                         deps=at_fin, n_dma=8, name="at_a2ast")

    def emit_a2a(eng):
        from concourse.aot_env import allow_ici_all_to_all
        with allow_ici_all_to_all():
            return eng.collective_compute(
                "AllToAll", ALU.bypass, replica_groups=[list(range(8))],
                ins=[at_a2a_in.ap().opt()], outs=[at_a2a_out.ap().opt()])
    st_a2a = kb.prog.coll(emit_a2a, deps=[st_ast], name="at_a2a")
    kb.free(out_attn, ones_bf, ident_bf, mask_sb)

    if DBGF:
        d_attn = kb.out("d_attn", [1024, T], BF16)
        st_da = kb.prog.dma("gpsimd",
                            lambda eng: [eng.dma_start(out=d_attn, in_=at_a2a_out.ap())],
                            deps=[st_a2a], n_dma=1, name="d_attn")
    lr_fin = emit_l0rest(kb, x_tiles, T, at_a2a_out.ap(), WoT, gate_wT, gate_b,
                         W13, W2, b13, be2, bs2, sel18, None, identf[:, :],
                         ones_col, ones_row, ones_col_bf,
                         deps=[st_a2a], name="lr")

    if DBGF:
        d_lr = kb.out("d_lr", [1024, T], F32)
        for g in range(8):
            store(kb, d_lr[g * 128:(g + 1) * 128, :], x_tiles[g], deps=lr_fin,
                  name=f"d_lr{g}")
    mb_fin = emit_mamba(kb, x_tiles, T, S, W_in, cw, cb, W_x, wdt, bdt, alog,
                        dcol, W_out, selC, hmask, tmask, ones_col, ones_row,
                        ones_col_bf=ones_col_bf, deps=lr_fin, name="mb")

    if DBGF:
        d_mb = kb.out("d_mb", [1024, T], F32)
        for g in range(8):
            store(kb, d_mb[g * 128:(g + 1) * 128, :], x_tiles[g], deps=mb_fin,
                  name=f"d_mb{g}")
    rw_fin = emit_rwkv(kb, x_tiles, T, WrT, WkT2, WvT2, mur, muk, hmask,
                       ones_col, ones_row, ones_col_bf, deps=mb_fin, name="rw")

    store(kb, y.rearrange("(g p) t -> p g t", p=128), x_sb[:, :, :],
          deps=rw_fin, name="st_y")
    kb.finish()
    return kb


def _prep_inputs(inp):
    x_in = np.ascontiguousarray(np.asarray(inp["L0_attn_x"])[0])
    Wq = np.asarray(inp["L0_attn_Wq"]); Wk = np.asarray(inp["L0_attn_Wk"])
    Wv = np.asarray(inp["L0_attn_Wv"]); Wo = np.asarray(inp["L0_attn_Wo"])
    mask = np.triu(np.ones((128, 128), np.float32))
    ident = np.eye(128, dtype=np.float32)
    gate_w = np.asarray(inp["L0_ffn_gate_w"])
    gate_b = np.asarray(inp["L0_ffn_gate_b"]).reshape(1, 16)
    We1 = np.asarray(inp["L0_ffn_We1"]); We2 = np.asarray(inp["L0_ffn_We2"])
    We3 = np.asarray(inp["L0_ffn_We3"])
    be1 = np.asarray(inp["L0_ffn_be1"]); be2 = np.asarray(inp["L0_ffn_be2"])
    be3 = np.asarray(inp["L0_ffn_be3"])
    ws1 = np.asarray(inp["L0_ffn_ws1"]); ws2 = np.asarray(inp["L0_ffn_ws2"])
    ws3 = np.asarray(inp["L0_ffn_ws3"])
    bs1 = np.asarray(inp["L0_ffn_bs1"])
    bs2 = np.asarray(inp["L0_ffn_bs2"]).reshape(1, 1024)
    bs3 = np.asarray(inp["L0_ffn_bs3"])
    W13 = np.zeros((18, 1024, 1024), np.float32)
    W2a = np.zeros((18, 512, 1024), np.float32)
    b13 = np.zeros((128, 8, 18), np.float32)
    for e in range(16):
        W13[e] = np.concatenate([We1[e], We3[e]], axis=1)
        W2a[e] = We2[e]
        b13[:, 0:4, e] = be1[e].reshape(4, 128).T
        b13[:, 4:8, e] = be3[e].reshape(4, 128).T
    for i, e in enumerate((16, 17)):
        hsl = slice(i * 512, (i + 1) * 512)
        W13[e] = np.concatenate([ws1[:, hsl], ws3[:, hsl]], axis=1)
        W2a[e] = ws2[hsl, :]
        b13[:, 0:4, e] = bs1[hsl].reshape(4, 128).T
        b13[:, 4:8, e] = bs3[hsl].reshape(4, 128).T
    sel18 = np.zeros((18, 18 * 128), np.float32)
    for e in range(18):
        sel18[e, e * 128:(e + 1) * 128] = 1.0
    W_in = np.asarray(inp["L1_attn_W_in"])
    conv_w = np.asarray(inp["L1_attn_conv_w"])
    conv_b = np.asarray(inp["L1_attn_conv_b"])
    W_x = np.asarray(inp["L1_attn_W_x"])
    W_dt = np.asarray(inp["L1_attn_W_dt"])
    b_dt = np.asarray(inp["L1_attn_b_dt"])
    A_log = np.asarray(inp["L1_attn_A_log"])
    D_par = np.asarray(inp["L1_attn_D_param"])
    W_out = np.asarray(inp["L1_attn_W_out"])
    selC = np.zeros((33, 1), np.float32)
    selC[16:32] = 1.0
    cw_cols = np.zeros((128, 16, 4), np.float32)
    cb_cols = np.zeros((128, 16), np.float32)
    d_cols = np.zeros((128, 16), np.float32)
    for og in range(16):
        cw_cols[:, og, :] = conv_w[og * 128:(og + 1) * 128, :]
        cb_cols[:, og] = conv_b[og * 128:(og + 1) * 128]
        d_cols[:, og] = D_par[og * 128:(og + 1) * 128]
    Wr = np.asarray(inp["L1_ffn_W_r"]); Wkk = np.asarray(inp["L1_ffn_W_k"])
    Wvv = np.asarray(inp["L1_ffn_W_v"])
    mur_c = np.asarray(inp["L1_ffn_mu_r"]).reshape(8, 128).T.copy()
    muk_c = np.asarray(inp["L1_ffn_mu_k"]).reshape(8, 128).T.copy()

    in_maps = []
    for c in range(8):
        sl = slice(c * 128, (c + 1) * 128)
        wdt_c = np.zeros((128, 2), np.float32)
        bdt_c = np.zeros((128, 2), np.float32)
        alog_c = np.zeros((128, 32), np.float32)
        for g in range(2):
            dd = slice(c * 256 + g * 128, c * 256 + (g + 1) * 128)
            wdt_c[:, g] = W_dt[0, dd]
            bdt_c[:, g] = b_dt[dd]
            for n in range(16):
                alog_c[:, n * 2 + g] = A_log[dd, n]
        hmask = np.zeros((8, 1), np.float32)
        if c > 0:
            hmask[c - 1, 0] = 1.0
        tmask = np.zeros((8, 1), np.float32)
        tmask[c, 0] = 1.0
        in_maps.append({
            "x": np.ascontiguousarray(x_in[c * T:(c + 1) * T].T),
            "WqT": np.ascontiguousarray(Wq.T[:, sl]),
            "WkT": np.ascontiguousarray(Wk.T[:, sl]),
            "WvT": np.ascontiguousarray(Wv.T[:, sl]),
            "mask": mask, "identf": ident,
            "WoT": np.ascontiguousarray(Wo.T),
            "gate_wT": np.ascontiguousarray(gate_w.T),
            "gate_b": gate_b, "W13": W13, "W2": W2a, "b13": b13,
            "be2": be2, "bs2": bs2, "sel18": sel18,
            "W_in": W_in, "cw": cw_cols, "cb": cb_cols, "W_x": W_x,
            "wdt": wdt_c, "bdt": bdt_c, "alog": alog_c, "dcol": d_cols,
            "W_out": W_out, "selC": selC, "hmask": hmask, "tmask": tmask,
            "WrT": np.ascontiguousarray(Wr.T),
            "WkT2": np.ascontiguousarray(Wkk.T),
            "WvT2": np.ascontiguousarray(Wvv.T),
            "mur": mur_c, "muk": muk_c,
        })
    return in_maps


def kernel(**inp):
    kb = _build_fused()
    in_maps = _prep_inputs(inp)
    res = run_bass_kernel_spmd(kb.nc, in_maps, core_ids=list(range(8)))
    out = np.concatenate([res.results[c]["y"].T for c in range(8)], axis=0)
    return out[None].astype(np.float32)
